# version 04
# speedup vs baseline: 1.2212x; 1.0769x over previous
import numpy as np
from contextlib import ExitStack

import concourse.bass as bass
import concourse.mybir as mybir
from concourse.alu_op_type import AluOpType as OP
from concourse.bass_utils import run_bass_kernel_spmd

F32 = mybir.dt.float32
BF16 = mybir.dt.bfloat16
AF = mybir.ActivationFunctionType
AX = mybir.AxisListType

NCORES = 8
B, T, D = 2, 8192, 1024
TPC = (B * T) // NCORES
SEGS_PER_BATCH = T // TPC
NTT = TPC // 128
NTC = TPC // 512
ND = D // 128

H0, DH0, C0 = 16, 64, 64
NCH0 = TPC // C0
E0, I0 = 16, 512
D_INNER, NH1, P1, N1, DCONV = 2048, 32, 64, 128, 4
D_IN_PROJ, CONV_DIM = 4384, 2304
Q1 = 128
NCH1 = TPC // Q1
E1, DV1 = 8, 128

ENGINES = ("sync", "tensor", "vector", "scalar", "gpsimd")


class Ctx:
    def __init__(self, nc, stack, const_vals=()):
        self.nc = nc
        self.st = stack
        self.blk = stack.enter_context(nc.Block())
        self.bar = stack.enter_context(nc.semaphore("bar"))
        self.barn = 0
        self.const_tiles = []
        for i, v in enumerate(const_vals):
            t = stack.enter_context(nc.sbuf_tensor(f"cst{i}", [128, 1], F32))
            nc.const_aps.register(F32, float(v), t[:, :])
            self.const_tiles.append((t, float(v)))

    def emit_const_memsets(self, v_eng):
        for t, val in self.const_tiles:
            v_eng.memset(t[:, :], val)

    def sem(self, name):
        return self.st.enter_context(self.nc.semaphore(name))

    def sb(self, name, shape, dtype, stack=None):
        return (stack or self.st).enter_context(self.nc.sbuf_tensor(name, shape, dtype))

    def ps(self, name, shape, dtype=F32, stack=None):
        return (stack or self.st).enter_context(self.nc.psum_tensor(name, shape, dtype))

    def seq(self, name, steps):
        sem = self.sem(name)
        dsems = {}
        for en in ("sync", "gpsimd", "scalar"):
            dsems[en] = self.sem(name + "_d_" + en)
        dcount = {en: 0 for en in dsems}
        plans = {en: [] for en in ENGINES}
        for i, (en, fn) in enumerate(steps):
            plans[en].append((i, fn))
        for en in ENGINES:
            mysteps = plans[en]
            def f(eng, mysteps=mysteps, en=en):
                for i, fn in mysteps:
                    if i > 0:
                        eng.wait_ge(sem, i)
                    ndma = [0]
                    if en in dsems:
                        def dma(out, in_, _eng=eng, _en=en, **kw):
                            ndma[0] += 1
                            return _eng.dma_start(out=out, in_=in_, **kw
                                                  ).then_inc(dsems[_en], 16)
                        dma.transpose = lambda out, in_, _eng=eng, _en=en: (
                            ndma.__setitem__(0, ndma[0] + 1),
                            _eng.dma_start_transpose(out, in_).then_inc(dsems[_en], 16))[1]
                    else:
                        dma = None
                    fn(eng, dma)
                    if ndma[0]:
                        dcount[en] += ndma[0]
                        eng.wait_ge(dsems[en], 16 * dcount[en])
                    eng.drain()
                    eng.sem_inc(sem, 1)
            getattr(self.blk, en)(f)

    def barrier(self):
        self.barn += 1
        tgt = 5 * self.barn
        bar = self.bar
        for en in ENGINES:
            def f(eng, tgt=tgt, bar=bar):
                eng.drain()
                eng.sem_inc(bar, 1)
                eng.wait_ge(bar, tgt)
            getattr(self.blk, en)(f)


def build_launch1(debug=(), upto='E', nch0=NCH0, cparts=3):
    nc = bass.Bass("TRN2", target_bir_lowering=False)
    dbg = set(debug)

    xd = nc.dram_tensor("x", [TPC, D], F32, kind="ExternalInput").ap()
    wqkv = nc.dram_tensor("wqkv", [D, 3 * D], F32, kind="ExternalInput").ap()
    gw = nc.dram_tensor("gw", [E0, D], F32, kind="ExternalInput").ap()
    ebias = nc.dram_tensor("ebias", [1, E0], F32, kind="ExternalInput").ap()
    gatep = nc.dram_tensor("gatep", [E0, D, I0], F32, kind="ExternalInput").ap()
    upp = nc.dram_tensor("upp", [E0, D, I0], F32, kind="ExternalInput").ap()
    downp = nc.dram_tensor("downp", [E0, I0, D], F32, kind="ExternalInput").ap()

    q_out = nc.dram_tensor("q_out", [D, TPC], BF16, kind="ExternalOutput").ap()
    ap_out = nc.dram_tensor("attn_part", [D, TPC], BF16, kind="ExternalOutput").ap()
    ckv_out = nc.dram_tensor("ckv_sums", [DH0, H0 * DH0], F32, kind="ExternalOutput").ap()
    g_out = nc.dram_tensor("g_out", [D, TPC], BF16, kind="ExternalOutput").ap()

    xnth = nc.dram_tensor("xnth", [TPC, D], BF16).ap()
    wqkvbf = nc.dram_tensor("wqkvbf", [D, 3 * D], BF16).ap()
    gwbf = nc.dram_tensor("gwbf", [E0, D], BF16).ap()
    kthb = nc.dram_tensor("kthb", [D, TPC], BF16).ap()
    gpbf = nc.dram_tensor("gpbf", [E0, D, I0], BF16).ap()
    upbf = nc.dram_tensor("upbf", [E0, D, I0], BF16).ap()
    dpbf = nc.dram_tensor("dpbf", [E0, I0, D], BF16).ap()

    dbg_tensors = {}
    def dbg_out(name, shape, dtype=F32):
        if name in dbg:
            dbg_tensors[name] = nc.dram_tensor("dbg_" + name, shape, dtype,
                                               kind="ExternalOutput").ap()
            return dbg_tensors[name]
        return None

    with ExitStack() as st:
        ctx = Ctx(nc, st, const_vals=(1e-5, 0.125, 1e-20, -1e30, -1.0))
        blk = ctx.blk

        ones_bf = ctx.sb("ones_bf", [128, 1], BF16)
        ones1_f = ctx.sb("ones1_f", [1, 128], F32)
        ones1_bf = ctx.sb("ones1_bf", [1, 128], BF16)
        iv = ctx.sb("iv", [128, 128], F32)
        ident_f = ctx.sb("ident_f", [128, 128], F32)
        ident_bf = ctx.sb("ident_bf", [128, 128], BF16)
        mskge_bf = ctx.sb("mskge_bf", [128, 128], BF16)
        ebias_t = ctx.sb("ebias_t", [128, E0], F32)
        bonus_t = ctx.sb("bonus_t", [128, E0], F32)

        wc_sem = ctx.sem("wc_sem")
        mc_sem = ctx.sem("mc_sem")
        eb_sem = ctx.sem("eb_sem")
        k_sem = ctx.sem("k_sem")
        xnh_sem = [ctx.sem("xnh_sem0"), ctx.sem("xnh_sem1")]

        def ph0_gpsimd(g):
            g.dma_start(out=wqkvbf[:, :], in_=wqkv[:, :]).then_inc(wc_sem, 16)
            g.dma_start(out=gwbf[:, :], in_=gw[:, :]).then_inc(wc_sem, 16)
            g.dma_start(out=ebias_t[:, :], in_=ebias[0:1, :].partition_broadcast(128)
                        ).then_inc(eb_sem, 16)
            g.iota(iv[:, :], pattern=[[1, 128]], base=0, channel_multiplier=-1,
                   allow_small_or_imprecise_dtypes=True)
            g.iota(bonus_t[:, :], pattern=[[-1, E0]], base=E0, channel_multiplier=0,
                   allow_small_or_imprecise_dtypes=True).then_inc(k_sem, 1)
        blk.gpsimd(ph0_gpsimd)

        def ph0_vector(v):
            ctx.emit_const_memsets(v)
            v.memset(ones_bf[:, :], 1.0)
            v.memset(ones1_f[:, :], 1.0)
            v.memset(ones1_bf[:, :], 1.0)
            v.wait_ge(k_sem, 1)
            v.tensor_single_scalar(ident_f[:, :], iv[:, :], 0.0, OP.is_equal)
            v.drain()
            v.tensor_copy(ident_bf[:, :], ident_f[:, :])
            v.tensor_single_scalar(mskge_bf[:, :], iv[:, :], 0.0, OP.is_ge)
            v.tensor_scalar_mul(bonus_t[:, :], bonus_t[:, :], 1e-20)
        blk.vector(ph0_vector)

        ctx.barrier()

        with ExitStack() as sa:
            xrow = [ctx.sb(f"xrow{i}", [128, D], F32, sa) for i in range(2)]
            sqr = [ctx.sb(f"sqr{i}", [128, D], F32, sa) for i in range(2)]
            stat = [ctx.sb(f"stat{i}", [128, 6], F32, sa) for i in range(2)]
            xnt = [ctx.sb(f"xnt{i}", [128, D], BF16, sa) for i in range(2)]

            xl_sem = [ctx.sem("xl_sem0"), ctx.sem("xl_sem1")]
            sq_sem = ctx.sem("sq_sem")
            st_sem = ctx.sem("st_sem")
            rs_sem = ctx.sem("rs_sem")
            xn_sem = ctx.sem("xn_sem")

            def pa_sync(sy):
                for tt in range(NTT):
                    p = tt % 2
                    if tt >= 2:
                        sy.wait_ge(xn_sem, tt - 1)
                    sy.dma_start(out=xrow[p][:, :], in_=xd[tt * 128:(tt + 1) * 128, :]
                                 ).then_inc(xl_sem[p], 16)
            blk.sync(pa_sync)

            def pa_scalar(s):
                for tt in range(NTT):
                    p = tt % 2
                    s.wait_ge(xl_sem[p], 16 * (tt // 2 + 1))
                    if tt >= 2:
                        s.wait_ge(st_sem, 2 * (tt - 1))
                    s.activation(out=sqr[p][:, :], in_=xrow[p][:, :],
                                 func=AF.Square).then_inc(sq_sem, 1)
                    s.wait_ge(st_sem, 2 * tt + 1)
                    s.activation(out=stat[p][:, 4:5], in_=stat[p][:, 3:4],
                                 func=AF.Ln, bias=1e-5)
                    s.drain()
                    s.activation(out=stat[p][:, 4:5], in_=stat[p][:, 4:5],
                                 func=AF.Exp, scale=-0.5).then_inc(rs_sem, 1)
                    s.wait_ge(st_sem, 2 * tt + 2)
                    if tt >= 2:
                        s.wait_ge(xnh_sem[p], 16 * (tt // 2))
                    s.activation(out=xnt[p][:, :], in_=xrow[p][:, :],
                                 func=AF.Identity,
                                 scale=stat[p][:, 4:5],
                                 bias=stat[p][:, 5:6]).then_inc(xn_sem, 1)
            blk.scalar(pa_scalar)



            def pa_vector(v):
                for tt in range(NTT):
                    p = tt % 2
                    v.wait_ge(xl_sem[p], 16 * (tt // 2 + 1))
                    v.wait_ge(sq_sem, tt + 1)
                    if tt >= 2:
                        v.wait_ge(xn_sem, tt - 1)
                    v.reduce_sum(stat[p][:, 0:1], xrow[p][:, :], axis=AX.X)
                    v.reduce_sum(stat[p][:, 1:2], sqr[p][:, :], axis=AX.X)
                    v.drain()
                    v.tensor_scalar_mul(stat[p][:, 2:3], stat[p][:, 0:1], 1.0 / D)
                    v.tensor_scalar_mul(stat[p][:, 3:4], stat[p][:, 1:2], 1.0 / D)
                    v.drain()
                    v.tensor_mul(stat[p][:, 5:6], stat[p][:, 2:3], stat[p][:, 2:3])
                    v.drain()
                    v.tensor_sub(stat[p][:, 3:4], stat[p][:, 3:4], stat[p][:, 5:6]
                                 ).then_inc(st_sem, 1)
                    v.wait_ge(rs_sem, tt + 1)
                    v.drain()
                    v.tensor_mul(stat[p][:, 5:6], stat[p][:, 2:3], stat[p][:, 4:5])
                    v.drain()
                    v.tensor_scalar_mul(stat[p][:, 5:6], stat[p][:, 5:6], -1.0
                                        ).then_inc(st_sem, 1)
                    v.drain()
            blk.vector(pa_vector)

            def pa_gpsimd(g):
                for tt in range(NTT):
                    p = tt % 2
                    g.wait_ge(xn_sem, tt + 1)
                    g.dma_start(out=xnth[tt * 128:(tt + 1) * 128, :], in_=xnt[p][:, :]
                                ).then_inc(xnh_sem[p], 16)
            blk.gpsimd(pa_gpsimd)

            def pa_gpsimd_casts(g):
                g.dma_start(out=gpbf[:, :, :], in_=gatep[:, :, :]).then_inc(mc_sem, 16)
                g.dma_start(out=upbf[:, :, :], in_=upp[:, :, :]).then_inc(mc_sem, 16)
                g.dma_start(out=dpbf[:, :, :], in_=downp[:, :, :]).then_inc(mc_sem, 16)
            blk.gpsimd(pa_gpsimd_casts)

            ctx.barrier()

        with ExitStack() as sb:
            qT = [ctx.sb(f"qT{c}", [128, TPC], BF16, sb) for c in range(8)]
            kT = [ctx.sb(f"kT{c}", [128, TPC], BF16, sb) for c in range(8)]
            vtok = [ctx.sb(f"vtok{i}", [128, D], BF16, sb) for i in range(NTT)]
            with ExitStack() as sb2:
                xn2 = [ctx.sb(f"xn2_{c}", [128, TPC], BF16, sb2) for c in range(ND)]
                wq = [ctx.sb(f"wq{c}", [128, 3 * D], BF16, sb2) for c in range(ND)]

                xn2_sem = ctx.sem("xn2_sem")
                wq_sem = ctx.sem("wq_sem")
                qk_mm = ctx.sem("qk_mm")
                qk_ev = ctx.sem("qk_ev")
                v_mm = ctx.sem("v_mm")
                v_ev = ctx.sem("v_ev")
                kb_sem = ctx.sem("kb_sem")
                qo_sem = ctx.sem("qo_sem")

                def pb_sync(sy):
                    sy.wait_ge(xnh_sem[0], 16 * (NTT // 2))
                    sy.wait_ge(xnh_sem[1], 16 * (NTT // 2))
                    for c in range(ND):
                        sy.dma_start_transpose(xn2[c][:, :],
                                               xnth[:, c * 128:(c + 1) * 128]
                                               ).then_inc(xn2_sem, 16)
                    sy.wait_ge(wc_sem, 32)
                    for c in range(ND):
                        sy.dma_start(out=wq[c][:, :],
                                     in_=wqkvbf[c * 128:(c + 1) * 128, :]
                                     ).then_inc(wq_sem, 16)
                blk.sync(pb_sync)

                with ExitStack() as sp:
                    ps_qk = [ctx.ps(f"ps_qk{i}", [128, 512], F32, sp) for i in range(8)]

                    def pb_tensor_qk(t):
                        t.wait_ge(xn2_sem, 16 * ND)
                        t.wait_ge(wq_sem, 16 * ND)
                        for oc in range(16):
                            if oc >= 2:
                                t.wait_ge(qk_ev, (oc - 1) * 4)
                            for d in range(ND):
                                for tc in range(NTC):
                                    ins = t.matmul(ps_qk[(oc % 2) * 4 + tc][:, :],
                                                   wq[d][:, oc * 128:(oc + 1) * 128],
                                                   xn2[d][:, tc * 512:(tc + 1) * 512],
                                                   start=(d == 0), stop=(d == ND - 1))
                            ins.then_inc(qk_mm, 1)
                    blk.tensor(pb_tensor_qk)

                    def pb_scalar_qk(s):
                        for oc in range(16):
                            s.wait_ge(qk_mm, oc + 1)
                            for tc in range(NTC):
                                src = ps_qk[(oc % 2) * 4 + tc][:, :]
                                sl = slice(tc * 512, (tc + 1) * 512)
                                if oc < 8:
                                    s.activation(out=qT[oc][:, sl], in_=src,
                                                 func=AF.Copy, scale=0.125
                                                 ).then_inc(qk_ev, 1)
                                else:
                                    s.activation(out=kT[oc - 8][:, sl], in_=src,
                                                 func=AF.Sign).then_inc(qk_ev, 1)
                    blk.scalar(pb_scalar_qk)

                    def pb_gpsimd_qk(g):
                        for oc in range(8):
                            g.wait_ge(qk_ev, oc * 4 + 4)
                            g.dma_start(out=q_out[oc * 128:(oc + 1) * 128, :],
                                        in_=qT[oc][:, :]).then_inc(qo_sem, 16)
                        for oc in range(8):
                            g.wait_ge(qk_ev, 32 + oc * 4 + 4)
                            g.dma_start(out=kthb[oc * 128:(oc + 1) * 128, :],
                                        in_=kT[oc][:, :]).then_inc(kb_sem, 16)
                    blk.gpsimd(pb_gpsimd_qk)

                    ctx.barrier()

                with ExitStack() as sp:
                    ps_v = [ctx.ps(f"ps_v{i}", [128, 512], F32, sp) for i in range(4)]

                    def pb_tensor_v(t):
                        for tt in range(NTT):
                            if tt >= 2:
                                t.wait_ge(v_ev, (tt - 1) * 2)
                            for d in range(ND):
                                for n2 in range(2):
                                    ins = t.matmul(ps_v[(tt % 2) * 2 + n2][:, :],
                                                   xn2[d][:, tt * 128:(tt + 1) * 128],
                                                   wq[d][:, 2 * D + n2 * 512:2 * D + (n2 + 1) * 512],
                                                   start=(d == 0), stop=(d == ND - 1))
                            ins.then_inc(v_mm, 1)
                    blk.tensor(pb_tensor_v)

                    def pb_scalar_v(s):
                        for tt in range(NTT):
                            s.wait_ge(v_mm, tt + 1)
                            for n2 in range(2):
                                s.activation(out=vtok[tt][:, n2 * 512:(n2 + 1) * 512],
                                             in_=ps_v[(tt % 2) * 2 + n2][:, :],
                                             func=AF.Sign).then_inc(v_ev, 1)
                    blk.scalar(pb_scalar_v)

                    ctx.barrier()

            dq = dbg_out("qT", [D, TPC])
            dk = dbg_out("kT", [D, TPC])
            dv = dbg_out("vtok", [TPC, D])
            if dq is not None:
                dbg_sem_b = ctx.sem("dbg_sem_b")
                def pb_dbg(g):
                    n = 0
                    for oc in range(8):
                        g.dma_start(out=dq[oc * 128:(oc + 1) * 128, :], in_=qT[oc][:, :]
                                    ).then_inc(dbg_sem_b, 16); n += 16
                        g.dma_start(out=dk[oc * 128:(oc + 1) * 128, :], in_=kT[oc][:, :]
                                    ).then_inc(dbg_sem_b, 16); n += 16
                    for tt in range(NTT):
                        g.dma_start(out=dv[tt * 128:(tt + 1) * 128, :], in_=vtok[tt][:, :]
                                    ).then_inc(dbg_sem_b, 16); n += 16
                    g.wait_ge(dbg_sem_b, n)
                blk.gpsimd(pb_dbg)
                ctx.barrier()

            if upto in ('A', 'B'):
                return nc, dbg_tensors
            with ExitStack() as sc:
                ktok = [ctx.sb(f"ktok{i}", [128, D], BF16, sc) for i in range(NTT)]
                attnT = [ctx.sb(f"attnT{c}", [128, TPC], BF16, sc) for c in range(8)]
                prefF = [ctx.sb(f"prefF{i}", [64, H0 * DH0], F32, sc) for i in range(2)]
                prefB = ctx.sb("prefB", [128, (H0 // 2) * DH0], BF16, sc)
                Sm = [ctx.sb(f"Sm{i}", [128, C0], BF16, sc) for i in range(2)]
                psS = [ctx.ps(f"psS{i}", [64, C0], F32, sc) for i in range(2)]
                psIC = [ctx.ps(f"psIC{i}", [64, DH0], F32, sc) for i in range(2)]
                psCR = [ctx.ps(f"psCR{i}", [64, DH0], F32, sc) for i in range(2)]
                psCK = [ctx.ps(f"psCK{i}", [64, DH0], F32, sc) for i in range(2)]

                kt_sem = ctx.sem("kt_sem")
                t_cr = ctx.sem("t_cr")
                s_cr = ctx.sem("s_cr")
                t_sm = ctx.sem("t_sm")
                t_ic = ctx.sem("t_ic")
                v_sm = ctx.sem("v_sm")
                v_dve = ctx.sem("v_dve")
                ao_sem = ctx.sem("ao_sem")

                NU = nch0 * H0

                def unit(j):
                    ch, h = divmod(j, H0)
                    return ch, h

                def pb_sl(h):
                    return prefB[64 * (h % 2):64 * (h % 2) + 64,
                                 (h // 2) * DH0:(h // 2) * DH0 + DH0]

                def pc_sync(sy):
                    sy.wait_ge(kb_sem, 16 * 8)
                    sy.wait_ge(qo_sem, 16 * 8)
                    for tt in range(NTT if cparts >= 0 else 0):
                        sy.dma_start_transpose(ktok[tt][:, :],
                                               kthb[:, tt * 128:(tt + 1) * 128]
                                               ).then_inc(kt_sem, 16)
                blk.sync(pc_sync)

                def pc_vector0(v):
                    v.memset(prefF[0][:, :], 0.0)
                    v.memset(prefF[1][:, :], 0.0)
                    v.memset(prefB[:, :], 0.0).then_inc(v_dve, 1)
                    v.drain()
                blk.vector(pc_vector0)

                def mm_unit(t, j):
                    ch, h = unit(j)
                    hs = slice(64 * (h % 2), 64 * (h % 2) + 64)
                    cs = slice(ch * 64, (ch + 1) * 64)
                    tt, cb = divmod(ch * 64, 128)
                    ts = slice(cb, cb + 64)
                    t.matmul(psIC[j % 2][:, :], vtok[tt][ts, h * 64:(h + 1) * 64],
                             Sm[j % 2][ts, :], start=True, stop=True)
                    if cparts >= 2:
                        t.matmul(psCR[j % 2][:, :], pb_sl(h),
                                 qT[h // 2][hs, cs], start=True, stop=True
                                 ).then_inc(t_cr, 1)
                    else:
                        t.nop().then_inc(t_cr, 1)
                    if cparts >= 3:
                        t.matmul(psCK[j % 2][:, :], ktok[tt][ts, h * 64:(h + 1) * 64],
                                 vtok[tt][ts, h * 64:(h + 1) * 64],
                                 start=True, stop=True).then_inc(t_ic, 1)
                    else:
                        t.nop().then_inc(t_ic, 1)

                def pc_tensor(t):
                    if cparts >= 0:
                        t.wait_ge(kt_sem, 16 * NTT)
                    t.wait_ge(v_dve, 1)
                    for j in range(NU if cparts >= 1 else 0):
                        ch, h = unit(j)
                        if j >= 2:
                            t.wait_ge(v_sm, j - 1)
                            t.wait_ge(v_dve, j - 1 + 1)
                            if cparts >= 2:
                                t.wait_ge(s_cr, j - 1)
                        hs = slice(64 * (h % 2), 64 * (h % 2) + 64)
                        cs = slice(ch * 64, (ch + 1) * 64)
                        t.matmul(psS[j % 2][:, :], kT[h // 2][hs, cs],
                                 qT[h // 2][hs, cs], start=True, stop=True
                                 ).then_inc(t_sm, 1)
                        if j >= 1:
                            t.wait_ge(v_sm, j)
                            mm_unit(t, j - 1)
                    if cparts >= 1:
                        t.wait_ge(v_sm, NU)
                        mm_unit(t, NU - 1)
                blk.tensor(pc_tensor)

                def pc_scalar(s):
                    for j in range(NU if cparts >= 2 else 0):
                        ch, h = unit(j)
                        s.wait_ge(t_cr, j + 1)
                        s.activation(out=attnT[h // 2][64 * (h % 2):64 * (h % 2) + 64,
                                                       ch * 64:(ch + 1) * 64],
                                     in_=psCR[j % 2][:, :],
                                     func=AF.Copy).then_inc(s_cr, 1)
                blk.scalar(pc_scalar)

                def ev_unit(v, j):
                    ch, h = unit(j)
                    cur, nxt = ch % 2, (ch + 1) % 2
                    hs64 = slice(h * 64, (h + 1) * 64)
                    asl = attnT[h // 2][64 * (h % 2):64 * (h % 2) + 64,
                                        ch * 64:(ch + 1) * 64]
                    if cparts >= 2:
                        v.wait_ge(s_cr, j + 1)
                        v.scalar_tensor_tensor(asl, psIC[j % 2][:, :], 1.0, asl,
                                               OP.mult, OP.add)
                    else:
                        v.tensor_copy(asl, psIC[j % 2][:, :])
                    if cparts >= 3:
                        v.tensor_add(pb_sl(h), psCK[j % 2][:, :], prefF[cur][:, hs64])
                        v.tensor_add(prefF[nxt][:, hs64], psCK[j % 2][:, :],
                                     prefF[cur][:, hs64]).then_inc(v_dve, 1)
                    else:
                        v.engine_nop().then_inc(v_dve, 1)

                def pc_vector(v):
                    for j in range(NU if cparts >= 1 else 0):
                        ch, h = unit(j)
                        cb = (ch * 64) % 128
                        v.wait_ge(t_sm, j + 1)
                        v.tensor_mul(Sm[j % 2][cb:cb + 64, :], psS[j % 2][:, :],
                                     mskge_bf[0:64, 0:64]).then_inc(v_sm, 1)
                        if j >= 1:
                            v.wait_ge(t_ic, j)
                            ev_unit(v, j - 1)
                    if cparts >= 1:
                        v.wait_ge(t_ic, NU)
                        ev_unit(v, NU - 1)
                blk.vector(pc_vector)

                def pc_gpsimd(g):
                    g.wait_ge(v_dve, (NU if cparts >= 1 else 0) + 1)
                    for c in range(8):
                        g.dma_start(out=ap_out[c * 128:(c + 1) * 128, :],
                                    in_=attnT[c][:, :]).then_inc(ao_sem, 16)
                    g.dma_start(out=ckv_out[:, :], in_=prefF[0][:, :]).then_inc(ao_sem, 16)
                    g.wait_ge(ao_sem, 16 * 9)
                    g.wait_ge(qo_sem, 16 * 8)
                blk.gpsimd(pc_gpsimd)

                ctx.barrier()

        if upto in ('A', 'B', 'C'):
            return nc, dbg_tensors
        with ExitStack() as sd:
            xn3 = [ctx.sb(f"xn3_{c}", [128, TPC], BF16, sd) for c in range(ND)]
            cmbT = ctx.sb("cmbT", [E0, TPC], F32, sd)
            yacc = [ctx.sb(f"yacc{c}", [128, TPC], F32, sd) for c in range(ND)]

            xn3_sem = ctx.sem("xn3_sem")
            gwt_sem = ctx.sem("gwt_sem")
            lg_mm = ctx.sem("lg_mm")
            lg_ev = ctx.sem("lg_ev")
            tr_mm = ctx.sem("tr_mm")
            ev_sem = ctx.sem("ev_sem")
            sg_sem = ctx.sem("sg_sem")
            cf_sem = ctx.sem("cf_sem")
            cb_mm = ctx.sem("cb_mm")
            cbT_sem = ctx.sem("cbT_sem")

            with ExitStack() as sd2:
                gwT = [ctx.sb(f"gwT{c}", [128, E0], BF16, sd2) for c in range(ND)]
                logT = ctx.sb("logT", [E0, TPC], F32, sd2)
                ltok = [ctx.sb(f"ltok{i}", [128, E0], F32, sd2) for i in range(NTT)]
                sc_t = [ctx.sb(f"sc{i}", [128, E0], F32, sd2) for i in range(NTT)]
                sfc_t = [ctx.sb(f"sfc{i}", [128, E0], F32, sd2) for i in range(NTT)]
                smk_t = [ctx.sb(f"smk{i}", [128, E0], F32, sd2) for i in range(NTT)]
                tmpb_t = [ctx.sb(f"tmpb{i}", [128, E0], F32, sd2) for i in range(NTT)]
                g4_t = [ctx.sb(f"g4_{i}", [128, 4], F32, sd2) for i in range(NTT)]
                p6_t = [ctx.sb(f"p6_{i}", [128, 4], F32, sd2) for i in range(NTT)]
                sec_t = [ctx.sb(f"sec{i}", [128, 1], F32, sd2) for i in range(NTT)]
                mx_t = [ctx.sb(f"mx{i}", [128, 1], F32, sd2) for i in range(NTT)]
                eq_t = [ctx.sb(f"eq{i}", [128, E0], F32, sd2) for i in range(NTT)]
                cmb_t = [ctx.sb(f"cmb{i}", [128, E0], F32, sd2) for i in range(NTT)]

                ps_lg = [ctx.ps(f"ps_lg{i}", [E0, 512], F32, sd2) for i in range(2)]
                ps_tr = [ctx.ps(f"ps_tr{i}", [128, E0], F32, sd2) for i in range(3)]
                ps_cb = [ctx.ps(f"ps_cb{i}", [E0, 128], F32, sd2) for i in range(3)]

                def pd_sync(sy):
                    sy.wait_ge(xnh_sem[0], 16 * (NTT // 2))
                    sy.wait_ge(xnh_sem[1], 16 * (NTT // 2))
                    for c in range(ND):
                        sy.dma_start_transpose(xn3[c][:, :],
                                               xnth[:, c * 128:(c + 1) * 128]
                                               ).then_inc(xn3_sem, 16)
                    for c in range(ND):
                        sy.dma_start_transpose(gwT[c][:, :],
                                               gwbf[:, c * 128:(c + 1) * 128]
                                               ).then_inc(gwt_sem, 16)
                blk.sync(pd_sync)

                def pd_tensor(t):
                    t.wait_ge(xn3_sem, 16 * ND)
                    t.wait_ge(gwt_sem, 16 * ND)
                    for tc in range(NTC):
                        if tc >= 2:
                            t.wait_ge(lg_ev, tc - 1)
                        for d in range(ND):
                            ins = t.matmul(ps_lg[tc % 2][:, :], gwT[d][:, :],
                                           xn3[d][:, tc * 512:(tc + 1) * 512],
                                           start=(d == 0), stop=(d == ND - 1))
                        ins.then_inc(lg_mm, 1)
                    t.wait_ge(lg_ev, NTC)
                    for tt in range(NTT):
                        if tt >= 3:
                            t.wait_ge(ev_sem, tt - 2)
                        t.transpose(ps_tr[tt % 3][:, :], logT[:, tt * 128:(tt + 1) * 128],
                                    ident_f[0:E0, 0:E0]).then_inc(tr_mm, 1)
                    for tt in range(NTT):
                        if tt >= 3:
                            t.wait_ge(cbT_sem, tt - 2)
                        t.wait_ge(cf_sem, tt + 1)
                        t.transpose(ps_cb[tt % 3][:, :], cmb_t[tt][:, :],
                                    ident_f[:, :]).then_inc(cb_mm, 1)
                blk.tensor(pd_tensor)

                def pd_scalar(s):
                    for tt in range(NTT):
                        s.wait_ge(ev_sem, tt + 1)
                        s.activation(out=sc_t[tt][:, :], in_=ltok[tt][:, :],
                                     func=AF.Sigmoid).then_inc(sg_sem, 1)
                blk.scalar(pd_scalar)

                def pd_vector(v):
                    for tc in range(NTC):
                        v.wait_ge(lg_mm, tc + 1)
                        v.tensor_copy(logT[:, tc * 512:(tc + 1) * 512],
                                      ps_lg[tc % 2][:, :]).then_inc(lg_ev, 1)
                    for tt in range(NTT):
                        v.wait_ge(tr_mm, tt + 1)
                        v.tensor_copy(ltok[tt][:, :], ps_tr[tt % 3][:, :]).then_inc(ev_sem, 1)
                    v.wait_ge(sg_sem, NTT)
                    v.wait_ge(eb_sem, 16)
                    v.drain()
                    for tt in range(NTT):
                        v.tensor_add(sfc_t[tt][:, :], sc_t[tt][:, :], ebias_t[:, :])
                    v.drain()
                    pairs = [(0, 1), (0, 2), (0, 3), (1, 2), (1, 3), (2, 3)]
                    for pi, (a, b) in enumerate(pairs):
                        for tt in range(NTT):
                            v.tensor_add(p6_t[tt][:, :], sfc_t[tt][:, a:16:4],
                                         sfc_t[tt][:, b:16:4])
                        v.drain()
                        for tt in range(NTT):
                            if pi == 0:
                                v.tensor_copy(g4_t[tt][:, :], p6_t[tt][:, :])
                            else:
                                v.tensor_max(g4_t[tt][:, :], g4_t[tt][:, :], p6_t[tt][:, :])
                        v.drain()
                    for pi, (a, b) in enumerate(pairs):
                        for tt in range(NTT):
                            v.tensor_tensor(p6_t[tt][:, 0:1], g4_t[tt][:, a:a + 1],
                                            g4_t[tt][:, b:b + 1], OP.min)
                        v.drain()
                        for tt in range(NTT):
                            if pi == 0:
                                v.tensor_copy(sec_t[tt][:, :], p6_t[tt][:, 0:1])
                            else:
                                v.tensor_max(sec_t[tt][:, :], sec_t[tt][:, :], p6_t[tt][:, 0:1])
                        v.drain()
                    for tt in range(NTT):
                        v.tensor_single_scalar(g4_t[tt][:, :], g4_t[tt][:, :],
                                               sec_t[tt][:, :], OP.is_ge)
                    v.drain()
                    for i in range(4):
                        for tt in range(NTT):
                            v.tensor_copy(smk_t[tt][:, i:16:4], g4_t[tt][:, :])
                    v.drain()
                    for tt in range(NTT):
                        v.tensor_mul(smk_t[tt][:, :], smk_t[tt][:, :], sfc_t[tt][:, :])
                    v.drain()
                    for tt in range(NTT):
                        v.tensor_add(tmpb_t[tt][:, :], smk_t[tt][:, :], bonus_t[:, :])
                    v.drain()
                    for tt in range(NTT):
                        v.tensor_copy(smk_t[tt][:, :], tmpb_t[tt][:, :])
                    v.drain()
                    for rnd in range(4):
                        for tt in range(NTT):
                            v.reduce_max(mx_t[tt][:, :], tmpb_t[tt][:, :], axis=AX.X)
                        v.drain()
                        if rnd < 3:
                            for tt in range(NTT):
                                v.tensor_single_scalar(eq_t[tt][:, :], tmpb_t[tt][:, :],
                                                       mx_t[tt][:, :], OP.is_equal)
                            v.drain()
                            for tt in range(NTT):
                                v.tensor_scalar_mul(eq_t[tt][:, :], eq_t[tt][:, :], -1e30)
                            v.drain()
                            for tt in range(NTT):
                                v.tensor_add(tmpb_t[tt][:, :], tmpb_t[tt][:, :],
                                             eq_t[tt][:, :])
                            v.drain()
                    for tt in range(NTT):
                        v.tensor_single_scalar(eq_t[tt][:, :], smk_t[tt][:, :],
                                               mx_t[tt][:, :], OP.is_ge)
                    v.drain()
                    for tt in range(NTT):
                        v.tensor_mul(cmb_t[tt][:, :], sc_t[tt][:, :], eq_t[tt][:, :])
                    v.drain()
                    for tt in range(NTT):
                        v.reduce_sum(mx_t[tt][:, :], cmb_t[tt][:, :], axis=AX.X)
                    v.drain()
                    for tt in range(NTT):
                        v.tensor_scalar_add(mx_t[tt][:, :], mx_t[tt][:, :], 1e-20)
                    v.drain()
                    for tt in range(NTT):
                        v.reciprocal(sec_t[tt][:, :], mx_t[tt][:, :])
                    v.drain()
                    for tt in range(NTT):
                        v.tensor_single_scalar(cmb_t[tt][:, :], cmb_t[tt][:, :],
                                               sec_t[tt][:, :], OP.mult).then_inc(cf_sem, 1)
                    v.drain()
                    for tt in range(NTT):
                        v.wait_ge(cb_mm, tt + 1)
                        v.tensor_copy(cmbT[:, tt * 128:(tt + 1) * 128],
                                      ps_cb[tt % 3][:, :]).then_inc(cbT_sem, 1)
                blk.vector(pd_vector)

                dcmb = dbg_out("cmb", [E0, TPC])
                if dcmb is not None:
                    dbg_sem_d = ctx.sem("dbg_sem_d")
                    def pd_dbg(g):
                        g.wait_ge(cbT_sem, NTT)
                        g.dma_start(out=dcmb[:, :], in_=cmbT[:, :]).then_inc(dbg_sem_d, 16)
                        g.wait_ge(dbg_sem_d, 16)
                    blk.gpsimd(pd_dbg)

                ctx.barrier()

            if upto == 'D':
                return nc, dbg_tensors
            with ExitStack() as se:
                NW = 20
                gT_w = [[ctx.sb(f"gTw{p}_{c}", [128, I0], BF16, se) for c in range(ND)] for p in range(2)]
                uT_w = [[ctx.sb(f"uTw{p}_{c}", [128, I0], BF16, se) for c in range(ND)] for p in range(2)]
                dT_w = [[ctx.sb(f"dTw{p}_{c}", [128, D], BF16, se) for c in range(4)] for p in range(2)]
                act_t = [ctx.sb(f"act{ic}", [128, TPC], BF16, se) for ic in range(4)]
                gob = [ctx.sb(f"gob{i}", [128, TPC], BF16, se) for i in range(2)]
                cstage = [ctx.sb(f"cstage{i}", [1, TPC], BF16, se) for i in range(2)]

                ps_g = [ctx.ps(f"ps_g{i}", [128, 512], F32, se) for i in range(2)]
                ps_u = [ctx.ps(f"ps_u{i}", [128, 512], F32, se) for i in range(2)]
                ps_d = [ctx.ps(f"ps_d{i}", [128, 512], F32, se) for i in range(3)]
                ps_wr = ctx.ps("ps_wr", [128, 512], F32, se)
                wrep = ctx.sb("wrep", [128, TPC], F32, se)
                sg2 = [ctx.sb(f"sg2_{i}", [128, 1024], F32, se) for i in range(2)]

                we_sem = [ctx.sem("we_sem0"), ctx.sem("we_sem1")]
                cs_sem = [ctx.sem("cs_sem0"), ctx.sem("cs_sem1")]
                wr_mm = ctx.sem("wr_mm")
                wr_ev = ctx.sem("wr_ev")
                g_mm = ctx.sem("g_mm")
                u_mm = ctx.sem("u_mm")
                s_sil = ctx.sem("s_sil")
                va1 = ctx.sem("va1")
                p_act = ctx.sem("p_act")
                y_mm = ctx.sem("y_mm")
                y_ev = ctx.sem("y_ev")
                sl_sem = ctx.sem("sl_sem")
                go_sem = [ctx.sem("go_sem0"), ctx.sem("go_sem1")]

                def pe_sync(sy):
                    sy.wait_ge(mc_sem, 48)
                    for e in range(E0):
                        if e >= 2:
                            sy.wait_ge(y_mm, (e - 1) * 16)
                        p = e % 2
                        for c in range(ND):
                            sy.dma_start(out=gT_w[p][c][:, :],
                                         in_=gpbf[e, c * 128:(c + 1) * 128, :]
                                         ).then_inc(we_sem[p], 16)
                        for c in range(ND):
                            sy.dma_start(out=uT_w[p][c][:, :],
                                         in_=upbf[e, c * 128:(c + 1) * 128, :]
                                         ).then_inc(we_sem[p], 16)
                        for c in range(4):
                            sy.dma_start(out=dT_w[p][c][:, :],
                                         in_=dpbf[e, c * 128:(c + 1) * 128, :]
                                         ).then_inc(we_sem[p], 16)
                blk.sync(pe_sync)

                def pe_gpsimd_rows(g):
                    g.wait_ge(cbT_sem, NTT)
                    for e in range(E0):
                        if e >= 2:
                            g.wait_ge(wr_mm, (e - 1) * 4)
                        g.dma_start(out=cstage[e % 2][:, :], in_=cmbT[e:e + 1, :]
                                    ).then_inc(cs_sem[e % 2], 16)
                blk.gpsimd(pe_gpsimd_rows)

                def pe_vector0(v):
                    for c in range(ND):
                        v.memset(yacc[c][:, :], 0.0)
                    v.drain()
                    v.engine_nop().then_inc(y_ev, 1)
                blk.vector(pe_vector0)

                def pe_tensor(t):
                    t.wait_ge(y_ev, 1)
                    for e in range(E0):
                        p = e % 2
                        t.wait_ge(we_sem[p], 16 * NW * (e // 2 + 1))
                        t.wait_ge(cs_sem[p], 16 * (e // 2 + 1))
                        for tc in range(NTC):
                            w0 = e * 4 + tc
                            if w0 >= 1:
                                t.wait_ge(wr_ev, w0)
                            if e >= 1 and tc == 0:
                                t.wait_ge(p_act, 16 * e)
                            t.matmul(ps_wr[:, :], ones1_bf[:, :],
                                     cstage[p][0:1, tc * 512:(tc + 1) * 512],
                                     start=True, stop=True).then_inc(wr_mm, 1)
                        for th in range(2):
                            tcs = (2 * th, 2 * th + 1)
                            for ic in range(4):
                                gs = (e * 2 + th) * 4 + ic
                                if gs >= 1:
                                    t.wait_ge(s_sil, gs)
                                ins = None
                                for d in range(ND):
                                    for j, tc in enumerate(tcs):
                                        ins = t.matmul(
                                            ps_g[j][:, :],
                                            gT_w[p][d][:, ic * 128:(ic + 1) * 128],
                                            xn3[d][:, tc * 512:(tc + 1) * 512],
                                            start=(d == 0), stop=(d == ND - 1))
                                ins.then_inc(g_mm, 1)
                                if gs >= 1:
                                    t.wait_ge(va1, 2 * gs)
                                ins = None
                                for d in range(ND):
                                    for j, tc in enumerate(tcs):
                                        ins = t.matmul(
                                            ps_u[j][:, :],
                                            uT_w[p][d][:, ic * 128:(ic + 1) * 128],
                                            xn3[d][:, tc * 512:(tc + 1) * 512],
                                            start=(d == 0), stop=(d == ND - 1))
                                ins.then_inc(u_mm, 1)
                            t.wait_ge(p_act, 2 * ((e * 2 + th) * 4 + 4))
                            for hc in range(ND):
                                ds = (e * 2 + th) * 8 + hc
                                if ds >= 2:
                                    t.wait_ge(y_ev, 2 * ds - 1)
                                if ds >= 1:
                                    t.wait_ge(y_ev, 2 * ds)
                                ins = None
                                for icd in range(4):
                                    for j, tc in enumerate(tcs):
                                        ins = t.matmul(
                                            ps_d[(2 * ds + j) % 3][:, :],
                                            dT_w[p][icd][:, hc * 128:(hc + 1) * 128],
                                            act_t[icd][:, tc * 512:(tc + 1) * 512],
                                            start=(icd == 0), stop=(icd == 3))
                                ins.then_inc(y_mm, 1)
                blk.tensor(pe_tensor)

                def pe_scalar(s):
                    for e in range(E0):
                        for tc in range(NTC):
                            s.wait_ge(wr_mm, e * 4 + tc + 1)
                            s.activation(out=wrep[:, tc * 512:(tc + 1) * 512],
                                         in_=ps_wr[:, :],
                                         func=AF.Copy).then_inc(wr_ev, 1)
                        for th in range(2):
                            for ic in range(4):
                                gs = (e * 2 + th) * 4 + ic
                                s.wait_ge(g_mm, gs + 1)
                                if gs >= 2:
                                    s.wait_ge(va1, 2 * gs - 2)
                                ins = None
                                for j in range(2):
                                    ins = s.activation(
                                        out=sg2[gs % 2][:, j * 512:(j + 1) * 512],
                                        in_=ps_g[j][:, :], func=AF.Silu)
                                ins.then_inc(s_sil, 1)
                blk.scalar(pe_scalar)

                def pe_vector(v):
                    for e in range(E0):
                        for th in range(2):
                            tcs = (2 * th, 2 * th + 1)
                            for ic in range(4):
                                gs = (e * 2 + th) * 4 + ic
                                v.wait_ge(s_sil, gs + 1)
                                v.wait_ge(u_mm, gs + 1)
                                if e >= 1 and th == 0 and ic == 0:
                                    v.wait_ge(y_mm, 16 * e)
                                v.wait_ge(wr_ev, 4 * (e + 1))
                                for j, tc in enumerate(tcs):
                                    v.tensor_mul(act_t[ic][:, tc * 512:(tc + 1) * 512],
                                                 sg2[gs % 2][:, j * 512:(j + 1) * 512],
                                                 ps_u[j][:, :]).then_inc(va1, 1)
                                v.drain()
                                for tc in tcs:
                                    v.tensor_mul(
                                        act_t[ic][:, tc * 512:(tc + 1) * 512],
                                        act_t[ic][:, tc * 512:(tc + 1) * 512],
                                        wrep[:, tc * 512:(tc + 1) * 512]
                                    ).then_inc(p_act, 1)
                            for hc in range(ND):
                                ds = (e * 2 + th) * 8 + hc
                                v.wait_ge(y_mm, ds + 1)
                                for j, tc in enumerate(tcs):
                                    v.scalar_tensor_tensor(
                                        yacc[hc][:, tc * 512:(tc + 1) * 512],
                                        ps_d[(2 * ds + j) % 3][:, :], 1.0,
                                        yacc[hc][:, tc * 512:(tc + 1) * 512],
                                        OP.mult, OP.add).then_inc(y_ev, 1)
                            v.drain()
                blk.vector(pe_vector)


                def pe_scalar2(s):
                    s.wait_ge(y_ev, 1 + E0 * 32)
                    s.drain()
                    for hc in range(ND):
                        if hc >= 2:
                            s.wait_ge(go_sem[hc % 2], 16 * (hc // 2))
                        s.activation(out=gob[hc % 2][:, :], in_=yacc[hc][:, :],
                                     func=AF.Silu).then_inc(sl_sem, 1)
                blk.scalar(pe_scalar2)

                def pe_gpsimd2(g):
                    for hc in range(ND):
                        g.wait_ge(sl_sem, hc + 1)
                        g.dma_start(out=g_out[hc * 128:(hc + 1) * 128, :],
                                    in_=gob[hc % 2][:, :]).then_inc(go_sem[hc % 2], 16)
                    g.wait_ge(go_sem[0], 16 * (ND // 2))
                    g.wait_ge(go_sem[1], 16 * (ND // 2))
                blk.gpsimd(pe_gpsimd2)

                ctx.barrier()

    return nc, dbg_tensors



def build_launch2(debug=(), upto='Z'):
    nc = bass.Bass("TRN2", target_bir_lowering=False)
    dbg = set(debug)

    xd = nc.dram_tensor("x", [TPC, D], F32, kind="ExternalInput").ap()
    qbf = nc.dram_tensor("qbf", [D, TPC], BF16, kind="ExternalInput").ap()
    apbf = nc.dram_tensor("apbf", [D, TPC], BF16, kind="ExternalInput").ap()
    gbf = nc.dram_tensor("gbf", [D, TPC], BF16, kind="ExternalInput").ap()
    ckvall = nc.dram_tensor("ckvall", [4 * DH0, H0 * DH0], F32, kind="ExternalInput").ap()
    mcur = nc.dram_tensor("mcur", [DH0, 4], F32, kind="ExternalInput").ap()
    posr = nc.dram_tensor("posr", [1, TPC], F32, kind="ExternalInput").ap()
    wo = nc.dram_tensor("wo", [D, D], F32, kind="ExternalInput").ap()
    win = nc.dram_tensor("win", [D, D_IN_PROJ], F32, kind="ExternalInput").ap()
    simm = nc.dram_tensor("simm", [D, E1], F32, kind="ExternalInput").ap()
    gatesd = nc.dram_tensor("gatesd", [E1, 1], F32, kind="ExternalInput").ap()
    qkp = nc.dram_tensor("qkp", [E1, D, 3 * DV1], F32, kind="ExternalInput").ap()

    x1_out = nc.dram_tensor("x1_out", [D, TPC], F32, kind="ExternalOutput").ap()
    szo = nc.dram_tensor("siluz_out", [D_INNER, TPC], BF16, kind="ExternalOutput").ap()
    xbco = nc.dram_tensor("xbc_out", [CONV_DIM, TPC], BF16, kind="ExternalOutput").ap()
    dto = nc.dram_tensor("dtraw_out", [NH1, TPC], F32, kind="ExternalOutput").ap()
    mrwo = nc.dram_tensor("mrw_out", [E1, TPC], F32, kind="ExternalOutput").ap()
    qkvo = nc.dram_tensor("qkv_out", [TPC, 3 * DV1], BF16, kind="ExternalOutput").ap()

    wobf = nc.dram_tensor("wobf", [D, D], BF16).ap()
    winbf = nc.dram_tensor("winbf", [D, D_IN_PROJ], BF16).ap()
    qkpbf = nc.dram_tensor("qkpbf", [E1, D, 3 * DV1], BF16).ap()

    dbg_tensors = {}
    def dbg_out(name, shape, dtype=F32):
        if name in dbg:
            dbg_tensors[name] = nc.dram_tensor("dbg_" + name, shape, dtype,
                                               kind="ExternalOutput").ap()
            return dbg_tensors[name]
        return None

    with ExitStack() as st:
        ctx = Ctx(nc, st, const_vals=(1e-5, 1e-20, -1e30, -1e9, 1e-12))
        blk = ctx.blk

        ones_bf = ctx.sb("ones_bf", [128, 1], BF16)
        ones1_f = ctx.sb("ones1_f", [1, 128], F32)
        iv = ctx.sb("iv", [128, 128], F32)
        ident_f = ctx.sb("ident_f", [128, 128], F32)
        mcur_t = ctx.sb("mcur_t", [DH0, 4], F32)
        gates_t = ctx.sb("gates_t", [E1, 1], F32)

        wc_sem = ctx.sem("wc_sem")
        qc_sem = ctx.sem("qc_sem")
        sm_sem = ctx.sem("sm_sem")
        k_sem = ctx.sem("k_sem")

        def p0_gpsimd(g):
            g.dma_start(out=wobf[:, :], in_=wo[:, :]).then_inc(wc_sem, 16)
            g.dma_start(out=winbf[:, :], in_=win[:, :]).then_inc(wc_sem, 16)
            g.dma_start(out=qkpbf[:, :, :], in_=qkp[:, :, :]).then_inc(qc_sem, 16)
            g.iota(iv[:, :], pattern=[[1, 128]], base=0, channel_multiplier=-1,
                   allow_small_or_imprecise_dtypes=True).then_inc(k_sem, 1)
        blk.gpsimd(p0_gpsimd)

        def p0_sync(sy):
            sy.dma_start(out=mcur_t[:, :], in_=mcur[:, :]).then_inc(sm_sem, 16)
            sy.dma_start(out=gates_t[:, :], in_=gatesd[:, :]).then_inc(sm_sem, 16)
        blk.sync(p0_sync)

        def p0_vector(v):
            ctx.emit_const_memsets(v)
            v.memset(ones_bf[:, :], 1.0)
            v.memset(ones1_f[:, :], 1.0)
            v.wait_ge(k_sem, 1)
            v.tensor_single_scalar(ident_f[:, :], iv[:, :], 0.0, OP.is_equal)
        blk.vector(p0_vector)

        ctx.barrier()

        xn1T = [ctx.sb(f"xn1T{c}", [128, TPC], BF16) for c in range(ND)]
        xnh_sem = ctx.sem("xnh_sem")

        with ExitStack() as s1:
            xf = [ctx.sb(f"xf{c}", [128, TPC], F32, s1) for c in range(ND)]
            apT = [ctx.sb(f"apT{c}", [128, TPC], BF16, s1) for c in range(ND)]
            gT = [ctx.sb(f"gT{c}", [128, TPC], BF16, s1) for c in range(ND)]
            ckvp = ctx.sb("ckvp", [64, H0 * DH0], F32, s1)
            ckvp_bf = ctx.sb("ckvp_bf", [128, (H0 // 2) * DH0], BF16, s1)
            ckvs = [ctx.sb(f"ckvs{s}", [64, H0 * DH0], F32, s1) for s in range(2)]
            invt = ctx.sb("invt", [1, TPC], F32, s1)
            scr1 = ctx.sb("scr1", [128, 512], F32, s1)
            scr2 = ctx.sb("scr2", [128, 512], F32, s1)

            ld1 = ctx.sem("ld1")
            ck_sem = ctx.sem("ck_sem")
            xr_sem = [ctx.sem("xr_sem0"), ctx.sem("xr_sem1")]
            xt1_mm = ctx.sem("xt1_mm")
            xt1_ev = ctx.sem("xt1_ev")
            cr_mm = ctx.sem("cr_mm")
            cr_ev = ctx.sem("cr_ev")
            iv_mm = ctx.sem("iv_mm")
            wo_sem = ctx.sem("wo_sem")
            wm_mm = ctx.sem("wm_mm")
            x1_ev = ctx.sem("x1_ev")
            x1o_sem = ctx.sem("x1o_sem")

            with ExitStack() as s1b:
                qT = [ctx.sb(f"qT{c}", [128, TPC], BF16, s1b) for c in range(2)]
                xrow = [ctx.sb(f"xrow{i}", [128, D], F32, s1b) for i in range(2)]
                woT = [ctx.sb(f"woT{c}", [128, D], BF16, s1b) for c in range(2)]

                ps_tx = [ctx.ps(f"ps_tx{i}", [128, 128], F32, s1b) for i in range(2)]
                ps_cr = [ctx.ps(f"ps_cr{i}", [64, 512], F32, s1b) for i in range(2)]
                ps_iv = [ctx.ps(f"ps_iv{i}", [128, 512], F32, s1b) for i in range(4)]

                qst = [ctx.sem("qst0"), ctx.sem("qst1")]
                cks = [ctx.sem("cks0"), ctx.sem("cks1")]

                def p1_sync(sy):
                    for c in range(ND):
                        sy.dma_start(out=apT[c][:, :], in_=apbf[c * 128:(c + 1) * 128, :]
                                     ).then_inc(ld1, 16)
                        sy.dma_start(out=gT[c][:, :], in_=gbf[c * 128:(c + 1) * 128, :]
                                     ).then_inc(ld1, 16)
                    sy.dma_start(out=invt[:, :], in_=posr[:, :]).then_inc(ld1, 16)
                    for s in range(4):
                        if s >= 2:
                            sy.wait_ge(ck_sem, s - 1)
                        sy.dma_start(out=ckvs[s % 2][:, :],
                                     in_=ckvall[s * DH0:(s + 1) * DH0, :]
                                     ).then_inc(cks[s % 2], 16)
                blk.sync(p1_sync)
                N_LD1 = 16 * (2 * ND)

                def p1_gpsimd(g):
                    for i in range(NTT):
                        if i >= 2:
                            g.wait_ge(xt1_mm, (i - 1) * ND)
                        g.dma_start(out=xrow[i % 2][:, :], in_=xd[i * 128:(i + 1) * 128, :]
                                    ).then_inc(xr_sem[i % 2], 16)
                blk.gpsimd(p1_gpsimd)

                def p1_tensor_tx(t):
                    for i in range(NTT):
                        t.wait_ge(xr_sem[i % 2], 16 * (i // 2 + 1))
                        for c in range(ND):
                            j = i * ND + c
                            if j >= 2:
                                t.wait_ge(xt1_ev, j - 1)
                            t.transpose(ps_tx[j % 2][:, :],
                                        xrow[i % 2][:, c * 128:(c + 1) * 128],
                                        ident_f[:, :]).then_inc(xt1_mm, 1)
                blk.tensor(p1_tensor_tx)

                def p1_vector_tx(v):
                    for i in range(NTT):
                        for c in range(ND):
                            j = i * ND + c
                            v.wait_ge(xt1_mm, j + 1)
                            v.tensor_copy(xf[c][:, i * 128:(i + 1) * 128],
                                          ps_tx[j % 2][:, :]).then_inc(xt1_ev, 1)
                blk.vector(p1_vector_tx)

                def p1b_vector(v):
                    v.wait_ge(sm_sem, 32)
                    v.memset(ckvp[:, :], 0.0)
                    v.drain()
                    for s in range(4):
                        v.wait_ge(cks[s % 2], 16 * (s // 2 + 1))
                        v.scalar_tensor_tensor(ckvp[:, :], ckvs[s % 2][:, :],
                                               mcur_t[:, s:s + 1], ckvp[:, :],
                                               OP.mult, OP.add)
                        v.drain()
                        v.engine_nop().then_inc(ck_sem, 1)
                    for h in range(H0):
                        v.tensor_copy(ckvp_bf[64 * (h % 2):64 * (h % 2) + 64,
                                              (h // 2) * DH0:(h // 2) * DH0 + DH0],
                                      ckvp[:, h * DH0:(h + 1) * DH0])
                    v.wait_ge(ld1, N_LD1 + 16)
                    v.reciprocal(invt[:, :], invt[:, :]).then_inc(ck_sem, 1)
                    v.drain()
                blk.vector(p1b_vector)

                ctx.barrier()

                def p1b_sync(sy):
                    for c in range(ND):
                        if c >= 2:
                            sy.wait_ge(cr_mm, (c - 1) * 2 * NTC)
                        sy.dma_start(out=qT[c % 2][:, :], in_=qbf[c * 128:(c + 1) * 128, :]
                                     ).then_inc(qst[c % 2], 16)
                blk.sync(p1b_sync)

                def p1b_tensor(t):
                    t.wait_ge(ck_sem, 5)
                    for tc in range(NTC):
                        t.matmul(ps_iv[tc][:, :], ones1_f[:, :],
                                 invt[:, tc * 512:(tc + 1) * 512],
                                 start=True, stop=True).then_inc(iv_mm, 1)
                    for c in range(ND):
                        t.wait_ge(qst[c % 2], 16 * (c // 2 + 1))
                        for hh in range(2):
                            h = c * 2 + hh
                            hs = slice(64 * hh, 64 * hh + 64)
                            for tc in range(NTC):
                                u = h * NTC + tc
                                if u >= 2:
                                    t.wait_ge(cr_ev, u - 1)
                                t.matmul(ps_cr[u % 2][:, :],
                                         ckvp_bf[hs, (h // 2) * DH0:(h // 2) * DH0 + DH0],
                                         qT[c % 2][hs, tc * 512:(tc + 1) * 512],
                                         start=True, stop=True).then_inc(cr_mm, 1)
                blk.tensor(p1b_tensor)

                def p1b_vector2(v):
                    v.wait_ge(iv_mm, NTC)
                    for h in range(H0):
                        hs = slice(64 * (h % 2), 64 * (h % 2) + 64)
                        for tc in range(NTC):
                            u = h * NTC + tc
                            sl = slice(tc * 512, (tc + 1) * 512)
                            v.wait_ge(cr_mm, u + 1)
                            v.scalar_tensor_tensor(scr1[hs, :], ps_cr[u % 2][:, :], 1.0,
                                                   apT[h // 2][hs, sl], OP.mult, OP.add)
                            v.drain()
                            v.tensor_mul(apT[h // 2][hs, sl], scr1[hs, :],
                                         ps_iv[tc][hs, :]).then_inc(cr_ev, 1)
                            v.drain()
                blk.vector(p1b_vector2)

                ctx.barrier()

                wos = [ctx.sem("wos0"), ctx.sem("wos1")]

                def p1c_sync(sy):
                    sy.wait_ge(wc_sem, 32)
                    for oc in range(ND):
                        if oc >= 2:
                            sy.wait_ge(wm_mm, (oc - 1) * NTC)
                        sy.dma_start(out=woT[oc % 2][:, :],
                                     in_=wobf[oc * 128:(oc + 1) * 128, :]
                                     ).then_inc(wos[oc % 2], 16)
                blk.sync(p1c_sync)

                def p1c_tensor(t):
                    for oc in range(ND):
                        t.wait_ge(wos[oc % 2], 16 * (oc // 2 + 1))
                        for tc in range(NTC):
                            u = oc * NTC + tc
                            if u >= 2:
                                t.wait_ge(x1_ev, 2 * (u - 1))
                            for d in range(ND):
                                ins = t.matmul(ps_iv[u % 2][:, :],
                                               woT[oc % 2][:, d * 128:(d + 1) * 128],
                                               apT[d][:, tc * 512:(tc + 1) * 512],
                                               start=(d == 0), stop=(d == ND - 1))
                            ins.then_inc(wm_mm, 1)
                blk.tensor(p1c_tensor)

                def p1c_vector(v):
                    for oc in range(ND):
                        for tc in range(NTC):
                            u = oc * NTC + tc
                            sl = slice(tc * 512, (tc + 1) * 512)
                            v.wait_ge(wm_mm, u + 1)
                            v.tensor_mul(scr2[:, :], ps_iv[u % 2][:, :], gT[oc][:, sl]
                                         ).then_inc(x1_ev, 1)
                            v.drain()
                            v.tensor_add(xf[oc][:, sl], xf[oc][:, sl], scr2[:, :]
                                         ).then_inc(x1_ev, 1)
                            v.drain()
                blk.vector(p1c_vector)

                def p1c_gpsimd(g):
                    for oc in range(ND):
                        g.wait_ge(x1_ev, 2 * (oc + 1) * NTC)
                        g.dma_start(out=x1_out[oc * 128:(oc + 1) * 128, :],
                                    in_=xf[oc][:, :]).then_inc(x1o_sem, 16)
                    g.wait_ge(x1o_sem, 16 * ND)
                blk.gpsimd(p1c_gpsimd)

                ctx.barrier()

            with ExitStack() as s2:
                sqb = [ctx.sb(f"sqb{i}", [128, 512], BF16, s2) for i in range(2)]
                xb = [ctx.sb(f"xb{i}", [128, 512], BF16, s2) for i in range(2)]
                sr = ctx.sb("sr", [1, 512], F32, s2)
                qr = ctx.sb("qr", [1, 512], F32, s2)
                mn = ctx.sb("mn", [1, 512], F32, s2)
                vr = ctx.sb("vr", [1, 512], F32, s2)
                rs = ctx.sb("rs", [1, 512], F32, s2)
                mr = ctx.sb("mr", [1, 512], F32, s2)
                tscr = ctx.sb("tscr", [128, 512], F32, s2)

                ps_s = [ctx.ps(f"ps_s{i}", [1, 512], F32, s2) for i in range(2)]
                ps_q = [ctx.ps(f"ps_q{i}", [1, 512], F32, s2) for i in range(2)]
                ps_r = [ctx.ps(f"ps_r{i}", [128, 512], F32, s2) for i in range(2)]
                ps_m = [ctx.ps(f"ps_m{i}", [128, 512], F32, s2) for i in range(2)]

                sq_sem = ctx.sem("sq_sem")
                tcc_sem = ctx.sem("tcc_sem")
                mm2_sem = ctx.sem("mm2_sem")
                row2_sem = ctx.sem("row2_sem")
                rst_sem = ctx.sem("rst_sem")
                bc2_sem = ctx.sem("bc2_sem")
                xn_sem = ctx.sem("xn_sem")

                def p2_scalar(s):
                    for tc in range(NTC):
                        sl = slice(tc * 512, (tc + 1) * 512)
                        for c in range(ND):
                            u = tc * ND + c
                            if u >= 2:
                                tcp, cp = divmod(u - 2, ND)
                                if cp < ND - 1:
                                    s.wait_ge(tcc_sem, tcp * (ND - 1) + cp + 1)
                                else:
                                    s.wait_ge(mm2_sem, tcp + 1)
                            s.activation(out=sqb[c % 2][:, :], in_=xf[c][:, sl],
                                         func=AF.Square)
                            s.drain()
                            s.activation(out=xb[c % 2][:, :], in_=xf[c][:, sl],
                                         func=AF.Copy).then_inc(sq_sem, 1)
                        s.wait_ge(row2_sem, 3 * tc + 2)
                        s.activation(out=rs[:, :], in_=vr[:, :], func=AF.Ln, bias=1e-5)
                        s.drain()
                        s.activation(out=rs[:, :], in_=rs[:, :], func=AF.Exp,
                                     scale=-0.5).then_inc(rst_sem, 1)
                        s.drain()
                blk.scalar(p2_scalar)

                def p2_tensor(t):
                    for tc in range(NTC):
                        for c in range(ND):
                            u = tc * ND + c
                            t.wait_ge(sq_sem, u + 1)
                            t.matmul(ps_q[tc % 2][:, :], ones_bf[:, :], sqb[c % 2][:, :],
                                     start=(c == 0), stop=(c == ND - 1))
                            ins = t.matmul(ps_s[tc % 2][:, :], ones_bf[:, :], xb[c % 2][:, :],
                                           start=(c == 0), stop=(c == ND - 1))
                            if c < ND - 1:
                                ins.then_inc(tcc_sem, 1)
                            else:
                                ins.then_inc(mm2_sem, 1)
                        t.wait_ge(row2_sem, 3 * tc + 3)
                        if tc >= 2:
                            t.wait_ge(xn_sem, (tc - 1) * ND)
                        t.matmul(ps_r[tc % 2][:, :], ones1_f[:, :], rs[:, :],
                                 start=True, stop=True)
                        t.matmul(ps_m[tc % 2][:, :], ones1_f[:, :], mr[:, :],
                                 start=True, stop=True).then_inc(bc2_sem, 1)
                blk.tensor(p2_tensor)

                def p2_vector(v):
                    for tc in range(NTC):
                        sl = slice(tc * 512, (tc + 1) * 512)
                        v.wait_ge(mm2_sem, tc + 1)
                        v.tensor_copy(sr[:, :], ps_s[tc % 2][:, :])
                        v.tensor_copy(qr[:, :], ps_q[tc % 2][:, :]).then_inc(row2_sem, 1)
                        v.drain()
                        v.tensor_scalar_mul(mn[:, :], sr[:, :], 1.0 / D)
                        v.tensor_scalar_mul(vr[:, :], qr[:, :], 1.0 / D)
                        v.drain()
                        v.tensor_mul(sr[:, :], mn[:, :], mn[:, :])
                        v.drain()
                        v.tensor_sub(vr[:, :], vr[:, :], sr[:, :]).then_inc(row2_sem, 1)
                        v.wait_ge(rst_sem, tc + 1)
                        v.drain()
                        v.tensor_mul(mr[:, :], mn[:, :], rs[:, :]).then_inc(row2_sem, 1)
                        v.drain()
                        v.wait_ge(bc2_sem, tc + 1)
                        for c in range(ND):
                            v.tensor_mul(tscr[:, :], xf[c][:, sl], ps_r[tc % 2][:, :])
                            v.drain()
                            v.tensor_sub(xn1T[c][:, sl], tscr[:, :],
                                         ps_m[tc % 2][:, :]).then_inc(xn_sem, 1)
                            v.drain()
                blk.vector(p2_vector)

                ctx.barrier()

        with ExitStack() as s3:
            wbuf = [[ctx.sb(f"wbuf{p}_{d}", [128, 128], BF16, s3) for d in range(ND)]
                    for p in range(2)]
            zb = [ctx.sb(f"zb{i}", [128, TPC], BF16, s3) for i in range(2)]
            dtt = ctx.sb("dtt", [NH1, TPC], F32, s3)

            ps_w3 = [ctx.ps(f"ps_w3{i}", [128, 512], F32, s3) for i in range(8)]

            wi_sem = [ctx.sem("wi_sem0"), ctx.sem("wi_sem1")]
            w3_mm = ctx.sem("w3_mm")
            w3_ev = ctx.sem("w3_ev")
            w3_st = [ctx.sem("w3_st0"), ctx.sem("w3_st1")]

            NOC = D_IN_PROJ // 128 + 1

            def ocrows(oc):
                return 32 if oc == NOC - 1 else 128

            def p3_sync(sy):
                sy.wait_ge(wc_sem, 32)
                for oc in range(NOC):
                    p = oc % 2
                    if oc >= 2:
                        sy.wait_ge(w3_mm, (oc - 1) * NTC)
                    for d in range(ND):
                        sy.dma_start(
                            out=wbuf[p][d][:, 0:ocrows(oc)],
                            in_=winbf[d * 128:(d + 1) * 128,
                                      oc * 128:oc * 128 + ocrows(oc)]
                        ).then_inc(wi_sem[p], 16)
            blk.sync(p3_sync)

            def p3_tensor(t):
                for oc in range(NOC):
                    p = oc % 2
                    t.wait_ge(wi_sem[p], 16 * ND * (oc // 2 + 1))
                    for tc in range(NTC):
                        u = oc * NTC + tc
                        if u >= 8:
                            t.wait_ge(w3_ev, u - 7)
                    ins = None
                    for d in range(ND):
                        for tc in range(NTC):
                            u = oc * NTC + tc
                            ins = t.matmul(ps_w3[u % 8][0:ocrows(oc), :],
                                           wbuf[p][d][:, 0:ocrows(oc)],
                                           xn1T[d][:, tc * 512:(tc + 1) * 512],
                                           start=(d == 0), stop=(d == ND - 1))
                            if d == ND - 1:
                                ins.then_inc(w3_mm, 1)
            blk.tensor(p3_tensor)

            def p3_scalar(s):
                for oc in range(NOC):
                    for tc in range(NTC):
                        u = oc * NTC + tc
                        s.wait_ge(w3_mm, u + 1)
                        if oc < 16:
                            if u >= 2:
                                s.wait_ge(w3_st[u % 2], 16 * ((u - 2) // 2 + 1))
                            s.activation(out=zb[u % 2][:, tc * 512:(tc + 1) * 512],
                                         in_=ps_w3[u % 8][:, :],
                                         func=AF.Silu).then_inc(w3_ev, 1)
                        elif oc < NOC - 1:
                            if u >= 2:
                                s.wait_ge(w3_st[u % 2], 16 * ((u - 2) // 2 + 1))
                            s.activation(out=zb[u % 2][:, tc * 512:(tc + 1) * 512],
                                         in_=ps_w3[u % 8][:, :],
                                         func=AF.Copy).then_inc(w3_ev, 1)
                        else:
                            s.wait_ge(w3_ev, u)
                            s.activation(out=dtt[:, tc * 512:(tc + 1) * 512],
                                         in_=ps_w3[u % 8][0:NH1, :],
                                         func=AF.Copy).then_inc(w3_ev, 1)
            blk.scalar(p3_scalar)

            def p3_gpsimd(g):
                for oc in range(NOC):
                    for tc in range(NTC):
                        u = oc * NTC + tc
                        if oc < NOC - 1:
                            g.wait_ge(w3_ev, u + 1)
                            dst = szo if oc < 16 else xbco
                            row = oc * 128 if oc < 16 else (oc - 16) * 128
                            g.dma_start(out=dst[row:row + 128, tc * 512:(tc + 1) * 512],
                                        in_=zb[u % 2][:, tc * 512:(tc + 1) * 512]
                                        ).then_inc(w3_st[u % 2], 16)
                g.wait_ge(w3_ev, NOC * NTC)
                g.dma_start(out=dto[:, :], in_=dtt[:, :]).then_inc(w3_st[0], 16)
                g.wait_ge(w3_st[0], 16 * ((NOC - 1) * NTC // 2 + ((NOC - 1) * NTC) % 2 + 1))
                g.wait_ge(w3_st[1], 16 * ((NOC - 1) * NTC // 2))
            blk.gpsimd(p3_gpsimd)

            ctx.barrier()

        m_tok = [ctx.sb(f"m_tok{i}", [128, E1], F32) for i in range(NTT)]
        with ExitStack() as s4:
            simt = [ctx.sb(f"simt{c}", [128, E1], F32, s4) for c in range(ND)]
            simn = [ctx.sb(f"simn{c}", [128, E1], BF16, s4) for c in range(ND)]
            sq8 = [ctx.sb(f"sq8_{c}", [128, 512], BF16, s4) for c in range(ND)]
            snr = ctx.sb("snr", [1, E1], F32, s4)
            grow = ctx.sb("grow", [1, E1], F32, s4)
            grep = ctx.sb("grep", [128, E1], F32, s4)
            rncol = [ctx.sb(f"rncol{i}", [128, 1], F32, s4) for i in range(NTT)]
            lgT = ctx.sb("lgT", [E1, TPC], F32, s4)
            lt = [ctx.sb(f"lt{i}", [128, E1], F32, s4) for i in range(NTT)]
            ga = [ctx.sb(f"ga{i}", [128, E1], F32, s4) for i in range(NTT)]
            eqt = [ctx.sb(f"eqt{i}", [128, E1], F32, s4) for i in range(NTT)]
            mxt = [ctx.sb(f"mxt{i}", [128, 1], F32, s4) for i in range(NTT)]
            smt = [ctx.sb(f"smt{i}", [128, 1], F32, s4) for i in range(NTT)]
            mrw = [ctx.sb(f"mrw{i}", [128, E1], F32, s4) for i in range(NTT)]
            srow4 = ctx.sb("srow4", [1, TPC], F32, s4)

            ps_a = [ctx.ps(f"ps_a{i}", [128, 512], F32, s4) for i in range(4)]
            ps_c = [ctx.ps(f"ps_c{i}", [128, 16], F32, s4) for i in range(2)]
            ps_d = [ctx.ps(f"ps_d{i}", [E1, 128], F32, s4) for i in range(2)]

            steps = []

            def s_load(sy, dma):
                for c in range(ND):
                    dma(simt[c][:, :], simm[c * 128:(c + 1) * 128, :])
            steps.append(("sync", s_load))

            def s_simsq(s, dma):
                for c in range(ND):
                    s.activation(out=simn[c][:, :], in_=simt[c][:, :], func=AF.Square)
            steps.append(("scalar", s_simsq))

            def s_colsum(t, dma):
                for c in range(ND):
                    t.matmul(ps_c[0][0:1, 0:E1], ones_bf[:, :], simn[c][:, :],
                             start=(c == 0), stop=(c == ND - 1))
                t.transpose(ps_c[1][0:1, 0:E1], gates_t[:, :], ident_f[0:E1, 0:E1])
                
            steps.append(("tensor", s_colsum))

            def s_snrow(v, dma):
                v.tensor_copy(snr[:, :], ps_c[0][0:1, 0:E1])
                v.tensor_copy(grow[:, :], ps_c[1][0:1, 0:E1])
            steps.append(("vector", s_snrow))

            def s_snmath(s, dma):
                s.activation(out=snr[:, :], in_=snr[:, :], func=AF.Ln, bias=1e-20)
                s.drain()
                s.activation(out=snr[:, :], in_=snr[:, :], func=AF.Exp, scale=0.5)
                s.drain()
                s.activation(out=grow[:, :], in_=grow[:, :], func=AF.Sigmoid)
            steps.append(("scalar", s_snmath))

            def s_snfin(v, dma):
                v.tensor_scalar_max(snr[:, :], snr[:, :], 1e-12)
                v.drain()
                v.reciprocal(snr[:, :], snr[:, :])
            steps.append(("vector", s_snfin))

            def s_snbc(t, dma):
                t.matmul(ps_c[0][:, 0:E1], ones1_f[:, :], snr[:, :],
                         start=True, stop=True)
                t.matmul(ps_c[1][:, 0:E1], ones1_f[:, :], grow[:, :],
                         start=True, stop=True)
            steps.append(("tensor", s_snbc))

            def s_simn(v, dma):
                v.tensor_copy(grep[:, :], ps_c[1][:, 0:E1])
                for c in range(ND):
                    v.tensor_mul(simn[c][:, :], simt[c][:, :], ps_c[0][:, 0:E1])
            steps.append(("vector", s_simn))

            for tc in range(NTC):
                def s_sq(s, dma, tc=tc):
                    for c in range(ND):
                        s.activation(out=sq8[c][:, :],
                                     in_=xn1T[c][:, tc * 512:(tc + 1) * 512],
                                     func=AF.Square)
                steps.append(("scalar", s_sq))

                def s_sum(t, dma, tc=tc):
                    for c in range(ND):
                        t.matmul(ps_a[tc % 4][0:1, :], ones_bf[:, :], sq8[c][:, :],
                                 start=(c == 0), stop=(c == ND - 1))
                steps.append(("tensor", s_sum))

                def s_ev(v, dma, tc=tc):
                    v.tensor_copy(srow4[:, tc * 512:(tc + 1) * 512], ps_a[tc % 4][0:1, :])
                steps.append(("vector", s_ev))

            for tc in range(NTC):
                def s_lg(t, dma, tc=tc):
                    for d in range(ND):
                        t.matmul(ps_a[tc % 4][0:E1, :], simn[d][:, :],
                                 xn1T[d][:, tc * 512:(tc + 1) * 512],
                                 start=(d == 0), stop=(d == ND - 1))
                steps.append(("tensor", s_lg))

                def s_lgev(v, dma, tc=tc):
                    v.tensor_copy(lgT[:, tc * 512:(tc + 1) * 512], ps_a[tc % 4][0:E1, :])
                steps.append(("vector", s_lgev))

            def s_tr(t, dma):
                for tt in range(NTT):
                    t.transpose(ps_c[tt % 2][:, 0:1], srow4[:, tt * 128:(tt + 1) * 128],
                                ident_f[:, :])
                    t.transpose(ps_c[2 + tt % 2][:, 0:E1],
                                lgT[:, tt * 128:(tt + 1) * 128],
                                ident_f[0:E1, 0:E1])
                    if tt % 2 == 1 or tt == NTT - 1:
                        pass
            for tt in range(NTT):
                def s_tr1(t, dma, tt=tt):
                    t.transpose(ps_c[tt % 2][:, 0:1], srow4[:, tt * 128:(tt + 1) * 128],
                                ident_f[0:1, 0:1])
                    t.transpose(ps_c[tt % 2][:, 1:1 + E1],
                                lgT[:, tt * 128:(tt + 1) * 128],
                                ident_f[0:E1, 0:E1])
                steps.append(("tensor", s_tr1))

                def s_tr1e(v, dma, tt=tt):
                    v.tensor_copy(rncol[tt][:, :], ps_c[tt % 2][:, 0:1])
                    v.tensor_copy(lt[tt][:, :], ps_c[tt % 2][:, 1:1 + E1])
                steps.append(("vector", s_tr1e))

            def s_rn1(s, dma):
                for tt in range(NTT):
                    s.activation(out=rncol[tt][:, :], in_=rncol[tt][:, :],
                                 func=AF.Ln, bias=1e-20)
                s.drain()
                for tt in range(NTT):
                    s.activation(out=rncol[tt][:, :], in_=rncol[tt][:, :],
                                 func=AF.Exp, scale=0.5)
            steps.append(("scalar", s_rn1))

            def s_rn2(v, dma):
                for tt in range(NTT):
                    v.tensor_scalar_max(rncol[tt][:, :], rncol[tt][:, :], 1e-12)
                v.drain()
                for tt in range(NTT):
                    v.reciprocal(rncol[tt][:, :], rncol[tt][:, :])
            steps.append(("vector", s_rn2))

            def s_route1(v, dma):
                for tt in range(NTT):
                    v.tensor_single_scalar(lt[tt][:, :], lt[tt][:, :], rncol[tt][:, :],
                                           OP.mult)
                v.drain()
                for tt in range(NTT):
                    v.tensor_sub(lt[tt][:, :], lt[tt][:, :], grep[:, :])
                v.drain()
                for tt in range(NTT):
                    v.tensor_relu(ga[tt][:, :], lt[tt][:, :])
                v.drain()
                for tt in range(NTT):
                    v.tensor_single_scalar(m_tok[tt][:, :], ga[tt][:, :], 0.0, OP.is_gt)
                v.drain()
                for tt in range(NTT):
                    v.reduce_max(mxt[tt][:, :], lt[tt][:, :], axis=AX.X)
                v.drain()
                for tt in range(NTT):
                    v.tensor_single_scalar(eqt[tt][:, :], lt[tt][:, :], mxt[tt][:, :],
                                           OP.is_equal)
                v.drain()
                for tt in range(NTT):
                    v.tensor_scalar_mul(eqt[tt][:, :], eqt[tt][:, :], -1e30)
                v.drain()
                for tt in range(NTT):
                    v.tensor_add(eqt[tt][:, :], eqt[tt][:, :], lt[tt][:, :])
                v.drain()
                for tt in range(NTT):
                    v.reduce_max(mxt[tt][:, :], eqt[tt][:, :], axis=AX.X)
                v.drain()
                for tt in range(NTT):
                    v.tensor_single_scalar(eqt[tt][:, :], lt[tt][:, :], mxt[tt][:, :],
                                           OP.is_ge)
                v.drain()
                for tt in range(NTT):
                    v.reduce_sum(smt[tt][:, :], m_tok[tt][:, :], axis=AX.X)
                v.drain()
                for tt in range(NTT):
                    v.tensor_single_scalar(smt[tt][:, :], smt[tt][:, :], 0.0, OP.is_equal)
                v.drain()
                for tt in range(NTT):
                    v.tensor_single_scalar(eqt[tt][:, :], eqt[tt][:, :], smt[tt][:, :],
                                           OP.mult)
                v.drain()
                for tt in range(NTT):
                    v.tensor_max(m_tok[tt][:, :], m_tok[tt][:, :], eqt[tt][:, :])
                v.drain()
                for tt in range(NTT):
                    v.tensor_scalar(eqt[tt][:, :], m_tok[tt][:, :], 1.0, 1e9,
                                    OP.subtract, OP.mult)
                v.drain()
                for tt in range(NTT):
                    v.tensor_mul(ga[tt][:, :], ga[tt][:, :], m_tok[tt][:, :])
                v.drain()
                for tt in range(NTT):
                    v.tensor_add(ga[tt][:, :], ga[tt][:, :], eqt[tt][:, :])
                v.drain()
                for tt in range(NTT):
                    v.reduce_max(mxt[tt][:, :], ga[tt][:, :], axis=AX.X)
                v.drain()
                for tt in range(NTT):
                    v.tensor_scalar_mul(mxt[tt][:, :], mxt[tt][:, :], -1.0)
            steps.append(("vector", s_route1))

            def s_exp(s, dma):
                for tt in range(NTT):
                    s.activation(out=mrw[tt][:, :], in_=ga[tt][:, :], func=AF.Exp,
                                 bias=mxt[tt][:, :])
            steps.append(("scalar", s_exp))

            def s_route2(v, dma):
                for tt in range(NTT):
                    v.reduce_sum(smt[tt][:, :], mrw[tt][:, :], axis=AX.X)
                v.drain()
                for tt in range(NTT):
                    v.reciprocal(smt[tt][:, :], smt[tt][:, :])
                v.drain()
                for tt in range(NTT):
                    v.tensor_single_scalar(mrw[tt][:, :], mrw[tt][:, :], smt[tt][:, :],
                                           OP.mult)
                v.drain()
                for tt in range(NTT):
                    v.tensor_mul(mrw[tt][:, :], mrw[tt][:, :], m_tok[tt][:, :])
            steps.append(("vector", s_route2))

            for tt in range(NTT):
                def s_trb(t, dma, tt=tt):
                    t.transpose(ps_d[tt % 2][:, :], mrw[tt][:, :], ident_f[:, :])
                steps.append(("tensor", s_trb))

                def s_trbe(v, dma, tt=tt):
                    v.tensor_copy(lgT[:, tt * 128:(tt + 1) * 128], ps_d[tt % 2][:, :])
                steps.append(("vector", s_trbe))

            def s_mrwout(g, dma):
                dma(mrwo[:, :], lgT[:, :])
            steps.append(("gpsimd", s_mrwout))

            ctx.seq("p4", steps)
            ctx.barrier()

        with ExitStack() as s5:
            wq8 = [[ctx.sb(f"wq8_{e}_{d}", [128, 3 * DV1], BF16, s5) for d in range(ND)]
                   for e in range(E1)]
            qacc = [ctx.sb(f"qacc{i}", [128, 3 * DV1], BF16, s5) for i in range(NTT)]
            ps5 = [ctx.ps(f"ps5_{i}", [128, 3 * DV1], F32, s5) for i in range(8)]

            w5 = ctx.sem("w5")
            in5 = ctx.sem("in5")
            mm5 = ctx.sem("mm5")
            ev5 = ctx.sem("ev5")
            st5 = [ctx.sem("st5_0"), ctx.sem("st5_1")]

            def p5_sync(sy):
                sy.wait_ge(qc_sem, 16)
                for e in range(E1):
                    for d in range(ND):
                        sy.dma_start(out=wq8[e][d][:, :],
                                     in_=qkpbf[e, d * 128:(d + 1) * 128, :]
                                     ).then_inc(w5, 16)
            blk.sync(p5_sync)

            def p5_vector0(v):
                for tt in range(NTT):
                    v.memset(qacc[tt][:, :], 0.0)
                v.drain()
                v.engine_nop().then_inc(in5, 1)
            blk.vector(p5_vector0)

            def p5_tensor(t):
                t.wait_ge(w5, 16 * E1 * ND)
                t.wait_ge(in5, 1)
                for tt in range(NTT):
                    for eg in range(2):
                        g = tt * 2 + eg
                        if g >= 2:
                            t.wait_ge(ev5, 4 * (g - 1))
                        for d in range(ND):
                            for e4 in range(4):
                                e = eg * 4 + e4
                                ins = t.matmul(ps5[(g % 2) * 4 + e4][:, :],
                                               xn1T[d][:, tt * 128:(tt + 1) * 128],
                                               wq8[e][d][:, :],
                                               start=(d == 0), stop=(d == ND - 1))
                        ins.then_inc(mm5, 1)
            blk.tensor(p5_tensor)

            def p5_vector(v):
                for tt in range(NTT):
                    for eg in range(2):
                        g = tt * 2 + eg
                        v.wait_ge(mm5, g + 1)
                        for e4 in range(4):
                            e = eg * 4 + e4
                            v.scalar_tensor_tensor(
                                qacc[tt][:, :], ps5[(g % 2) * 4 + e4][:, :],
                                m_tok[tt][:, e:e + 1], qacc[tt][:, :],
                                OP.mult, OP.add).then_inc(ev5, 1)
                            v.drain()
                blk_dummy = None
            blk.vector(p5_vector)

            def p5_gpsimd(g):
                for tt in range(NTT):
                    g.wait_ge(ev5, 4 * (tt * 2 + 2))
                    g.dma_start(out=qkvo[tt * 128:(tt + 1) * 128, :],
                                in_=qacc[tt][:, :]).then_inc(st5[tt % 2], 16)
                g.wait_ge(st5[0], 16 * (NTT // 2))
                g.wait_ge(st5[1], 16 * (NTT // 2))
            blk.gpsimd(p5_gpsimd)

            ctx.barrier()

    return nc, dbg_tensors



def build_launch3(debug=(), upto='Z'):
    nc = bass.Bass("TRN2", target_bir_lowering=False)
    dbg = set(debug)

    xbcp = nc.dram_tensor("xbcp", [CONV_DIM, TPC], BF16, kind="ExternalInput").ap()
    bnd3 = nc.dram_tensor("bnd3", [CONV_DIM, 4], BF16, kind="ExternalInput").ap()
    convw = nc.dram_tensor("convw", [CONV_DIM, DCONV], F32, kind="ExternalInput").ap()
    convb = nc.dram_tensor("convb", [CONV_DIM, 1], F32, kind="ExternalInput").ap()
    dtraw = nc.dram_tensor("dtraw", [NH1, TPC], F32, kind="ExternalInput").ap()
    dtb = nc.dram_tensor("dtb", [NH1, 1], F32, kind="ExternalInput").ap()
    alog = nc.dram_tensor("alog", [NH1, 1], F32, kind="ExternalInput").ap()
    dpar = nc.dram_tensor("dpar", [1, NH1], F32, kind="ExternalInput").ap()
    qtl = nc.dram_tensor("qtl", [TPC, DV1], BF16, kind="ExternalInput").ap()
    KROWS = 160 * 128
    kcat = nc.dram_tensor("kcat", [KROWS, DV1], BF16, kind="ExternalInput").ap()
    vcat = nc.dram_tensor("vcat", [KROWS, DV1], BF16, kind="ExternalInput").ap()
    kdg = nc.dram_tensor("kdg", [TPC, DV1], BF16, kind="ExternalInput").ap()
    vdgd = nc.dram_tensor("vdg", [TPC, DV1], BF16, kind="ExternalInput").ap()
    mvals = nc.dram_tensor("mvals", [160, 512], F32, kind="ExternalInput").ap()
    mrwg = nc.dram_tensor("mrwg", [E1, TPC], F32, kind="ExternalInput").ap()
    oproj = nc.dram_tensor("oproj", [E1, DV1, D], F32, kind="ExternalInput").ap()

    ylo = nc.dram_tensor("ylocal_out", [TPC, D_INNER], BF16, kind="ExternalOutput").ap()
    sto = nc.dram_tensor("states_out", [N1, NH1 * P1], BF16, kind="ExternalOutput").ap()
    eco = nc.dram_tensor("expcum_out", [TPC, NH1], F32, kind="ExternalOutput").ap()
    dco = nc.dram_tensor("decay_out", [NH1, 1], F32, kind="ExternalOutput").ap()
    co = nc.dram_tensor("C_out", [N1, TPC], BF16, kind="ExternalOutput").ap()
    g1o = nc.dram_tensor("g1_out", [D, TPC], BF16, kind="ExternalOutput").ap()

    xoh = nc.dram_tensor("xoh", [D_INNER, TPC], BF16, kind="ExternalOutput").ap()
    bhb = nc.dram_tensor("bhb", [N1, TPC], BF16).ap()
    cumhb = nc.dram_tensor("cumhb", [NH1, TPC], F32).ap()
    opbf = nc.dram_tensor("opbf", [E1, DV1, D], BF16).ap()

    dbg_tensors = {}
    def dbg_out(name, shape, dtype=F32):
        if name in dbg:
            dbg_tensors[name] = nc.dram_tensor("dbg_" + name, shape, dtype,
                                               kind="ExternalOutput").ap()
            return dbg_tensors[name]
        return None

    NBC = CONV_DIM // 128
    with ExitStack() as st:
        ctx = Ctx(nc, st, const_vals=(1e-5, 1e-20, -1e30, -1e9, 1e-12, 30.0))
        blk = ctx.blk

        ones_bf = ctx.sb("ones_bf", [128, 1], BF16)
        ones1_f = ctx.sb("ones1_f", [1, 128], F32)
        iv = ctx.sb("iv", [128, 128], F32)
        ident_f = ctx.sb("ident_f", [128, 128], F32)
        ident_bf = ctx.sb("ident_bf", [128, 128], BF16)
        bigmask = ctx.sb("bigmask", [128, 128], F32)
        bigmask4 = ctx.sb("bigmask4", [128, 512], F32)
        dmaskadd = ctx.sb("dmaskadd", [128, 128], F32)

        oc_sem = ctx.sem("oc_sem")
        k_sem = ctx.sem("k_sem")

        def p0_gpsimd(g):
            g.dma_start(out=opbf[:, :, :], in_=oproj[:, :, :]).then_inc(oc_sem, 16)
            g.iota(iv[:, :], pattern=[[1, 128]], base=0, channel_multiplier=-1,
                   allow_small_or_imprecise_dtypes=True).then_inc(k_sem, 1)
        blk.gpsimd(p0_gpsimd)

        def p0_vector(v):
            ctx.emit_const_memsets(v)
            v.memset(ones_bf[:, :], 1.0)
            v.memset(ones1_f[:, :], 1.0)
            v.wait_ge(k_sem, 1)
            v.tensor_single_scalar(ident_f[:, :], iv[:, :], 0.0, OP.is_equal)
            v.drain()
            v.tensor_copy(ident_bf[:, :], ident_f[:, :])
            v.tensor_single_scalar(bigmask[:, :], iv[:, :], 0.0, OP.is_lt)
            v.drain()
            v.tensor_scalar_mul(bigmask[:, :], bigmask[:, :], -1e9)
            v.drain()
            for _i4 in range(4):
                v.tensor_copy(bigmask4[:, _i4 * 128:(_i4 + 1) * 128], bigmask[:, :])
            v.tensor_single_scalar(dmaskadd[:, :], iv[:, :], 0.0, OP.is_lt)
            v.drain()
            v.tensor_scalar_mul(dmaskadd[:, :], dmaskadd[:, :], -1e9)
        blk.vector(p0_vector)

        ctx.barrier()

        xbcs_bc = [ctx.sb(f"xbcs_bc{i}", [128, TPC], BF16) for i in range(2)]
        dtT = ctx.sb("dtT", [NH1, TPC], F32)
        cumc = ctx.sb("cumc", [NH1, TPC], F32)
        lnb = ctx.sb("lnb", [NH1, TPC], F32)
        wdtf = ctx.sb("wdtf", [NH1, TPC], F32)
        expcf = ctx.sb("expcf", [NH1, TPC], F32)
        tot_c = ctx.sb("tot_c", [NH1, 16], F32)
        base_c = ctx.sb("base_c", [NH1, 16], F32)
        dect = ctx.sb("dect", [NH1, 16], F32)
        bias_tok = [ctx.sb(f"bias_tok{i}", [128, NH1], F32) for i in range(NCH1)]
        expc_tok = [ctx.sb(f"expc_tok{i}", [128, NH1], F32) for i in range(NCH1)]
        wdt_tok = [ctx.sb(f"wdt_tok{i}", [128, NH1], F32) for i in range(NCH1)]
        dpar_t = ctx.sb("dpar_t", [128, NH1], F32)

        with ExitStack() as q1:
            xpad = [ctx.sb(f"xpad{i}", [128, TPC + 4], BF16, q1) for i in range(2)]
            cw = [ctx.sb(f"cw{i}", [128, DCONV], F32, q1) for i in range(2)]
            cb = [ctx.sb(f"cb{i}", [128, 1], F32, q1) for i in range(2)]
            cacc = [ctx.sb(f"cacc{i}", [128, TPC], F32, q1) for i in range(2)]
            xout = [ctx.sb(f"xout{i}", [128, TPC], BF16, q1) for i in range(2)]

            ld1 = [ctx.sem("ld1_0"), ctx.sem("ld1_1")]
            cv_ev = ctx.sem("cv_ev")
            sl_ev = ctx.sem("sl_ev")
            st1 = [ctx.sem("st1_0"), ctx.sem("st1_1")]

            def q1_sync(sy):
                for c in range(NBC):
                    p = c % 2
                    if c >= 2:
                        sy.wait_ge(sl_ev, c - 1)
                    sy.dma_start(out=xpad[p][:, 0:4], in_=bnd3[c * 128:(c + 1) * 128, :]
                                 ).then_inc(ld1[p], 16)
                    sy.dma_start(out=xpad[p][:, 4:4 + TPC],
                                 in_=xbcp[c * 128:(c + 1) * 128, :]).then_inc(ld1[p], 16)
                    sy.dma_start(out=cw[p][:, :], in_=convw[c * 128:(c + 1) * 128, :]
                                 ).then_inc(ld1[p], 16)
                    sy.dma_start(out=cb[p][:, :], in_=convb[c * 128:(c + 1) * 128, :]
                                 ).then_inc(ld1[p], 16)
            blk.sync(q1_sync)

            def q1_vector(v):
                for c in range(NBC):
                    p = c % 2
                    v.wait_ge(ld1[p], 64 * (c // 2 + 1))
                    if c >= 2:
                        v.wait_ge(sl_ev, c - 1)
                    v.tensor_single_scalar(cacc[p][:, :], xpad[p][:, 1:1 + TPC],
                                           cw[p][:, 0:1], OP.mult)
                    v.drain()
                    for kk in range(1, DCONV):
                        v.scalar_tensor_tensor(cacc[p][:, :],
                                               xpad[p][:, 1 + kk:1 + kk + TPC],
                                               cw[p][:, kk:kk + 1], cacc[p][:, :],
                                               OP.mult, OP.add)
                        v.drain()
                    v.engine_nop().then_inc(cv_ev, 1)
            blk.vector(q1_vector)

            def q1_scalar(s):
                for c in range(NBC):
                    p = c % 2
                    s.wait_ge(cv_ev, c + 1)
                    if c >= 2 and c < 16:
                        s.wait_ge(st1[p], 16 * ((c - 2) // 2 + 1))
                    dst = (xout[p] if c < 16 else xbcs_bc[c - 16])
                    s.activation(out=dst[:, :], in_=cacc[p][:, :], func=AF.Silu,
                                 bias=cb[p][:, :]).then_inc(sl_ev, 1)
            blk.scalar(q1_scalar)

            def q1_gpsimd(g):
                for c in range(NBC):
                    p = c % 2
                    g.wait_ge(sl_ev, c + 1)
                    if c < 16:
                        g.dma_start(out=xoh[c * 128:(c + 1) * 128, :], in_=xout[p][:, :]
                                    ).then_inc(st1[p], 16)
                    elif c == 16:
                        g.dma_start(out=bhb[:, :], in_=xbcs_bc[0][:, :]).then_inc(st1[0], 16)
                    else:
                        g.dma_start(out=co[:, :], in_=xbcs_bc[1][:, :]).then_inc(st1[1], 16)
                g.wait_ge(st1[0], 16 * 9)
                g.wait_ge(st1[1], 16 * 9)
            blk.gpsimd(q1_gpsimd)

            ctx.barrier()

        with ExitStack() as q2:
            scr = ctx.sb("scr_q2", [NH1, TPC], F32, q2)
            acol = ctx.sb("acol", [NH1, 1], F32, q2)
            ecst = [ctx.sb(f"ecst{i}", [128, NH1], F32, q2) for i in range(2)]
            ps_q2 = [ctx.ps(f"ps_q2{i}", [128, 128], F32, q2) for i in range(4)]

            steps = []

            def s_ld(sy, dma):
                dma(dtT[:, :], dtraw[:, :])
                dma(acol[:, :], dtb[:, :])
                dma(scr[:, 0:1], alog[:, :])
                dma(dpar_t[:, :], dpar[0:1, :].partition_broadcast(128))
            steps.append(("sync", s_ld))

            def s_dt1(v, dma):
                v.tensor_single_scalar(dtT[:, :], dtT[:, :], acol[:, :], OP.add)
            steps.append(("vector", s_dt1))

            def s_dt2(s, dma):
                s.activation(out=scr[:, 1:2], in_=scr[:, 0:1], func=AF.Exp)
                s.drain()
                s.activation(out=lnb[:, :], in_=dtT[:, :], func=AF.Abs)
                s.drain()
                s.activation(out=lnb[:, :], in_=lnb[:, :], func=AF.Exp, scale=-1.0)
                s.drain()
                s.activation(out=lnb[:, :], in_=lnb[:, :], func=AF.Ln, bias=1.0)
            steps.append(("scalar", s_dt2))

            def s_dt3(v, dma):
                v.tensor_scalar_mul(acol[:, :], scr[:, 1:2], -1.0)
                v.drain()
                v.tensor_scalar_max(scr[:, :], dtT[:, :], 0.0)
                v.drain()
                v.tensor_add(dtT[:, :], scr[:, :], lnb[:, :])
            steps.append(("vector", s_dt3))

            def s_dt4(s, dma):
                s.activation(out=lnb[:, :], in_=dtT[:, :], func=AF.Ln)
            steps.append(("scalar", s_dt4))

            def s_dt5(v, dma):
                v.tensor_single_scalar(scr[:, :], dtT[:, :], acol[:, :], OP.mult)
            steps.append(("vector", s_dt5))

            def s_cum(v, dma):
                for ch in range(NCH1):
                    sl = slice(ch * Q1, (ch + 1) * Q1)
                    v.tensor_tensor_scan(cumc[:, sl], scr[:, sl], scr[:, sl],
                                         0.0, OP.add, OP.bypass)
                v.drain()
                for ch in range(NCH1):
                    v.tensor_copy(tot_c[:, ch:ch + 1], cumc[:, ch * Q1 + Q1 - 1:ch * Q1 + Q1])
                v.drain()
                v.tensor_tensor_scan(base_c[:, :], tot_c[:, :], tot_c[:, :],
                                     0.0, OP.add, OP.bypass)
                v.drain()
                v.tensor_sub(base_c[:, :], base_c[:, :], tot_c[:, :])
                v.drain()
                v.tensor_sub(lnb[:, :], lnb[:, :], cumc[:, :])
            steps.append(("vector", s_cum))

            def s_exp1(s, dma):
                s.activation(out=expcf[:, :], in_=cumc[:, :], func=AF.Exp)
                for ch in range(NCH1):
                    sl = slice(ch * Q1, (ch + 1) * Q1)
                    s.activation(out=wdtf[:, sl], in_=cumc[:, sl], func=AF.Exp,
                                 scale=-1.0, bias=tot_c[:, ch:ch + 1])
                s.drain()
                s.activation(out=dect[:, :], in_=tot_c[:, :], func=AF.Exp)
            steps.append(("scalar", s_exp1))

            def s_wdt(v, dma):
                v.tensor_mul(wdtf[:, :], wdtf[:, :], dtT[:, :])
            steps.append(("vector", s_wdt))

            def s_ec(s, dma):
                for ch in range(NCH1):
                    sl = slice(ch * Q1, (ch + 1) * Q1)
                    s.activation(out=scr[:, sl], in_=cumc[:, sl], func=AF.Exp,
                                 bias=base_c[:, ch:ch + 1])
            steps.append(("scalar", s_ec))

            def s_dec(v, dma):
                v.tensor_copy(acol[:, :], scr[:, TPC - 1:TPC])
            steps.append(("vector", s_dec))

            def s_deco(g, dma):
                dma(dco[:, :], acol[:, :])
            steps.append(("gpsimd", s_deco))

            for ch in range(NCH1):
                def s_tr(t, dma, ch=ch):
                    sl = slice(ch * Q1, (ch + 1) * Q1)
                    t.transpose(ps_q2[0][:, 0:NH1], scr[:, sl], ident_f[0:NH1, 0:NH1])
                    t.transpose(ps_q2[1][:, 0:NH1], lnb[:, sl], ident_f[0:NH1, 0:NH1])
                    t.transpose(ps_q2[2][:, 0:NH1], expcf[:, sl], ident_f[0:NH1, 0:NH1])
                    t.transpose(ps_q2[3][:, 0:NH1], wdtf[:, sl], ident_f[0:NH1, 0:NH1])
                steps.append(("tensor", s_tr))

                def s_tre(v, dma, ch=ch):
                    v.tensor_copy(bias_tok[ch][:, :], ps_q2[1][:, 0:NH1])
                    v.tensor_copy(expc_tok[ch][:, :], ps_q2[2][:, 0:NH1])
                    v.tensor_copy(wdt_tok[ch][:, :], ps_q2[3][:, 0:NH1])
                    v.tensor_copy(ecst[ch % 2][:, :], ps_q2[0][:, 0:NH1])
                steps.append(("vector", s_tre))

                def s_ecg(g, dma, ch=ch):
                    dma(eco[ch * 128:(ch + 1) * 128, :], ecst[ch % 2][:, :])
                steps.append(("gpsimd", s_ecg))

            def s_cumout(g, dma):
                dma(cumhb[:, :], cumc[:, :])
            steps.append(("gpsimd", s_cumout))

            ctx.seq("q2", steps)
            ctx.barrier()

        if upto == 'B':
            return nc, dbg_tensors

        with ExitStack() as q4:
            X_tok = [ctx.sb(f"X_tok{i}", [128, D_INNER], BF16, q4) for i in range(NCH1)]
            B_tok = [ctx.sb(f"B_tok{i}", [128, N1], BF16, q4) for i in range(NCH1)]
            ylp = [ctx.sb(f"ylp{i}", [128, D_INNER], BF16, q4) for i in range(2)]
            Sst = [ctx.sb(f"Sst{i}", [128, D_INNER], BF16, q4) for i in range(2)]
            slsb = ctx.sb("slsb", [128, D_INNER], F32, q4)
            wdtX = ctx.sb("wdtX", [128, D_INNER], BF16, q4)
            CBm = ctx.sb("CBm", [128, 128], BF16, q4)
            expdt = [ctx.sb(f"expdt{i}", [128, 128], F32, q4) for i in range(2)]
            scT = [ctx.sb(f"scT{i}", [128, 128], BF16, q4) for i in range(NH1)]
            stage = ctx.sb("stage", [1, NH1 * 128], F32, q4)
            drow = ctx.sb("drow", [1, NH1], F32, q4)
            decn = ctx.sb("decn", [128, NH1], F32, q4)

            psTw = [ctx.ps(f"psTw{i}", [128, 512], F32, q4) for i in range(2)]
            psCB = ctx.ps("psCB", [128, 128], F32, q4)
            psIN = [ctx.ps(f"psIN{i}", [128, 512], F32, q4) for i in range(2)]
            psCR = [ctx.ps(f"psCR{i}", [128, 512], F32, q4) for i in range(2)]
            psDN = ctx.ps("psDN", [128, NH1], F32, q4)

            xt3 = ctx.sem("xt3")
            bt3 = ctx.sem("bt3")
            stg = ctx.sem("stg")
            s_wdt = ctx.sem("s_wdt")
            t_cb = ctx.sem("t_cb")
            v_cb = ctx.sem("v_cb")
            t_trep = ctx.sem("t_trep")
            s_exp = ctx.sem("s_exp")
            v_sc = ctx.sem("v_sc")
            t_ic = ctx.sem("t_ic")
            s_ic = ctx.sem("s_ic")
            v_cr = ctx.sem("v_cr")
            t_sl = ctx.sem("t_sl")
            t_dnt = ctx.sem("t_dnt")
            v_dn = ctx.sem("v_dn")
            t_dnb = ctx.sem("t_dnb")
            v_rec = ctx.sem("v_rec")
            s_sl = ctx.sem("s_sl")
            yst = [ctx.sem("yst0"), ctx.sem("yst1")]
            fin = ctx.sem("fin")
            t_in = ctx.sem("t_in")
            v_init = ctx.sem("v_init")

            def q4_sync(sy):
                sy.wait_ge(st1[0], 16 * 9)
                sy.wait_ge(st1[1], 16 * 9)
                for i in range(NCH1):
                    sy.dma_start_transpose(X_tok[i][:, :], xoh[:, i * 128:(i + 1) * 128]
                                           ).then_inc(xt3, 16)
                for i in range(NCH1):
                    sy.dma_start_transpose(B_tok[i][:, :], bhb[:, i * 128:(i + 1) * 128]
                                           ).then_inc(bt3, 16)
            blk.sync(q4_sync)

            def scw(h):
                return v_sc

            def scv(ch, h):
                return ch * 32 + h + 1

            def q4_gpsimd(g):
                g.wait_ge(xt3, 16 * NCH1)
                for ch in range(NCH1):
                    if ch >= 1:
                        g.wait_ge(t_trep, 8 * ch)
                    g.dma_start(out=stage[:, :], in_=cumhb[:, ch * 128:(ch + 1) * 128]
                                ).then_inc(stg, 16)
                    g.wait_ge(s_ic, (ch + 1) * 4)
                    g.wait_ge(v_cr, (ch + 1) * 32)
                    g.dma_start(out=ylo[ch * 128:(ch + 1) * 128, :], in_=ylp[ch % 2][:, :]
                                ).then_inc(yst[ch % 2], 16)
                g.wait_ge(v_rec, 16 * NCH1)
                g.dma_start(out=sto[:, :], in_=Sst[NCH1 % 2][:, :]).then_inc(fin, 16)
                g.wait_ge(fin, 16)
                g.wait_ge(yst[0], 16 * (NCH1 // 2))
                g.wait_ge(yst[1], 16 * (NCH1 // 2))
            blk.gpsimd(q4_gpsimd)

            def q4_vector0(v):
                v.memset(Sst[0][:, :], 0.0)
                v.drain()
                v.engine_nop().then_inc(v_init, 1)
            blk.vector(q4_vector0)

            def q4_tensor(t):
                t.wait_ge(bt3, 16 * NCH1)
                t.wait_ge(v_init, 1)
                for ch in range(NCH1):
                    sl = slice(ch * 128, (ch + 1) * 128)
                    cur = ch % 2
                    t.wait_ge(stg, 16 * (ch + 1))
                    if ch >= 1:
                        t.wait_ge(v_cb, ch)
                    t.matmul(psCB[:, :], xbcs_bc[0][:, sl], xbcs_bc[1][:, sl],
                             start=True, stop=True).then_inc(t_cb, 1)
                    for gg in range(8):
                        G = ch * 8 + gg
                        if G >= 2:
                            t.wait_ge(s_exp, 4 * (G - 1))
                        t.matmul(psTw[G % 2][:, :], ones1_f[:, :],
                                 stage[0:1, gg * 512:(gg + 1) * 512],
                                 start=True, stop=False)
                        t.matmul(psTw[G % 2][:, :], ident_f[:, :], bigmask4[:, :],
                                 start=False, stop=True).then_inc(t_trep, 1)
                    for oct in range(4):
                        G = ch * 4 + oct
                        if G >= 2:
                            t.wait_ge(s_ic, G - 1)
                            t.wait_ge(v_cr, 8 * (G - 1))
                        if ch >= 1 and oct == 0:
                            t.wait_ge(v_rec, 16 * ch)
                        for h8 in range(8):
                            h = oct * 8 + h8
                            t.wait_ge(scw(h), scv(ch, h))
                            t.matmul(psIN[oct % 2][:, h8 * 64:(h8 + 1) * 64],
                                     scT[h][:, :],
                                     X_tok[ch][:, h * P1:(h + 1) * P1],
                                     start=True, stop=True).then_inc(t_in, 1)
                        t.matmul(psCR[oct % 2][:, :], xbcs_bc[1][:, sl],
                                 Sst[cur][:, oct * 512:(oct + 1) * 512],
                                 start=True, stop=True).then_inc(t_ic, 1)
                    t.wait_ge(s_wdt, ch + 1)
                    t.wait_ge(s_ic, (ch + 1) * 4)
                    t.wait_ge(v_cr, (ch + 1) * 32)
                    slb = [psIN[0], psIN[1], psCR[0], psCR[1]]
                    ins = None
                    for oct in range(4):
                        ins = t.matmul(slb[oct][:, :], B_tok[ch][:, :],
                                       wdtX[:, oct * 512:(oct + 1) * 512],
                                       start=True, stop=True)
                    ins.then_inc(t_sl, 1)
                    if ch >= 1:
                        t.wait_ge(v_dn, 2 * ch)
                    t.transpose(psDN[0:1, :], dect[:, ch:ch + 1],
                                ident_f[0:NH1, 0:NH1]).then_inc(t_dnt, 1)
                    t.wait_ge(v_dn, 2 * ch + 1)
                    t.matmul(psDN[:, :], ones1_f[:, :], drow[:, :],
                             start=True, stop=True).then_inc(t_dnb, 1)
            blk.tensor(q4_tensor)

            def q4_scalar(s):
                for ch in range(NCH1):
                    for h in range(NH1):
                        u = ch * 32 + h
                        t_ = u - 2
                        if t_ >= 0:
                            s.wait_ge(scw(t_ % 32), scv(t_ // 32, t_ % 32))
                        G = ch * 8 + h // 4
                        s.wait_ge(t_trep, G + 1)
                        s.activation(out=expdt[h % 2][:, :],
                                     in_=psTw[G % 2][:, (h % 4) * 128:(h % 4 + 1) * 128],
                                     func=AF.Exp,
                                     bias=bias_tok[ch][:, h:h + 1]).then_inc(s_exp, 1)
                    if ch >= 1:
                        s.wait_ge(t_sl, ch)
                    ins = None
                    for h in range(NH1):
                        ins = s.activation(out=wdtX[:, h * P1:(h + 1) * P1],
                                           in_=X_tok[ch][:, h * P1:(h + 1) * P1],
                                           func=AF.Copy,
                                           scale=wdt_tok[ch][:, h:h + 1])
                    ins.then_inc(s_wdt, 1)
                    for oct in range(4):
                        G = ch * 4 + oct
                        s.wait_ge(t_in, ch * 32 + oct * 8 + 8)
                        if oct == 0:
                            s.wait_ge(yst[ch % 2], 16 * (ch // 2))
                        s.activation(out=ylp[ch % 2][:, oct * 512:(oct + 1) * 512],
                                     in_=psIN[oct % 2][:, :],
                                     func=AF.Copy).then_inc(s_ic, 1)
                    s.wait_ge(t_sl, ch + 1)
                    ins = None
                    for oct in range(4):
                        slb = [psIN[0], psIN[1], psCR[0], psCR[1]]
                        ins = s.activation(out=slsb[:, oct * 512:(oct + 1) * 512],
                                           in_=slb[oct][:, :], func=AF.Copy)
                    ins.then_inc(s_sl, 1)
            blk.scalar(q4_scalar)

            def q4_vector(v):
                for ch in range(NCH1):
                    cur, nxt = ch % 2, (ch + 1) % 2
                    v.wait_ge(t_cb, ch + 1)
                    v.tensor_copy(CBm[:, :], psCB[:, :]).then_inc(v_cb, 1)
                    v.drain()
                    for h in range(NH1):
                        u = ch * 32 + h
                        v.wait_ge(s_exp, u + 1)
                        if ch >= 1:
                            v.wait_ge(t_in, (ch - 1) * 32 + h + 1)
                        v.tensor_mul(scT[h][:, :], expdt[h % 2][:, :], CBm[:, :]
                                     ).then_inc(v_sc, 1)
                    for oct in range(4):
                        G = ch * 4 + oct
                        v.wait_ge(t_ic, G + 1)
                        v.wait_ge(s_ic, G + 1)
                        for h8 in range(8):
                            h = oct * 8 + h8
                            v.scalar_tensor_tensor(
                                ylp[ch % 2][:, h * P1:(h + 1) * P1],
                                psCR[oct % 2][:, h8 * 64:(h8 + 1) * 64],
                                expc_tok[ch][:, h:h + 1],
                                ylp[ch % 2][:, h * P1:(h + 1) * P1],
                                OP.mult, OP.add).then_inc(v_cr, 1)
                        v.drain()
                    v.wait_ge(t_dnt, ch + 1)
                    v.tensor_copy(drow[:, :], psDN[0:1, :]).then_inc(v_dn, 1)
                    v.drain()
                    v.wait_ge(t_dnb, ch + 1)
                    v.tensor_copy(decn[:, :], psDN[:, :]).then_inc(v_dn, 1)
                    v.drain()
                    v.wait_ge(s_sl, ch + 1)
                    for h in range(NH1):
                        v.scalar_tensor_tensor(Sst[nxt][:, h * P1:(h + 1) * P1],
                                               Sst[cur][:, h * P1:(h + 1) * P1],
                                               decn[:, h:h + 1],
                                               slsb[:, h * P1:(h + 1) * P1],
                                               OP.mult, OP.add)
                    v.drain()
                    v.engine_nop().then_inc(v_rec, 16)
            blk.vector(q4_vector)

            ctx.barrier()

        if upto == 'C':
            return nc, dbg_tensors

        ao = ctx.sb("ao", [128, TPC], F32)
        lr = ctx.sb("lr", [1, TPC], F32)
        with ExitStack() as q5:
            qTt = ctx.sb("qTt", [128, TPC], BF16, q5)
            kdgT = ctx.sb("kdgT", [128, TPC], BF16, q5)
            ktt = [ctx.sb(f"ktt{i}", [128, TPC], BF16, q5) for i in range(10)]
            vtt = [ctx.sb(f"vtt{i}", [128, DV1], BF16, q5) for i in range(160)]
            vdgt = [ctx.sb(f"vdgt{i}", [128, DV1], BF16, q5) for i in range(NTT)]
            dmaskq = ctx.sb("dmaskq", [128, 128], F32, q5)
            mxq = ctx.sb("mxq", [128, NTT], F32, q5)
            m512 = ctx.sb("m512", [128, 1], F32, q5)
            mrow = ctx.sb("mrow", [1, TPC], F32, q5)
            mstg = [ctx.sb(f"mstg{i}", [1, 512], F32, q5) for i in range(2)]
            mrow2 = [ctx.sb(f"mrow2_{i}", [1, 512], F32, q5) for i in range(2)]
            mrepb = [ctx.sb(f"mrepb{i}", [128, 512], F32, q5) for i in range(2)]
            tdiff = [ctx.sb(f"tdiff{i}", [128, 512], F32, q5) for i in range(2)]
            PT = [ctx.sb(f"PT{i}", [128, 512], BF16, q5) for i in range(2)]

            psS = [ctx.ps(f"psS{i}", [128, 512], F32, q5) for i in range(2)]
            psM = [ctx.ps(f"psM{i}", [128, 512], F32, q5) for i in range(2)]
            psST = [ctx.ps(f"psST{i}", [128, 512], F32, q5) for i in range(2)]
            psAV = ctx.ps("psAV", [128, 512], F32, q5)
            psL = ctx.ps("psL", [1, 512], F32, q5)

            a_ld = ctx.sem("a_ld")
            q_ld = ctx.sem("q_ld")
            av_ld = ctx.sem("av_ld")
            v_ini = ctx.sem("v_ini")
            p1mm = ctx.sem("p1mm")
            p1ev = ctx.sem("p1ev")
            trx = ctx.sem("trx")
            trev = ctx.sem("trev")
            mgl = [ctx.sem("mgl0"), ctx.sem("mgl1")]
            mrw2v = ctx.sem("mrw2v")
            mbc = ctx.sem("mbc")
            mre = ctx.sem("mre")
            stmm = ctx.sem("stmm")
            sdv = ctx.sem("sdv")
            pte = ctx.sem("pte")
            avmm = ctx.sem("avmm")
            aoev = ctx.sem("aoev")

            KBC = [16 * (g + 1) for g in range(4)]
            KB0 = [sum(KBC[:g]) for g in range(4)]
            P1U = []
            for g in range(4):
                for r in range(4):
                    j = g * 4 + r
                    for kc in range(KBC[g] // 4):
                        P1U.append((j, "k", KB0[g] + kc * 4))
                    P1U.append((j, "d", j))
            NP1 = len(P1U)
            ULAST = {}
            for i, (j, kind, idx) in enumerate(P1U):
                ULAST[j] = i + 1
            P2U = []
            for g in range(4):
                for kb in range(KBC[g]):
                    P2U.append((g, "k", kb))
                for r in range(4):
                    P2U.append((g, "d", r))
            NP2 = len(P2U)
            GFIRST = {}
            n = 0
            for g in range(4):
                GFIRST[g] = n
                n += KBC[g] + 4
            GLAST = {g: GFIRST[g] + KBC[g] + 4 for g in range(4)}

            def kslice(gkb):
                return ktt[gkb // NTT], (gkb % NTT) * 128

            def q5_sync(sy):
                sy.dma_start_transpose(qTt[:, :], qtl[:, :]).then_inc(q_ld, 16)
                sy.dma_start_transpose(kdgT[:, :], kdg[:, :]).then_inc(a_ld, 16)
                for i in range(10):
                    sy.dma_start_transpose(ktt[i][:, :],
                                           kcat[i * TPC:(i + 1) * TPC, :]).then_inc(a_ld, 16)
                for i in range(160):
                    sy.dma_start(out=vtt[i][:, :], in_=vcat[i * 128:(i + 1) * 128, :]
                                 ).then_inc(av_ld, 16)
                for i in range(NTT):
                    sy.dma_start(out=vdgt[i][:, :], in_=vdgd[i * 128:(i + 1) * 128, :]
                                 ).then_inc(av_ld, 16)
            blk.sync(q5_sync)

            def q5_gpsimd(g_):
                for n2, (g, kind, idx) in enumerate(P2U):
                    if kind == "k":
                        if n2 >= 2:
                            g_.wait_ge(mrw2v, n2 - 1)
                        g_.dma_start(out=mstg[n2 % 2][:, :],
                                     in_=mvals[KB0[g] + idx:KB0[g] + idx + 1, :]
                                     ).then_inc(mgl[n2 % 2], 16)
            blk.gpsimd(q5_gpsimd)

            def q5_vector0(v):
                v.tensor_single_scalar(dmaskq[:, :], iv[:, :], 0.0, OP.is_gt)
                v.drain()
                v.tensor_scalar_mul(dmaskq[:, :], dmaskq[:, :], -1e9)
                v.drain()
                v.memset(mxq[:, :], -1e30)
                v.wait_ge(q_ld, 16)
                v.drain()
                v.tensor_scalar_mul(qTt[:, :], qTt[:, :], 0.08838834764831845)
                v.drain()
                v.engine_nop().then_inc(v_ini, 1)
            blk.vector(q5_vector0)

            def q5_tensor(t):
                t.wait_ge(a_ld, 16 * 11)
                t.wait_ge(av_ld, 16 * (160 + NTT))
                t.wait_ge(v_ini, 1)
                for n1, (j, kind, idx) in enumerate(P1U):
                    qs = slice(j * 128, (j + 1) * 128)
                    if n1 >= 2:
                        t.wait_ge(p1ev, n1 - 1)
                    if kind == "k":
                        kt, k0 = kslice(idx)
                        t.matmul(psS[n1 % 2][:, :], qTt[:, qs], kt[:, k0:k0 + 512],
                                 start=True, stop=True).then_inc(p1mm, 1)
                    else:
                        t.matmul(psS[n1 % 2][:, 0:128], qTt[:, qs], kdgT[:, qs],
                                 start=True, stop=True).then_inc(p1mm, 1)
                t.wait_ge(p1ev, NP1)
                for j in range(NTT):
                    if j >= 1:
                        t.wait_ge(trev, j)
                    t.transpose(psS[0][0:1, 0:128], mxq[:, j:j + 1], ident_f[:, :]
                                ).then_inc(trx, 1)
                for n2, (g, kind, idx) in enumerate(P2U):
                    qg = slice(g * 512, (g + 1) * 512)
                    t.wait_ge(mrw2v, n2 + 1)
                    if n2 >= 2:
                        t.wait_ge(mre, n2 - 1)
                    w = 512 if kind == "k" else 128
                    t.matmul(psM[n2 % 2][:, 0:w], ones1_f[:, :], mrow2[n2 % 2][0:1, 0:w],
                             start=True, stop=True).then_inc(mbc, 1)
                    if n2 >= 2:
                        t.wait_ge(sdv, n2 - 1)
                    if kind == "k":
                        kt, k0 = kslice(KB0[g] + idx)
                        t.matmul(psST[n2 % 2][:, :], kt[:, k0:k0 + 128], qTt[:, qg],
                                 start=True, stop=True).then_inc(stmm, 1)
                    else:
                        j = g * 4 + idx
                        qs = slice(j * 128, (j + 1) * 128)
                        t.matmul(psST[n2 % 2][:, 0:128], kdgT[:, qs], qTt[:, qs],
                                 start=True, stop=True).then_inc(stmm, 1)
                    if n2 >= 2:
                        emit_avl(t, n2 - 2)
                for nn in (NP2 - 2, NP2 - 1):
                    emit_avl(t, nn)
            def emit_avl(t, nn):
                g, kind, idx = P2U[nn]
                first = nn == GFIRST[g]
                last = nn == GLAST[g] - 1
                t.wait_ge(pte, nn + 1)
                if first:
                    t.wait_ge(aoev, 2 * g)
                if kind == "k":
                    t.matmul(psAV[:, :], vtt[KB0[g] + idx][:, :], PT[nn % 2][:, :],
                             start=first, stop=last)
                    t.matmul(psL[:, :], ones_bf[:, :], PT[nn % 2][:, :],
                             start=first, stop=last).then_inc(avmm, 1)
                else:
                    j = g * 4 + idx
                    rs = slice(idx * 128, (idx + 1) * 128)
                    t.matmul(psAV[:, rs], vdgt[j][:, :], PT[nn % 2][:, 0:128],
                             start=False, stop=last)
                    t.matmul(psL[0:1, rs], ones_bf[:, :], PT[nn % 2][:, 0:128],
                             start=False, stop=last).then_inc(avmm, 1)
            blk.tensor(q5_tensor)

            def q5_vector(v):
                for n1, (j, kind, idx) in enumerate(P1U):
                    v.wait_ge(p1mm, n1 + 1)
                    if kind == "k":
                        v.reduce_max(m512[:, :], psS[n1 % 2][:, :], axis=AX.X)
                        v.drain()
                    else:
                        v.tensor_add(tdiff[0][:, 0:128], psS[n1 % 2][:, 0:128],
                                     dmaskq[:, :])
                        v.drain()
                        v.reduce_max(m512[:, :], tdiff[0][:, 0:128], axis=AX.X)
                        v.drain()
                    v.tensor_max(mxq[:, j:j + 1], mxq[:, j:j + 1], m512[:, :]
                                 ).then_inc(p1ev, 1)
                    v.drain()
                for j in range(NTT):
                    v.wait_ge(trx, j + 1)
                    v.tensor_copy(mrow[0:1, j * 128:(j + 1) * 128], psS[0][0:1, 0:128]
                                  ).then_inc(trev, 1)
                    v.drain()
                for n2, (g, kind, idx) in enumerate(P2U):
                    qg = slice(g * 512, (g + 1) * 512)
                    if kind == "k":
                        v.wait_ge(mgl[n2 % 2], 16 * ((n2 - GFIRST[g]) // 2 + 1)
                                  if False else 16 * (len([x for x in range(n2 + 1)
                                       if P2U[x][1] == "k" and x % 2 == n2 % 2])))
                        v.tensor_add(mrow2[n2 % 2][:, :], mrow[0:1, qg], mstg[n2 % 2][:, :]
                                     ).then_inc(mrw2v, 1)
                    else:
                        j = g * 4 + idx
                        v.tensor_copy(mrow2[n2 % 2][0:1, 0:128],
                                      mrow[0:1, j * 128:(j + 1) * 128]).then_inc(mrw2v, 1)
                    v.drain()
                    v.wait_ge(stmm, n2 + 1)
                    v.wait_ge(mre, n2 + 1)
                    w = 512 if kind == "k" else 128
                    v.tensor_sub(tdiff[n2 % 2][:, 0:w], psST[n2 % 2][:, 0:w],
                                 mrepb[n2 % 2][:, 0:w])
                    v.drain()
                    if kind == "d":
                        v.tensor_add(tdiff[n2 % 2][:, 0:128], tdiff[n2 % 2][:, 0:128],
                                     dmaskadd[:, :])
                        v.drain()
                    v.engine_nop().then_inc(sdv, 1)
                    for gg in range(4):
                        if n2 == GLAST[gg] + 1:
                            v.wait_ge(avmm, GLAST[gg])
                            v.tensor_copy(lr[0:1, gg * 512:(gg + 1) * 512], psL[0:1, :]
                                          ).then_inc(aoev, 1)
                            v.drain()
                v.wait_ge(avmm, NP2)
                v.tensor_copy(lr[0:1, 3 * 512:4 * 512], psL[0:1, :]).then_inc(aoev, 1)
                v.drain()
            blk.vector(q5_vector)

            def q5_scalar(s):
                for n2, (g, kind, idx) in enumerate(P2U):
                    qg = slice(g * 512, (g + 1) * 512)
                    w = 512 if kind == "k" else 128
                    s.wait_ge(mbc, n2 + 1)
                    s.activation(out=mrepb[n2 % 2][:, 0:w], in_=psM[n2 % 2][:, 0:w],
                                 func=AF.Copy).then_inc(mre, 1)
                    s.wait_ge(sdv, n2 + 1)
                    if n2 >= 2:
                        s.wait_ge(avmm, n2 - 1)
                    s.activation(out=PT[n2 % 2][:, 0:w], in_=tdiff[n2 % 2][:, 0:w],
                                 func=AF.Exp).then_inc(pte, 1)
                    for gg in range(4):
                        if n2 == GLAST[gg] + 1:
                            s.wait_ge(avmm, GLAST[gg])
                            s.activation(out=ao[:, gg * 512:(gg + 1) * 512], in_=psAV[:, :],
                                         func=AF.Copy).then_inc(aoev, 1)
                s.wait_ge(avmm, NP2)
                s.activation(out=ao[:, 3 * 512:4 * 512], in_=psAV[:, :],
                             func=AF.Copy).then_inc(aoev, 1)
            blk.scalar(q5_scalar)

            ctx.barrier()

        with ExitStack() as q6:
            opw = [ctx.sb(f"opw{e}", [128, D], BF16, q6) for e in range(E1)]
            mrwt = ctx.sb("mrwt", [E1, TPC], F32, q6)
            lrec = ctx.sb("lrec", [1, TPC], F32, q6)
            wstg = [ctx.sb(f"wstg{i}", [1, 512], F32, q6) for i in range(2)]
            rhse = [ctx.sb(f"rhse{i}", [128, 512], BF16, q6) for i in range(E1)]
            aos = ctx.sb("aos", [128, TPC], F32, q6)
            g1t = [ctx.sb(f"g1t{i}", [128, 512], BF16, q6) for i in range(2)]

            psW = [ctx.ps(f"psW{i}", [128, 512], F32, q6) for i in range(2)]
            psG = [ctx.ps(f"psG{i}", [128, 512], F32, q6) for i in range(2)]

            o_ld = ctx.sem("o_ld")
            o_pre = ctx.sem("o_pre")
            o_lbc = ctx.sem("o_lbc")
            o_aos = ctx.sem("o_aos")
            o_row = [ctx.sem("o_row0"), ctx.sem("o_row1")]
            o_wbc = ctx.sem("o_wbc")
            o_rhs = ctx.sem("o_rhs")
            o_mm = ctx.sem("o_mm")
            o_sil = ctx.sem("o_sil")
            o_st = [ctx.sem("o_st0"), ctx.sem("o_st1")]

            def q6_sync(sy):
                sy.wait_ge(oc_sem, 16)
                for e in range(E1):
                    sy.dma_start(out=opw[e][:, :], in_=opbf[e, :, :]).then_inc(o_ld, 16)
                sy.dma_start(out=mrwt[:, :], in_=mrwg[:, :]).then_inc(o_ld, 16)
            blk.sync(q6_sync)

            def q6_gpsimd(g_):
                for qc in range(NTC):
                    if qc >= 1:
                        for oc in range(ND):
                            go = (qc - 1) * ND + oc
                            g_.wait_ge(o_sil, go + 1)
                            g_.dma_start(out=g1o[oc * 128:(oc + 1) * 128,
                                                 (qc - 1) * 512:qc * 512],
                                         in_=g1t[go % 2][:, :]).then_inc(o_st[go % 2], 16)
                    for e in range(E1):
                        u = qc * E1 + e
                        if u >= 2:
                            g_.wait_ge(o_wbc, u - 1)
                        g_.dma_start(out=wstg[u % 2][:, :],
                                     in_=mrwg[e:e + 1, qc * 512:(qc + 1) * 512]
                                     ).then_inc(o_row[u % 2], 16)
                for oc in range(ND):
                    go = 3 * ND + oc
                    g_.wait_ge(o_sil, go + 1)
                    g_.dma_start(out=g1o[oc * 128:(oc + 1) * 128, 3 * 512:4 * 512],
                                 in_=g1t[go % 2][:, :]).then_inc(o_st[go % 2], 16)
                g_.wait_ge(o_st[0], 16 * 16)
                g_.wait_ge(o_st[1], 16 * 16)
            blk.gpsimd(q6_gpsimd)

            def q6_tensor(t):
                t.wait_ge(o_ld, 16 * (E1 + 1))
                t.wait_ge(o_pre, 1)
                for qc in range(NTC):
                    if qc >= 2:
                        t.wait_ge(o_aos, qc - 1)
                    t.matmul(psW[qc % 2][:, :], ones1_f[:, :],
                             lrec[0:1, qc * 512:(qc + 1) * 512],
                             start=True, stop=True).then_inc(o_lbc, 1)
                for qc in range(NTC):
                    for e in range(E1):
                        u = qc * E1 + e
                        t.wait_ge(o_row[u % 2], 16 * (u // 2 + 1))
                        if u == 0:
                            t.wait_ge(o_aos, NTC)
                        if u >= 2:
                            t.wait_ge(o_rhs, u - 1)
                        t.matmul(psW[u % 2][:, :], ones1_f[:, :], wstg[u % 2][:, :],
                                 start=True, stop=True).then_inc(o_wbc, 1)
                    for oc in range(ND):
                        go = qc * ND + oc
                        if go >= 2:
                            t.wait_ge(o_sil, go - 1)
                        for e in range(E1):
                            t.wait_ge(o_rhs, qc * E1 + e + 1)
                            ins = t.matmul(psG[go % 2][:, :],
                                           opw[e][:, oc * 128:(oc + 1) * 128],
                                           rhse[e][:, :],
                                           start=(e == 0), stop=(e == E1 - 1))
                        ins.then_inc(o_mm, 1)
            blk.tensor(q6_tensor)

            def q6_vector(v):
                v.wait_ge(aoev, 8)
                v.reciprocal(lrec[:, :], lr[:, :]).then_inc(o_pre, 1)
                v.drain()
                for qc in range(NTC):
                    v.wait_ge(o_lbc, qc + 1)
                    v.tensor_mul(aos[:, qc * 512:(qc + 1) * 512],
                                 ao[:, qc * 512:(qc + 1) * 512],
                                 psW[qc % 2][:, :]).then_inc(o_aos, 1)
                    v.drain()
                for qc in range(NTC):
                    for e in range(E1):
                        u = qc * E1 + e
                        v.wait_ge(o_wbc, u + 1)
                        if u >= E1:
                            v.wait_ge(o_mm, (qc - 1) * ND + ND)
                        v.tensor_mul(rhse[e][:, :], aos[:, qc * 512:(qc + 1) * 512],
                                     psW[u % 2][:, :]).then_inc(o_rhs, 1)
                        v.drain()
            blk.vector(q6_vector)

            def q6_scalar(s):
                for qc in range(NTC):
                    for oc in range(ND):
                        go = qc * ND + oc
                        s.wait_ge(o_mm, go + 1)
                        if go >= 2:
                            s.wait_ge(o_st[go % 2], 16 * ((go - 2) // 2 + 1))
                        s.activation(out=g1t[go % 2][:, :], in_=psG[go % 2][:, :],
                                     func=AF.Silu).then_inc(o_sil, 1)
            blk.scalar(q6_scalar)


            ctx.barrier()

    return nc, dbg_tensors



def build_launch4(debug=()):
    nc = bass.Bass("TRN2", target_bir_lowering=False)
    dbg = set(debug)
    dbg_tensors = {}

    ylt = nc.dram_tensor("ylt", [TPC, D_INNER], BF16, kind="ExternalInput").ap()
    xfm = nc.dram_tensor("xfm", [D_INNER, TPC], BF16, kind="ExternalInput").ap()
    szf = nc.dram_tensor("szf", [D_INNER, TPC], BF16, kind="ExternalInput").ap()
    g1f = nc.dram_tensor("g1f", [D, TPC], BF16, kind="ExternalInput").ap()
    x1f = nc.dram_tensor("x1f", [D, TPC], F32, kind="ExternalInput").ap()
    cto4 = nc.dram_tensor("cto", [N1, TPC], BF16, kind="ExternalInput").ap()
    exct = nc.dram_tensor("exct", [NH1, TPC], F32, kind="ExternalInput").ap()
    st_in = nc.dram_tensor("st_in", [3, N1, D_INNER], BF16, kind="ExternalInput").ap()
    dfr = nc.dram_tensor("dfr", [3, D_INNER], F32, kind="ExternalInput").ap()
    nwd = nc.dram_tensor("nwd", [128, D_INNER // 128], F32, kind="ExternalInput").ap()
    dpd = nc.dram_tensor("dpd", [128, D_INNER // 128], F32, kind="ExternalInput").ap()
    wout = nc.dram_tensor("wout", [D_INNER, D], F32, kind="ExternalInput").ap()

    x2o = nc.dram_tensor("x2_out", [D, TPC], F32, kind="ExternalOutput").ap()
    woutbf = nc.dram_tensor("woutbf", [D_INNER, D], BF16).ap()

    def dbg_out(name, shape, dtype):
        if name in dbg:
            dbg_tensors[name] = nc.dram_tensor("dbg_" + name, shape, dtype,
                                               kind="ExternalOutput").ap()
            return dbg_tensors[name]
        return None

    NFC = D_INNER // 128
    with ExitStack() as st:
        ctx = Ctx(nc, st, const_vals=(1e-5,))
        blk = ctx.blk

        ones1_f = ctx.sb("ones1_f", [1, 128], F32)
        ones_bf = ctx.sb("ones_bf", [128, 1], BF16)
        ygb = [ctx.sb(f"ygb{i}", [128, TPC], BF16) for i in range(NFC)]
        cto_s = ctx.sb("cto_s", [N1, TPC], BF16)
        Sbf = ctx.sb("Sbf", [N1, D_INNER], BF16)
        nwst = ctx.sb("nwst", [128, NFC], F32)
        dpst = ctx.sb("dpst", [128, NFC], F32)
        rstd_sb = ctx.sb("rstd_sb", [1, TPC], F32)
        rstdrep = ctx.sb("rstdrep", [128, TPC], F32)

        cast_sem = ctx.sem("cast_sem")

        def p0_gpsimd(g):
            g.dma_start(out=woutbf[:, :], in_=wout[:, :]).then_inc(cast_sem, 16)

        def p0_vector(v):
            ctx.emit_const_memsets(v)
            v.memset(ones1_f[:, :], 1.0)
            v.memset(ones_bf[:, :], 1.0)

        blk.gpsimd(p0_gpsimd)
        blk.vector(p0_vector)
        ctx.barrier()

        with ExitStack() as p1:
            acc = ctx.sb("acc", [N1, D_INNER], F32, p1)
            sts = [ctx.sb(f"sts{j}", [N1, D_INNER], BF16, p1) for j in range(3)]
            dfs = [ctx.sb(f"dfs{j}", [1, D_INNER], F32, p1) for j in range(3)]
            psD = [ctx.ps(f"psD{i}", [128, 512], F32, p1) for i in range(2)]

            steps = []

            def s_load(sy, dma):
                for j in range(3):
                    dma(out=sts[j][:, :], in_=st_in[j, :, :])
                    dma(out=dfs[j][:, :], in_=dfr[j:j + 1, :])
                dma(out=cto_s[:, :], in_=cto4[:, :])
                dma(out=nwst[:, :], in_=nwd[:, :])
                dma(out=dpst[:, :], in_=dpd[:, :])
            steps.append(("sync", s_load))

            def s_init(v, dma):
                v.tensor_copy(acc[:, :], sts[0][:, :])
            steps.append(("vector", s_init))

            for j in (1, 2):
                for half in range(2):
                    def s_psd(t, dma, j=j, half=half):
                        for i in range(2):
                            tc = half * 2 + i
                            t.matmul(psD[i][:, :], ones1_f[0:1, :],
                                     dfs[j][0:1, tc * 512:(tc + 1) * 512],
                                     start=True, stop=True)
                    steps.append(("tensor", s_psd))

                    def s_mul(v, dma, half=half):
                        for i in range(2):
                            tc = half * 2 + i
                            sl = slice(tc * 512, (tc + 1) * 512)
                            v.tensor_mul(acc[:, sl], acc[:, sl], psD[i][:, :])
                            v.drain()
                    steps.append(("vector", s_mul))

                def s_add(v, dma, j=j):
                    v.tensor_tensor(acc[:, :], acc[:, :], sts[j][:, :], OP.add)
                steps.append(("vector", s_add))

            def s_cast(v, dma):
                v.tensor_copy(Sbf[:, :], acc[:, :])
            steps.append(("vector", s_cast))

            ctx.seq("p1s", steps)
            ctx.barrier()

        with ExitStack() as pA:
            psS = [ctx.ps(f"psS{i}", [1, 512], F32, pA) for i in range(NTC)]
            with ExitStack() as p2:
                ylTt = [ctx.sb(f"ylTt{i}", [128, TPC], BF16, p2) for i in range(2)]
                Xt2 = [ctx.sb(f"Xt{i}", [128, TPC], BF16, p2) for i in range(2)]
                szt2 = [ctx.sb(f"szt{i}", [128, TPC], BF16, p2) for i in range(2)]
                exc2 = [[ctx.sb(f"exc2_{i}_{hh}", [1, TPC], F32, p2)
                         for hh in range(2)] for i in range(2)]
                yt2 = [ctx.sb(f"yt{i}", [128, TPC], F32, p2) for i in range(2)]
                sqb2 = [ctx.sb(f"sqb{i}", [128, TPC], BF16, p2) for i in range(2)]
                exrep2 = [ctx.sb(f"exrep{i}", [128, TPC], F32, p2) for i in range(2)]
                psE = [ctx.ps(f"psE{i}", [128, 512], F32, p2) for i in range(2)]
                psC = [ctx.ps(f"psC{i}", [128, 512], F32, p2) for i in range(2)]
                dyg = dbg_out("yg", [D_INNER, TPC], BF16)

                l4l = [ctx.sem("l4l0"), ctx.sem("l4l1")]
                t_pe = ctx.sem("t_pe")
                t_pc = ctx.sem("t_pc")
                t_sq = ctx.sem("t_sq")
                s_ex = ctx.sem("s_ex")
                a_cast = ctx.sem("a_cast")
                v_cr = ctx.sem("v_cr")
                v_gate = ctx.sem("v_gate")
                v_done = ctx.sem("v_done")

                def p2_sync(sy):
                    for fc in range(NFC):
                        p = fc % 2
                        if fc >= 2:
                            sy.wait_ge(v_done, fc - 1)
                        sy.dma_start_transpose(ylTt[p][:, :],
                                               ylt[:, fc * 128:(fc + 1) * 128]
                                               ).then_inc(l4l[p], 16)
                        sy.dma_start(out=Xt2[p][:, :],
                                     in_=xfm[fc * 128:(fc + 1) * 128, :]
                                     ).then_inc(l4l[p], 16)
                        sy.dma_start(out=szt2[p][:, :],
                                     in_=szf[fc * 128:(fc + 1) * 128, :]
                                     ).then_inc(l4l[p], 16)
                        for hh in range(2):
                            sy.dma_start(out=exc2[p][hh][:, :],
                                         in_=exct[2 * fc + hh:2 * fc + hh + 1, :]
                                         ).then_inc(l4l[p], 16)
                blk.sync(p2_sync)

                def p2_tensor(t):
                    def emit_pss(fc):
                        t.wait_ge(v_done, fc + 1)
                        for tc in range(NTC):
                            t.matmul(psS[tc][:, :], ones_bf[:, :],
                                     sqb2[fc % 2][:, tc * 512:(tc + 1) * 512],
                                     start=(fc == 0), stop=(fc == NFC - 1)
                                     ).then_inc(t_sq, 1)
                    for fc in range(NFC):
                        p = fc % 2
                        t.wait_ge(l4l[p], 80 * (fc // 2 + 1))
                        for tc in range(NTC):
                            u = fc * 4 + tc
                            sl = slice(tc * 512, (tc + 1) * 512)
                            if u >= 2:
                                t.wait_ge(s_ex, u - 1)
                            for hh in range(2):
                                ins = t.matmul(psE[u % 2][hh * 64:(hh + 1) * 64, :],
                                               ones1_f[0:1, 0:64],
                                               exc2[p][hh][0:1, sl],
                                               start=True, stop=True)
                            ins.then_inc(t_pe, 1)
                            if u >= 2:
                                t.wait_ge(v_cr, u - 1)
                            t.matmul(psC[u % 2][:, :],
                                     Sbf[:, fc * 128:(fc + 1) * 128],
                                     cto_s[:, sl], start=True, stop=True
                                     ).then_inc(t_pc, 1)
                        if fc >= 1:
                            emit_pss(fc - 1)
                    emit_pss(NFC - 1)
                blk.tensor(p2_tensor)

                def p2_scalar(s):
                    for fc in range(NFC):
                        p = fc % 2
                        for tc in range(NTC):
                            u = fc * 4 + tc
                            if tc == 0 and fc >= 2:
                                s.wait_ge(v_cr, 4 * (fc - 1))
                            s.wait_ge(t_pe, u + 1)
                            s.activation(out=exrep2[p][:, tc * 512:(tc + 1) * 512],
                                         in_=psE[u % 2][:, :],
                                         func=AF.Copy).then_inc(s_ex, 1)
                        s.wait_ge(v_done, fc + 1)
                        s.activation(out=ygb[fc][:, :], in_=yt2[p][:, :],
                                     func=AF.Copy).then_inc(a_cast, 1)
                        if dyg is not None:
                            s.drain()
                            s.dma_start(out=dyg[fc * 128:(fc + 1) * 128, :],
                                        in_=ygb[fc][:, :])
                blk.scalar(p2_scalar)

                def p2_vector(v):
                    for fc in range(NFC):
                        p = fc % 2
                        for tc in range(NTC):
                            u = fc * 4 + tc
                            sl = slice(tc * 512, (tc + 1) * 512)
                            v.wait_ge(t_pc, u + 1)
                            v.wait_ge(s_ex, u + 1)
                            if tc == 0 and fc >= 2:
                                v.wait_ge(a_cast, fc - 1)
                            v.tensor_mul(yt2[p][:, sl], psC[u % 2][:, :],
                                         exrep2[p][:, sl]).then_inc(v_cr, 1)
                        v.wait_ge(l4l[p], 80 * (fc // 2 + 1))
                        v.drain()
                        v.tensor_tensor(yt2[p][:, :], yt2[p][:, :], ylTt[p][:, :],
                                        OP.add)
                        v.drain()
                        v.scalar_tensor_tensor(yt2[p][:, :], Xt2[p][:, :],
                                               dpst[:, fc:fc + 1], yt2[p][:, :],
                                               OP.mult, OP.add)
                        v.drain()
                        v.tensor_mul(yt2[p][:, :], yt2[p][:, :], szt2[p][:, :]
                                     ).then_inc(v_gate, 1)
                blk.vector(p2_vector)

                def p2_gpsimd(g):
                    for fc in range(NFC):
                        p = fc % 2
                        g.wait_ge(v_gate, fc + 1)
                        if fc >= 2:
                            g.wait_ge(t_sq, 4 * (fc - 1))
                        g.tensor_mul(sqb2[p][:, :], yt2[p][:, :], yt2[p][:, :]
                                     ).then_inc(v_done, 1)
                blk.gpsimd(p2_gpsimd)

                ctx.barrier()

            psR = [ctx.ps(f"psR{i}", [128, 512], F32, pA) for i in range(2)]
            drs = dbg_out("rstd", [1, TPC], F32)
            steps = []

            def s_rstd(s, dma):
                for tc in range(NTC):
                    sl = slice(tc * 512, (tc + 1) * 512)
                    s.activation(out=rstd_sb[0:1, sl], in_=psS[tc][0:1, :],
                                 func=AF.Ln, scale=1.0 / D_INNER, bias=1e-5)
                    s.drain()
                    s.activation(out=rstd_sb[0:1, sl], in_=rstd_sb[0:1, sl],
                                 func=AF.Exp, scale=-0.5)
                    s.drain()
                if drs is not None:
                    dma(out=drs[:, :], in_=rstd_sb[:, :])
            steps.append(("scalar", s_rstd))

            for half in range(2):
                def s_psr(t, dma, half=half):
                    for i in range(2):
                        tc = half * 2 + i
                        t.matmul(psR[i][:, :], ones1_f[0:1, :],
                                 rstd_sb[0:1, tc * 512:(tc + 1) * 512],
                                 start=True, stop=True)
                steps.append(("tensor", s_psr))

                def s_rcp(s, dma, half=half):
                    for i in range(2):
                        tc = half * 2 + i
                        sl = slice(tc * 512, (tc + 1) * 512)
                        s.activation(out=rstdrep[:, sl], in_=psR[i][:, :],
                                     func=AF.Copy)
                        s.drain()
                steps.append(("scalar", s_rcp))

            ctx.seq("p3s", steps)
            ctx.barrier()

        dyn = dbg_out("yn", [D_INNER, TPC], BF16)
        steps = []
        for fc in range(NFC):
            def s_yn(v, dma, fc=fc):
                v.scalar_tensor_tensor(ygb[fc][:, :], ygb[fc][:, :],
                                       nwst[:, fc:fc + 1], rstdrep[:, :],
                                       OP.mult, OP.mult)
            steps.append(("vector", s_yn))
            if dyn is not None:
                def s_dyn(g, dma, fc=fc):
                    dma(out=dyn[fc * 128:(fc + 1) * 128, :], in_=ygb[fc][:, :])
                steps.append(("gpsimd", s_dyn))
        ctx.seq("p4s", steps)
        ctx.barrier()

        with ExitStack() as p5:
            woT = [ctx.sb(f"woT{i}", [128, D], BF16, p5) for i in range(NFC)]
            x1t2 = [ctx.sb(f"x1t{i}", [128, TPC], F32, p5) for i in range(2)]
            g1t2 = [ctx.sb(f"g1t{i}", [128, TPC], BF16, p5) for i in range(2)]
            x2t2 = [ctx.sb(f"x2t{i}", [128, TPC], F32, p5) for i in range(2)]
            psO = [ctx.ps(f"psO{i}", [128, 512], F32, p5) for i in range(2)]

            w5 = ctx.sem("w5")
            l5 = [ctx.sem("l5_0"), ctx.sem("l5_1")]
            t_o = ctx.sem("t_o")
            v_f = ctx.sem("v_f")
            v_oc = ctx.sem("v_oc")
            st5 = [ctx.sem("st5_0"), ctx.sem("st5_1")]

            def p5_sync(sy):
                sy.wait_ge(cast_sem, 16)
                for i in range(NFC):
                    sy.dma_start(out=woT[i][:, :],
                                 in_=woutbf[i * 128:(i + 1) * 128, :]
                                 ).then_inc(w5, 16)
                for oc in range(ND):
                    p = oc % 2
                    if oc >= 2:
                        sy.wait_ge(st5[p], 16 * (oc // 2))
                    sy.dma_start(out=x1t2[p][:, :],
                                 in_=x1f[oc * 128:(oc + 1) * 128, :]
                                 ).then_inc(l5[p], 16)
                    sy.dma_start(out=g1t2[p][:, :],
                                 in_=g1f[oc * 128:(oc + 1) * 128, :]
                                 ).then_inc(l5[p], 16)
            blk.sync(p5_sync)

            def p5_tensor(t):
                t.wait_ge(w5, 16 * NFC)
                for oc in range(ND):
                    for tc in range(NTC):
                        u = oc * 4 + tc
                        if u >= 2:
                            t.wait_ge(v_f, u - 1)
                        ins = None
                        for fc in range(NFC):
                            ins = t.matmul(psO[u % 2][:, :],
                                           woT[fc][:, oc * 128:(oc + 1) * 128],
                                           ygb[fc][:, tc * 512:(tc + 1) * 512],
                                           start=(fc == 0), stop=(fc == NFC - 1))
                        ins.then_inc(t_o, 1)
            blk.tensor(p5_tensor)

            def p5_vector(v):
                for oc in range(ND):
                    p = oc % 2
                    v.wait_ge(l5[p], 32 * (oc // 2 + 1))
                    if oc >= 2:
                        v.wait_ge(st5[p], 16 * (oc // 2))
                    for tc in range(NTC):
                        u = oc * 4 + tc
                        sl = slice(tc * 512, (tc + 1) * 512)
                        v.wait_ge(t_o, u + 1)
                        v.tensor_mul(x2t2[p][:, sl], psO[u % 2][:, :],
                                     g1t2[p][:, sl]).then_inc(v_f, 1)
                    v.drain()
                    v.engine_nop().then_inc(v_oc, 1)
            blk.vector(p5_vector)

            def p5_gpsimd(g):
                for oc in range(ND):
                    p = oc % 2
                    g.wait_ge(v_oc, oc + 1)
                    g.wait_ge(l5[p], 32 * (oc // 2 + 1))
                    if oc >= 2:
                        g.wait_ge(st5[p], 16 * (oc // 2))
                    for tc in range(NTC):
                        sl = slice(tc * 512, (tc + 1) * 512)
                        g.tensor_tensor(x2t2[p][:, sl], x2t2[p][:, sl],
                                        x1t2[p][:, sl], OP.add)
                    g.drain()
                    g.dma_start(out=x2o[oc * 128:(oc + 1) * 128, :],
                                in_=x2t2[p][:, :]).then_inc(st5[p], 16)
                g.wait_ge(st5[0], 16 * (ND // 2))
                g.wait_ge(st5[1], 16 * (ND // 2))
            blk.gpsimd(p5_gpsimd)

            ctx.barrier()

    return nc, dbg_tensors



def kernel(**inputs):
    import ml_dtypes
    bf = ml_dtypes.bfloat16
    f32 = np.float32
    CORES = list(range(NCORES))

    x = np.ascontiguousarray(np.asarray(inputs["L0_attn_x"], f32).reshape(B * T, D))
    wqkvT = np.ascontiguousarray(np.asarray(inputs["L0_attn_W_qkv"], f32).T)
    gpT = np.ascontiguousarray(
        np.asarray(inputs["L0_ffn_gate_proj"], f32).transpose(0, 2, 1))
    upT = np.ascontiguousarray(
        np.asarray(inputs["L0_ffn_up_proj"], f32).transpose(0, 2, 1))
    dpT = np.ascontiguousarray(
        np.asarray(inputs["L0_ffn_down_proj"], f32).transpose(0, 2, 1))

    maps1 = []
    for c in CORES:
        maps1.append({
            "x": np.ascontiguousarray(x[c * TPC:(c + 1) * TPC]),
            "wqkv": wqkvT,
            "gw": np.asarray(inputs["L0_ffn_gate_weight"], f32),
            "ebias": np.asarray(inputs["L0_ffn_e_bias"], f32).reshape(1, E0),
            "gatep": gpT,
            "upp": upT,
            "downp": dpT,
        })
    nc1, _ = build_launch1()
    r1 = run_bass_kernel_spmd(nc1, maps1, core_ids=CORES).results

    qkp = np.ascontiguousarray(np.concatenate(
        [np.asarray(inputs["L1_ffn_q_proj"], f32),
         np.asarray(inputs["L1_ffn_k_proj"], f32),
         np.asarray(inputs["L1_ffn_v_proj"], f32)], axis=2))
    woT_h = np.ascontiguousarray(np.asarray(inputs["L0_attn_W_o"], f32).T)
    winT_h = np.ascontiguousarray(np.asarray(inputs["L1_attn_W_in"], f32).T)
    maps2 = []
    for c in CORES:
        b, seg = c // 4, c % 4
        ckvall = np.zeros((4 * DH0, H0 * DH0), f32)
        for s in range(4):
            ckvall[s * DH0:(s + 1) * DH0] = np.asarray(
                r1[b * 4 + s]["ckv_sums"], f32)
        mcur = np.zeros((DH0, 4), f32)
        mcur[:, :seg] = 1.0
        posr = np.arange(seg * TPC + 1, (seg + 1) * TPC + 1,
                         dtype=f32).reshape(1, TPC)
        maps2.append({
            "x": np.ascontiguousarray(x[c * TPC:(c + 1) * TPC]),
            "qbf": np.asarray(r1[c]["q_out"]),
            "apbf": np.asarray(r1[c]["attn_part"]),
            "gbf": np.asarray(r1[c]["g_out"]),
            "ckvall": ckvall,
            "mcur": mcur,
            "posr": posr,
            "wo": woT_h,
            "win": winT_h,
            "simm": np.asarray(inputs["L1_ffn_sim_matrix"], f32),
            "gatesd": np.asarray(inputs["L1_ffn_gates"], f32).reshape(E1, 1),
            "qkp": qkp,
        })
    nc2, _ = build_launch2()
    r2 = run_bass_kernel_spmd(nc2, maps2, core_ids=CORES).results

    qkvall = np.concatenate([np.asarray(r2[c]["qkv_out"]) for c in CORES],
                            axis=0)
    mrwall = np.concatenate([np.asarray(r2[c]["mrw_out"], f32) for c in CORES],
                            axis=1)
    cw = np.asarray(inputs["L1_attn_conv_w"], f32).reshape(CONV_DIM, DCONV)
    maps3 = []
    for c in CORES:
        b, r = c // 4, c % 4
        seg = c % 4
        bnd = np.zeros((CONV_DIM, 4), bf)
        if seg > 0:
            bnd[:, 1:4] = np.asarray(r2[c - 1]["xbc_out"])[:, TPC - 3:TPC]
        qrows, krows, vrows, kdrows, vdrows, mvals_l, mrwg_l = \
            [], [], [], [], [], [], []
        for j in range(16):
            i = 4 * j + r
            g_sl = slice(b * T + i * 128, b * T + (i + 1) * 128)
            qrows.append(qkvall[g_sl, 0:128])
            kdrows.append(qkvall[g_sl, 128:256])
            vdrows.append(qkvall[g_sl, 256:384])
            mrwg_l.append(mrwall[:, g_sl])
        for g in range(4):
            nkb = 16 * (g + 1)
            krows.append(qkvall[b * T:b * T + nkb * 128, 128:256])
            vrows.append(qkvall[b * T:b * T + nkb * 128, 256:384])
            M = np.zeros((nkb, 512), f32)
            for rr in range(4):
                j = 4 * g + rr
                i = 4 * j + r
                for kb in range(nkb):
                    if kb >= i:
                        M[kb, rr * 128:(rr + 1) * 128] = 1e9
            mvals_l.append(M)
        maps3.append({
            "xbcp": np.asarray(r2[c]["xbc_out"]),
            "bnd3": bnd,
            "convw": cw,
            "convb": np.asarray(inputs["L1_attn_conv_b"], f32).reshape(CONV_DIM, 1),
            "dtraw": np.asarray(r2[c]["dtraw_out"], f32),
            "dtb": np.asarray(inputs["L1_attn_dt_bias"], f32).reshape(NH1, 1),
            "alog": np.asarray(inputs["L1_attn_A_log"], f32).reshape(NH1, 1),
            "dpar": np.asarray(inputs["L1_attn_D_param"], f32).reshape(1, NH1),
            "qtl": np.ascontiguousarray(np.concatenate(qrows, 0)),
            "kcat": np.ascontiguousarray(np.concatenate(krows, 0)),
            "vcat": np.ascontiguousarray(np.concatenate(vrows, 0)),
            "kdg": np.ascontiguousarray(np.concatenate(kdrows, 0)),
            "vdg": np.ascontiguousarray(np.concatenate(vdrows, 0)),
            "mvals": np.concatenate(mvals_l, 0),
            "mrwg": np.ascontiguousarray(np.concatenate(mrwg_l, 1)),
        })
        maps3[-1]["oproj"] = np.asarray(inputs["L1_ffn_o_proj"], f32)
    nc3, _ = build_launch3()
    r3 = run_bass_kernel_spmd(nc3, maps3, core_ids=CORES).results

    nwd = np.ascontiguousarray(
        np.asarray(inputs["L1_attn_norm_w"], f32).reshape(NFC_, 128).T)
    dpd = np.ascontiguousarray(np.repeat(
        np.asarray(inputs["L1_attn_D_param"], f32), P1).reshape(NFC_, 128).T)
    g1glob = np.empty((B * T, D), dtype=np.asarray(r3[0]["g1_out"]).dtype)
    for c in CORES:
        b, r = c // 4, c % 4
        g1c = np.asarray(r3[c]["g1_out"])
        for j in range(16):
            i = 4 * j + r
            g1glob[b * T + i * 128:b * T + (i + 1) * 128] = \
                g1c[:, j * 128:(j + 1) * 128].T
    maps4 = []
    for c in CORES:
        b = c // 4
        sts = np.zeros((3, N1, D_INNER), bf)
        dfrs = np.ones((3, D_INNER), f32)
        js = list(range(4 * b, c))
        for slot, j in zip(range(3 - len(js), 3), js):
            sts[slot] = np.asarray(r3[j]["states_out"])
            dfrs[slot] = np.repeat(np.asarray(r3[j]["decay_out"], f32).ravel(), P1)
        maps4.append({
            "ylt": np.asarray(r3[c]["ylocal_out"]),
            "xfm": np.asarray(r3[c]["xoh"]),
            "szf": np.asarray(r2[c]["siluz_out"]),
            "g1f": np.ascontiguousarray(g1glob[c * TPC:(c + 1) * TPC].T),
            "x1f": np.asarray(r2[c]["x1_out"], f32),
            "cto": np.asarray(r3[c]["C_out"]),
            "exct": np.ascontiguousarray(np.asarray(r3[c]["expcum_out"], f32).T),
            "st_in": sts,
            "dfr": dfrs,
            "nwd": nwd,
            "dpd": dpd,
            "wout": np.ascontiguousarray(np.asarray(inputs["L1_attn_W_out"], f32).T),
        })
    nc4, _ = build_launch4()
    r4 = run_bass_kernel_spmd(nc4, maps4, core_ids=CORES).results

    out = np.empty((B * T, D), f32)
    for c in CORES:
        out[c * TPC:(c + 1) * TPC] = np.asarray(r4[c]["x2_out"], f32).T
    return out.reshape(B, T, D)


NFC_ = D_INNER // 128


# version 05 (final)
# speedup vs baseline: 1.3042x; 1.0680x over previous
import numpy as np
from contextlib import ExitStack

import concourse.bass as bass
import concourse.mybir as mybir
from concourse.alu_op_type import AluOpType as OP
from concourse.bass_utils import run_bass_kernel_spmd

F32 = mybir.dt.float32
BF16 = mybir.dt.bfloat16
AF = mybir.ActivationFunctionType
AX = mybir.AxisListType

NCORES = 8
B, T, D = 2, 8192, 1024
TPC = (B * T) // NCORES
SEGS_PER_BATCH = T // TPC
NTT = TPC // 128
NTC = TPC // 512
ND = D // 128

H0, DH0, C0 = 16, 64, 64
NCH0 = TPC // C0
E0, I0 = 16, 512
D_INNER, NH1, P1, N1, DCONV = 2048, 32, 64, 128, 4
D_IN_PROJ, CONV_DIM = 4384, 2304
Q1 = 128
NCH1 = TPC // Q1
E1, DV1 = 8, 128

ENGINES = ("sync", "tensor", "vector", "scalar", "gpsimd")


class Ctx:
    def __init__(self, nc, stack, const_vals=()):
        self.nc = nc
        self.st = stack
        self.blk = stack.enter_context(nc.Block())
        self.bar = stack.enter_context(nc.semaphore("bar"))
        self.barn = 0
        self.const_tiles = []
        for i, v in enumerate(const_vals):
            t = stack.enter_context(nc.sbuf_tensor(f"cst{i}", [128, 1], F32))
            nc.const_aps.register(F32, float(v), t[:, :])
            self.const_tiles.append((t, float(v)))

    def emit_const_memsets(self, v_eng):
        for t, val in self.const_tiles:
            v_eng.memset(t[:, :], val)

    def sem(self, name):
        return self.st.enter_context(self.nc.semaphore(name))

    def sb(self, name, shape, dtype, stack=None):
        return (stack or self.st).enter_context(self.nc.sbuf_tensor(name, shape, dtype))

    def ps(self, name, shape, dtype=F32, stack=None):
        return (stack or self.st).enter_context(self.nc.psum_tensor(name, shape, dtype))

    def seq(self, name, steps):
        sem = self.sem(name)
        dsems = {}
        for en in ("sync", "gpsimd", "scalar"):
            dsems[en] = self.sem(name + "_d_" + en)
        dcount = {en: 0 for en in dsems}
        plans = {en: [] for en in ENGINES}
        for i, (en, fn) in enumerate(steps):
            plans[en].append((i, fn))
        for en in ENGINES:
            mysteps = plans[en]
            def f(eng, mysteps=mysteps, en=en):
                for i, fn in mysteps:
                    if i > 0:
                        eng.wait_ge(sem, i)
                    ndma = [0]
                    if en in dsems:
                        def dma(out, in_, _eng=eng, _en=en, **kw):
                            ndma[0] += 1
                            return _eng.dma_start(out=out, in_=in_, **kw
                                                  ).then_inc(dsems[_en], 16)
                        dma.transpose = lambda out, in_, _eng=eng, _en=en: (
                            ndma.__setitem__(0, ndma[0] + 1),
                            _eng.dma_start_transpose(out, in_).then_inc(dsems[_en], 16))[1]
                    else:
                        dma = None
                    fn(eng, dma)
                    if ndma[0]:
                        dcount[en] += ndma[0]
                        eng.wait_ge(dsems[en], 16 * dcount[en])
                    eng.drain()
                    eng.sem_inc(sem, 1)
            getattr(self.blk, en)(f)

    def barrier(self):
        self.barn += 1
        tgt = 5 * self.barn
        bar = self.bar
        for en in ENGINES:
            def f(eng, tgt=tgt, bar=bar):
                eng.drain()
                eng.sem_inc(bar, 1)
                eng.wait_ge(bar, tgt)
            getattr(self.blk, en)(f)


def build_launch1(debug=(), upto='E', nch0=NCH0, cparts=3):
    nc = bass.Bass("TRN2", target_bir_lowering=False)
    dbg = set(debug)

    xd = nc.dram_tensor("x", [TPC, D], F32, kind="ExternalInput").ap()
    wqkv = nc.dram_tensor("wqkv", [D, 3 * D], F32, kind="ExternalInput").ap()
    gw = nc.dram_tensor("gw", [E0, D], F32, kind="ExternalInput").ap()
    ebias = nc.dram_tensor("ebias", [1, E0], F32, kind="ExternalInput").ap()
    gatep = nc.dram_tensor("gatep", [E0, D, I0], F32, kind="ExternalInput").ap()
    upp = nc.dram_tensor("upp", [E0, D, I0], F32, kind="ExternalInput").ap()
    downp = nc.dram_tensor("downp", [E0, I0, D], F32, kind="ExternalInput").ap()

    q_out = nc.dram_tensor("q_out", [D, TPC], BF16, kind="ExternalOutput").ap()
    ap_out = nc.dram_tensor("attn_part", [D, TPC], BF16, kind="ExternalOutput").ap()
    ckv_out = nc.dram_tensor("ckv_sums", [DH0, H0 * DH0], F32, kind="ExternalOutput").ap()
    g_out = nc.dram_tensor("g_out", [D, TPC], BF16, kind="ExternalOutput").ap()

    xnth = nc.dram_tensor("xnth", [TPC, D], BF16).ap()
    wqkvbf = nc.dram_tensor("wqkvbf", [D, 3 * D], BF16).ap()
    gwbf = nc.dram_tensor("gwbf", [E0, D], BF16).ap()
    kthb = nc.dram_tensor("kthb", [D, TPC], BF16).ap()
    gpbf = nc.dram_tensor("gpbf", [E0, D, I0], BF16).ap()
    upbf = nc.dram_tensor("upbf", [E0, D, I0], BF16).ap()
    dpbf = nc.dram_tensor("dpbf", [E0, I0, D], BF16).ap()

    dbg_tensors = {}
    def dbg_out(name, shape, dtype=F32):
        if name in dbg:
            dbg_tensors[name] = nc.dram_tensor("dbg_" + name, shape, dtype,
                                               kind="ExternalOutput").ap()
            return dbg_tensors[name]
        return None

    with ExitStack() as st:
        ctx = Ctx(nc, st, const_vals=(1e-5, 0.125, 1e-20, -1e30, -1.0))
        blk = ctx.blk

        ones_bf = ctx.sb("ones_bf", [128, 1], BF16)
        ones1_f = ctx.sb("ones1_f", [1, 128], F32)
        ones1_bf = ctx.sb("ones1_bf", [1, 128], BF16)
        iv = ctx.sb("iv", [128, 128], F32)
        ident_f = ctx.sb("ident_f", [128, 128], F32)
        ident_bf = ctx.sb("ident_bf", [128, 128], BF16)
        mskge_bf = ctx.sb("mskge_bf", [128, 128], BF16)
        ebias_t = ctx.sb("ebias_t", [128, E0], F32)
        bonus_t = ctx.sb("bonus_t", [128, E0], F32)

        wc_sem = ctx.sem("wc_sem")
        mc_sem = ctx.sem("mc_sem")
        eb_sem = ctx.sem("eb_sem")
        k_sem = ctx.sem("k_sem")
        xnh_sem = [ctx.sem("xnh_sem0"), ctx.sem("xnh_sem1")]

        def ph0_gpsimd(g):
            g.dma_start(out=wqkvbf[:, :], in_=wqkv[:, :]).then_inc(wc_sem, 16)
            g.dma_start(out=gwbf[:, :], in_=gw[:, :]).then_inc(wc_sem, 16)
            g.dma_start(out=ebias_t[:, :], in_=ebias[0:1, :].partition_broadcast(128)
                        ).then_inc(eb_sem, 16)
            g.iota(iv[:, :], pattern=[[1, 128]], base=0, channel_multiplier=-1,
                   allow_small_or_imprecise_dtypes=True)
            g.iota(bonus_t[:, :], pattern=[[-1, E0]], base=E0, channel_multiplier=0,
                   allow_small_or_imprecise_dtypes=True).then_inc(k_sem, 1)
        blk.gpsimd(ph0_gpsimd)

        def ph0_vector(v):
            ctx.emit_const_memsets(v)
            v.memset(ones_bf[:, :], 1.0)
            v.memset(ones1_f[:, :], 1.0)
            v.memset(ones1_bf[:, :], 1.0)
            v.wait_ge(k_sem, 1)
            v.tensor_single_scalar(ident_f[:, :], iv[:, :], 0.0, OP.is_equal)
            v.drain()
            v.tensor_copy(ident_bf[:, :], ident_f[:, :])
            v.tensor_single_scalar(mskge_bf[:, :], iv[:, :], 0.0, OP.is_ge)
            v.tensor_scalar_mul(bonus_t[:, :], bonus_t[:, :], 1e-20)
        blk.vector(ph0_vector)

        ctx.barrier()

        with ExitStack() as sa:
            xrow = [ctx.sb(f"xrow{i}", [128, D], F32, sa) for i in range(2)]
            sqr = [ctx.sb(f"sqr{i}", [128, D], F32, sa) for i in range(2)]
            stat = [ctx.sb(f"stat{i}", [128, 6], F32, sa) for i in range(2)]
            xnt = [ctx.sb(f"xnt{i}", [128, D], BF16, sa) for i in range(2)]

            xl_sem = [ctx.sem("xl_sem0"), ctx.sem("xl_sem1")]
            sq_sem = ctx.sem("sq_sem")
            st_sem = ctx.sem("st_sem")
            rs_sem = ctx.sem("rs_sem")
            xn_sem = ctx.sem("xn_sem")

            def pa_sync(sy):
                for tt in range(NTT):
                    p = tt % 2
                    if tt >= 2:
                        sy.wait_ge(xn_sem, tt - 1)
                    sy.dma_start(out=xrow[p][:, :], in_=xd[tt * 128:(tt + 1) * 128, :]
                                 ).then_inc(xl_sem[p], 16)
            blk.sync(pa_sync)

            def pa_scalar(s):
                for tt in range(NTT):
                    p = tt % 2
                    s.wait_ge(xl_sem[p], 16 * (tt // 2 + 1))
                    if tt >= 2:
                        s.wait_ge(st_sem, 2 * (tt - 1))
                    s.activation(out=sqr[p][:, :], in_=xrow[p][:, :],
                                 func=AF.Square).then_inc(sq_sem, 1)
                    s.wait_ge(st_sem, 2 * tt + 1)
                    s.activation(out=stat[p][:, 4:5], in_=stat[p][:, 3:4],
                                 func=AF.Ln, bias=1e-5)
                    s.drain()
                    s.activation(out=stat[p][:, 4:5], in_=stat[p][:, 4:5],
                                 func=AF.Exp, scale=-0.5).then_inc(rs_sem, 1)
                    s.wait_ge(st_sem, 2 * tt + 2)
                    if tt >= 2:
                        s.wait_ge(xnh_sem[p], 16 * (tt // 2))
                    s.activation(out=xnt[p][:, :], in_=xrow[p][:, :],
                                 func=AF.Identity,
                                 scale=stat[p][:, 4:5],
                                 bias=stat[p][:, 5:6]).then_inc(xn_sem, 1)
            blk.scalar(pa_scalar)



            def pa_vector(v):
                for tt in range(NTT):
                    p = tt % 2
                    v.wait_ge(xl_sem[p], 16 * (tt // 2 + 1))
                    v.wait_ge(sq_sem, tt + 1)
                    if tt >= 2:
                        v.wait_ge(xn_sem, tt - 1)
                    v.reduce_sum(stat[p][:, 0:1], xrow[p][:, :], axis=AX.X)
                    v.reduce_sum(stat[p][:, 1:2], sqr[p][:, :], axis=AX.X)
                    v.drain()
                    v.tensor_scalar_mul(stat[p][:, 2:3], stat[p][:, 0:1], 1.0 / D)
                    v.tensor_scalar_mul(stat[p][:, 3:4], stat[p][:, 1:2], 1.0 / D)
                    v.drain()
                    v.tensor_mul(stat[p][:, 5:6], stat[p][:, 2:3], stat[p][:, 2:3])
                    v.drain()
                    v.tensor_sub(stat[p][:, 3:4], stat[p][:, 3:4], stat[p][:, 5:6]
                                 ).then_inc(st_sem, 1)
                    v.wait_ge(rs_sem, tt + 1)
                    v.drain()
                    v.tensor_mul(stat[p][:, 5:6], stat[p][:, 2:3], stat[p][:, 4:5])
                    v.drain()
                    v.tensor_scalar_mul(stat[p][:, 5:6], stat[p][:, 5:6], -1.0
                                        ).then_inc(st_sem, 1)
                    v.drain()
            blk.vector(pa_vector)

            def pa_gpsimd(g):
                for tt in range(NTT):
                    p = tt % 2
                    g.wait_ge(xn_sem, tt + 1)
                    g.dma_start(out=xnth[tt * 128:(tt + 1) * 128, :], in_=xnt[p][:, :]
                                ).then_inc(xnh_sem[p], 16)
            blk.gpsimd(pa_gpsimd)

            def pa_gpsimd_casts(g):
                g.dma_start(out=gpbf[:, :, :], in_=gatep[:, :, :]).then_inc(mc_sem, 16)
                g.dma_start(out=upbf[:, :, :], in_=upp[:, :, :]).then_inc(mc_sem, 16)
                g.dma_start(out=dpbf[:, :, :], in_=downp[:, :, :]).then_inc(mc_sem, 16)
            blk.gpsimd(pa_gpsimd_casts)

            ctx.barrier()

        with ExitStack() as sb:
            qT = [ctx.sb(f"qT{c}", [128, TPC], BF16, sb) for c in range(8)]
            kT = [ctx.sb(f"kT{c}", [128, TPC], BF16, sb) for c in range(8)]
            vtok = [ctx.sb(f"vtok{i}", [128, D], BF16, sb) for i in range(NTT)]
            with ExitStack() as sb2:
                xn2 = [ctx.sb(f"xn2_{c}", [128, TPC], BF16, sb2) for c in range(ND)]
                wq = [ctx.sb(f"wq{c}", [128, 3 * D], BF16, sb2) for c in range(ND)]

                xn2_sem = ctx.sem("xn2_sem")
                wq_sem = ctx.sem("wq_sem")
                qk_mm = ctx.sem("qk_mm")
                qk_ev = ctx.sem("qk_ev")
                v_mm = ctx.sem("v_mm")
                v_ev = ctx.sem("v_ev")
                kb_sem = ctx.sem("kb_sem")
                qo_sem = ctx.sem("qo_sem")

                def pb_sync(sy):
                    sy.wait_ge(xnh_sem[0], 16 * (NTT // 2))
                    sy.wait_ge(xnh_sem[1], 16 * (NTT // 2))
                    for c in range(ND):
                        sy.dma_start_transpose(xn2[c][:, :],
                                               xnth[:, c * 128:(c + 1) * 128]
                                               ).then_inc(xn2_sem, 16)
                    sy.wait_ge(wc_sem, 32)
                    for c in range(ND):
                        sy.dma_start(out=wq[c][:, :],
                                     in_=wqkvbf[c * 128:(c + 1) * 128, :]
                                     ).then_inc(wq_sem, 16)
                blk.sync(pb_sync)

                with ExitStack() as sp:
                    ps_qk = [ctx.ps(f"ps_qk{i}", [128, 512], F32, sp) for i in range(8)]

                    def pb_tensor_qk(t):
                        t.wait_ge(xn2_sem, 16 * ND)
                        t.wait_ge(wq_sem, 16 * ND)
                        for oc in range(16):
                            if oc >= 2:
                                t.wait_ge(qk_ev, (oc - 1) * 4)
                            for d in range(ND):
                                for tc in range(NTC):
                                    ins = t.matmul(ps_qk[(oc % 2) * 4 + tc][:, :],
                                                   wq[d][:, oc * 128:(oc + 1) * 128],
                                                   xn2[d][:, tc * 512:(tc + 1) * 512],
                                                   start=(d == 0), stop=(d == ND - 1))
                            ins.then_inc(qk_mm, 1)
                    blk.tensor(pb_tensor_qk)

                    def pb_scalar_qk(s):
                        for oc in range(16):
                            s.wait_ge(qk_mm, oc + 1)
                            for tc in range(NTC):
                                src = ps_qk[(oc % 2) * 4 + tc][:, :]
                                sl = slice(tc * 512, (tc + 1) * 512)
                                if oc < 8:
                                    s.activation(out=qT[oc][:, sl], in_=src,
                                                 func=AF.Copy, scale=0.125
                                                 ).then_inc(qk_ev, 1)
                                else:
                                    s.activation(out=kT[oc - 8][:, sl], in_=src,
                                                 func=AF.Sign).then_inc(qk_ev, 1)
                    blk.scalar(pb_scalar_qk)

                    def pb_gpsimd_qk(g):
                        for oc in range(8):
                            g.wait_ge(qk_ev, oc * 4 + 4)
                            g.dma_start(out=q_out[oc * 128:(oc + 1) * 128, :],
                                        in_=qT[oc][:, :]).then_inc(qo_sem, 16)
                        for oc in range(8):
                            g.wait_ge(qk_ev, 32 + oc * 4 + 4)
                            g.dma_start(out=kthb[oc * 128:(oc + 1) * 128, :],
                                        in_=kT[oc][:, :]).then_inc(kb_sem, 16)
                    blk.gpsimd(pb_gpsimd_qk)

                    ctx.barrier()

                with ExitStack() as sp:
                    ps_v = [ctx.ps(f"ps_v{i}", [128, 512], F32, sp) for i in range(4)]

                    def pb_tensor_v(t):
                        for tt in range(NTT):
                            if tt >= 2:
                                t.wait_ge(v_ev, (tt - 1) * 2)
                            for d in range(ND):
                                for n2 in range(2):
                                    ins = t.matmul(ps_v[(tt % 2) * 2 + n2][:, :],
                                                   xn2[d][:, tt * 128:(tt + 1) * 128],
                                                   wq[d][:, 2 * D + n2 * 512:2 * D + (n2 + 1) * 512],
                                                   start=(d == 0), stop=(d == ND - 1))
                            ins.then_inc(v_mm, 1)
                    blk.tensor(pb_tensor_v)

                    def pb_scalar_v(s):
                        for tt in range(NTT):
                            s.wait_ge(v_mm, tt + 1)
                            for n2 in range(2):
                                s.activation(out=vtok[tt][:, n2 * 512:(n2 + 1) * 512],
                                             in_=ps_v[(tt % 2) * 2 + n2][:, :],
                                             func=AF.Sign).then_inc(v_ev, 1)
                    blk.scalar(pb_scalar_v)

                    ctx.barrier()

            dq = dbg_out("qT", [D, TPC])
            dk = dbg_out("kT", [D, TPC])
            dv = dbg_out("vtok", [TPC, D])
            if dq is not None:
                dbg_sem_b = ctx.sem("dbg_sem_b")
                def pb_dbg(g):
                    n = 0
                    for oc in range(8):
                        g.dma_start(out=dq[oc * 128:(oc + 1) * 128, :], in_=qT[oc][:, :]
                                    ).then_inc(dbg_sem_b, 16); n += 16
                        g.dma_start(out=dk[oc * 128:(oc + 1) * 128, :], in_=kT[oc][:, :]
                                    ).then_inc(dbg_sem_b, 16); n += 16
                    for tt in range(NTT):
                        g.dma_start(out=dv[tt * 128:(tt + 1) * 128, :], in_=vtok[tt][:, :]
                                    ).then_inc(dbg_sem_b, 16); n += 16
                    g.wait_ge(dbg_sem_b, n)
                blk.gpsimd(pb_dbg)
                ctx.barrier()

            if upto in ('A', 'B'):
                return nc, dbg_tensors
            with ExitStack() as sc:
                ktok = [ctx.sb(f"ktok{i}", [128, D], BF16, sc) for i in range(NTT)]
                attnT = [ctx.sb(f"attnT{c}", [128, TPC], BF16, sc) for c in range(8)]
                prefF = [ctx.sb(f"prefF{i}", [64, H0 * DH0], F32, sc) for i in range(2)]
                prefB = ctx.sb("prefB", [128, (H0 // 2) * DH0], BF16, sc)
                Sm = [ctx.sb(f"Sm{i}", [128, 128], BF16, sc) for i in range(2)]
                psS = [ctx.ps(f"psS{i}", [128, 128], F32, sc) for i in range(2)]
                psIC = [ctx.ps(f"psIC{i}", [64, 128], F32, sc) for i in range(2)]
                psCR = [ctx.ps(f"psCR{i}", [64, 128], F32, sc) for i in range(2)]
                psCK = [ctx.ps(f"psCK{i}", [64, DH0], F32, sc) for i in range(2)]

                kt_sem = ctx.sem("kt_sem")
                t_cr = ctx.sem("t_cr")
                s_cr = ctx.sem("s_cr")
                t_sm = ctx.sem("t_sm")
                t_ic = ctx.sem("t_ic")
                v_sm = ctx.sem("v_sm")
                v_dve = ctx.sem("v_dve")
                ao_sem = ctx.sem("ao_sem")

                NU = (nch0 // 2) * H0

                def unit(j):
                    ch, h = divmod(j, H0)
                    return ch, h

                def pb_sl(h):
                    return prefB[64 * (h % 2):64 * (h % 2) + 64,
                                 (h // 2) * DH0:(h // 2) * DH0 + DH0]

                def pc_sync(sy):
                    sy.wait_ge(kb_sem, 16 * 8)
                    sy.wait_ge(qo_sem, 16 * 8)
                    for tt in range(NTT if cparts >= 0 else 0):
                        sy.dma_start_transpose(ktok[tt][:, :],
                                               kthb[:, tt * 128:(tt + 1) * 128]
                                               ).then_inc(kt_sem, 16)
                blk.sync(pc_sync)

                def pc_vector0(v):
                    v.memset(prefF[0][:, :], 0.0)
                    v.memset(prefF[1][:, :], 0.0)
                    v.memset(prefB[:, :], 0.0).then_inc(v_dve, 1)
                    v.drain()
                blk.vector(pc_vector0)

                def mm_unit(t, j):
                    ch, h = unit(j)
                    hs = slice(64 * (h % 2), 64 * (h % 2) + 64)
                    cs = slice(ch * 128, (ch + 1) * 128)
                    t.matmul(psIC[j % 2][:, :], vtok[ch][:, h * 64:(h + 1) * 64],
                             Sm[j % 2][:, :], start=True, stop=True)
                    if cparts >= 2:
                        t.matmul(psCR[j % 2][:, :], pb_sl(h),
                                 qT[h // 2][hs, cs], start=True, stop=True
                                 ).then_inc(t_cr, 1)
                    else:
                        t.nop().then_inc(t_cr, 1)
                    if cparts >= 3:
                        t.matmul(psCK[j % 2][:, :], ktok[ch][:, h * 64:(h + 1) * 64],
                                 vtok[ch][:, h * 64:(h + 1) * 64],
                                 start=True, stop=True).then_inc(t_ic, 1)
                    else:
                        t.nop().then_inc(t_ic, 1)

                def pc_tensor(t):
                    if cparts >= 0:
                        t.wait_ge(kt_sem, 16 * NTT)
                    t.wait_ge(v_dve, 1)
                    for j in range(NU if cparts >= 1 else 0):
                        ch, h = unit(j)
                        if j >= 2:
                            t.wait_ge(v_sm, j - 1)
                            t.wait_ge(v_dve, j - 1 + 1)
                            if cparts >= 2:
                                t.wait_ge(s_cr, j - 1)
                        hs = slice(64 * (h % 2), 64 * (h % 2) + 64)
                        cs = slice(ch * 128, (ch + 1) * 128)
                        t.matmul(psS[j % 2][:, :], kT[h // 2][hs, cs],
                                 qT[h // 2][hs, cs], start=True, stop=True
                                 ).then_inc(t_sm, 1)
                        if j >= 1:
                            t.wait_ge(v_sm, j)
                            mm_unit(t, j - 1)
                    if cparts >= 1:
                        t.wait_ge(v_sm, NU)
                        mm_unit(t, NU - 1)
                blk.tensor(pc_tensor)

                def pc_scalar(s):
                    for j in range(NU if cparts >= 2 else 0):
                        ch, h = unit(j)
                        s.wait_ge(t_cr, j + 1)
                        s.activation(out=attnT[h // 2][64 * (h % 2):64 * (h % 2) + 64,
                                                       ch * 128:(ch + 1) * 128],
                                     in_=psCR[j % 2][:, :],
                                     func=AF.Copy).then_inc(s_cr, 1)
                blk.scalar(pc_scalar)

                def ev_unit(v, j):
                    ch, h = unit(j)
                    cur, nxt = ch % 2, (ch + 1) % 2
                    hs64 = slice(h * 64, (h + 1) * 64)
                    asl = attnT[h // 2][64 * (h % 2):64 * (h % 2) + 64,
                                        ch * 128:(ch + 1) * 128]
                    if cparts >= 2:
                        v.wait_ge(s_cr, j + 1)
                        v.scalar_tensor_tensor(asl, psIC[j % 2][:, :], 1.0, asl,
                                               OP.mult, OP.add)
                    else:
                        v.tensor_copy(asl, psIC[j % 2][:, :])
                    if cparts >= 3:
                        v.tensor_add(pb_sl(h), psCK[j % 2][:, :], prefF[cur][:, hs64])
                        v.tensor_add(prefF[nxt][:, hs64], psCK[j % 2][:, :],
                                     prefF[cur][:, hs64]).then_inc(v_dve, 1)
                    else:
                        v.engine_nop().then_inc(v_dve, 1)

                def pc_vector(v):
                    for j in range(NU if cparts >= 1 else 0):
                        ch, h = unit(j)
                        v.wait_ge(t_sm, j + 1)
                        v.tensor_mul(Sm[j % 2][:, :], psS[j % 2][:, :],
                                     mskge_bf[:, :]).then_inc(v_sm, 1)
                        if j >= 1:
                            v.wait_ge(t_ic, j)
                            ev_unit(v, j - 1)
                    if cparts >= 1:
                        v.wait_ge(t_ic, NU)
                        ev_unit(v, NU - 1)
                blk.vector(pc_vector)

                def pc_gpsimd(g):
                    g.wait_ge(v_dve, (NU if cparts >= 1 else 0) + 1)
                    for c in range(8):
                        g.dma_start(out=ap_out[c * 128:(c + 1) * 128, :],
                                    in_=attnT[c][:, :]).then_inc(ao_sem, 16)
                    g.dma_start(out=ckv_out[:, :], in_=prefF[0][:, :]).then_inc(ao_sem, 16)
                    g.wait_ge(ao_sem, 16 * 9)
                    g.wait_ge(qo_sem, 16 * 8)
                blk.gpsimd(pc_gpsimd)

                ctx.barrier()

        if upto in ('A', 'B', 'C'):
            return nc, dbg_tensors
        with ExitStack() as sd:
            xn3 = [ctx.sb(f"xn3_{c}", [128, TPC], BF16, sd) for c in range(ND)]
            cmbT = ctx.sb("cmbT", [E0, TPC], F32, sd)
            yacc = [ctx.sb(f"yacc{c}", [128, TPC], F32, sd) for c in range(ND)]

            xn3_sem = ctx.sem("xn3_sem")
            gwt_sem = ctx.sem("gwt_sem")
            lg_mm = ctx.sem("lg_mm")
            lg_ev = ctx.sem("lg_ev")
            tr_mm = ctx.sem("tr_mm")
            ev_sem = ctx.sem("ev_sem")
            sg_sem = ctx.sem("sg_sem")
            cf_sem = ctx.sem("cf_sem")
            cb_mm = ctx.sem("cb_mm")
            cbT_sem = ctx.sem("cbT_sem")

            with ExitStack() as sd2:
                gwT = [ctx.sb(f"gwT{c}", [128, E0], BF16, sd2) for c in range(ND)]
                logT = ctx.sb("logT", [E0, TPC], F32, sd2)
                ltok = [ctx.sb(f"ltok{i}", [128, E0], F32, sd2) for i in range(NTT)]
                sc_t = [ctx.sb(f"sc{i}", [128, E0], F32, sd2) for i in range(NTT)]
                sfc_t = [ctx.sb(f"sfc{i}", [128, E0], F32, sd2) for i in range(NTT)]
                smk_t = [ctx.sb(f"smk{i}", [128, E0], F32, sd2) for i in range(NTT)]
                tmpb_t = [ctx.sb(f"tmpb{i}", [128, E0], F32, sd2) for i in range(NTT)]
                g4_t = [ctx.sb(f"g4_{i}", [128, 4], F32, sd2) for i in range(NTT)]
                p6_t = [ctx.sb(f"p6_{i}", [128, 4], F32, sd2) for i in range(NTT)]
                sec_t = [ctx.sb(f"sec{i}", [128, 1], F32, sd2) for i in range(NTT)]
                mx_t = [ctx.sb(f"mx{i}", [128, 1], F32, sd2) for i in range(NTT)]
                eq_t = [ctx.sb(f"eq{i}", [128, E0], F32, sd2) for i in range(NTT)]
                cmb_t = [ctx.sb(f"cmb{i}", [128, E0], F32, sd2) for i in range(NTT)]

                ps_lg = [ctx.ps(f"ps_lg{i}", [E0, 512], F32, sd2) for i in range(2)]
                ps_tr = [ctx.ps(f"ps_tr{i}", [128, E0], F32, sd2) for i in range(3)]
                ps_cb = [ctx.ps(f"ps_cb{i}", [E0, 128], F32, sd2) for i in range(3)]

                def pd_sync(sy):
                    sy.wait_ge(xnh_sem[0], 16 * (NTT // 2))
                    sy.wait_ge(xnh_sem[1], 16 * (NTT // 2))
                    for c in range(ND):
                        sy.dma_start_transpose(xn3[c][:, :],
                                               xnth[:, c * 128:(c + 1) * 128]
                                               ).then_inc(xn3_sem, 16)
                    for c in range(ND):
                        sy.dma_start_transpose(gwT[c][:, :],
                                               gwbf[:, c * 128:(c + 1) * 128]
                                               ).then_inc(gwt_sem, 16)
                blk.sync(pd_sync)

                def pd_tensor(t):
                    t.wait_ge(xn3_sem, 16 * ND)
                    t.wait_ge(gwt_sem, 16 * ND)
                    for tc in range(NTC):
                        if tc >= 2:
                            t.wait_ge(lg_ev, tc - 1)
                        for d in range(ND):
                            ins = t.matmul(ps_lg[tc % 2][:, :], gwT[d][:, :],
                                           xn3[d][:, tc * 512:(tc + 1) * 512],
                                           start=(d == 0), stop=(d == ND - 1))
                        ins.then_inc(lg_mm, 1)
                    t.wait_ge(lg_ev, NTC)
                    for tt in range(NTT):
                        if tt >= 3:
                            t.wait_ge(ev_sem, tt - 2)
                        t.transpose(ps_tr[tt % 3][:, :], logT[:, tt * 128:(tt + 1) * 128],
                                    ident_f[0:E0, 0:E0]).then_inc(tr_mm, 1)
                    for tt in range(NTT):
                        if tt >= 3:
                            t.wait_ge(cbT_sem, tt - 2)
                        t.wait_ge(cf_sem, tt + 1)
                        t.transpose(ps_cb[tt % 3][:, :], cmb_t[tt][:, :],
                                    ident_f[:, :]).then_inc(cb_mm, 1)
                blk.tensor(pd_tensor)

                def pd_scalar(s):
                    for tt in range(NTT):
                        s.wait_ge(ev_sem, tt + 1)
                        s.activation(out=sc_t[tt][:, :], in_=ltok[tt][:, :],
                                     func=AF.Sigmoid).then_inc(sg_sem, 1)
                blk.scalar(pd_scalar)

                def pd_vector(v):
                    for tc in range(NTC):
                        v.wait_ge(lg_mm, tc + 1)
                        v.tensor_copy(logT[:, tc * 512:(tc + 1) * 512],
                                      ps_lg[tc % 2][:, :]).then_inc(lg_ev, 1)
                    for tt in range(NTT):
                        v.wait_ge(tr_mm, tt + 1)
                        v.tensor_copy(ltok[tt][:, :], ps_tr[tt % 3][:, :]).then_inc(ev_sem, 1)
                    v.wait_ge(sg_sem, NTT)
                    v.wait_ge(eb_sem, 16)
                    v.drain()
                    for tt in range(NTT):
                        v.tensor_add(sfc_t[tt][:, :], sc_t[tt][:, :], ebias_t[:, :])
                    v.drain()
                    pairs = [(0, 1), (0, 2), (0, 3), (1, 2), (1, 3), (2, 3)]
                    for pi, (a, b) in enumerate(pairs):
                        for tt in range(NTT):
                            v.tensor_add(p6_t[tt][:, :], sfc_t[tt][:, a:16:4],
                                         sfc_t[tt][:, b:16:4])
                        v.drain()
                        for tt in range(NTT):
                            if pi == 0:
                                v.tensor_copy(g4_t[tt][:, :], p6_t[tt][:, :])
                            else:
                                v.tensor_max(g4_t[tt][:, :], g4_t[tt][:, :], p6_t[tt][:, :])
                        v.drain()
                    for pi, (a, b) in enumerate(pairs):
                        for tt in range(NTT):
                            v.tensor_tensor(p6_t[tt][:, 0:1], g4_t[tt][:, a:a + 1],
                                            g4_t[tt][:, b:b + 1], OP.min)
                        v.drain()
                        for tt in range(NTT):
                            if pi == 0:
                                v.tensor_copy(sec_t[tt][:, :], p6_t[tt][:, 0:1])
                            else:
                                v.tensor_max(sec_t[tt][:, :], sec_t[tt][:, :], p6_t[tt][:, 0:1])
                        v.drain()
                    for tt in range(NTT):
                        v.tensor_single_scalar(g4_t[tt][:, :], g4_t[tt][:, :],
                                               sec_t[tt][:, :], OP.is_ge)
                    v.drain()
                    for i in range(4):
                        for tt in range(NTT):
                            v.tensor_copy(smk_t[tt][:, i:16:4], g4_t[tt][:, :])
                    v.drain()
                    for tt in range(NTT):
                        v.tensor_mul(smk_t[tt][:, :], smk_t[tt][:, :], sfc_t[tt][:, :])
                    v.drain()
                    for tt in range(NTT):
                        v.tensor_add(tmpb_t[tt][:, :], smk_t[tt][:, :], bonus_t[:, :])
                    v.drain()
                    for tt in range(NTT):
                        v.tensor_copy(smk_t[tt][:, :], tmpb_t[tt][:, :])
                    v.drain()
                    for rnd in range(4):
                        for tt in range(NTT):
                            v.reduce_max(mx_t[tt][:, :], tmpb_t[tt][:, :], axis=AX.X)
                        v.drain()
                        if rnd < 3:
                            for tt in range(NTT):
                                v.tensor_single_scalar(eq_t[tt][:, :], tmpb_t[tt][:, :],
                                                       mx_t[tt][:, :], OP.is_equal)
                            v.drain()
                            for tt in range(NTT):
                                v.tensor_scalar_mul(eq_t[tt][:, :], eq_t[tt][:, :], -1e30)
                            v.drain()
                            for tt in range(NTT):
                                v.tensor_add(tmpb_t[tt][:, :], tmpb_t[tt][:, :],
                                             eq_t[tt][:, :])
                            v.drain()
                    for tt in range(NTT):
                        v.tensor_single_scalar(eq_t[tt][:, :], smk_t[tt][:, :],
                                               mx_t[tt][:, :], OP.is_ge)
                    v.drain()
                    for tt in range(NTT):
                        v.tensor_mul(cmb_t[tt][:, :], sc_t[tt][:, :], eq_t[tt][:, :])
                    v.drain()
                    for tt in range(NTT):
                        v.reduce_sum(mx_t[tt][:, :], cmb_t[tt][:, :], axis=AX.X)
                    v.drain()
                    for tt in range(NTT):
                        v.tensor_scalar_add(mx_t[tt][:, :], mx_t[tt][:, :], 1e-20)
                    v.drain()
                    for tt in range(NTT):
                        v.reciprocal(sec_t[tt][:, :], mx_t[tt][:, :])
                    v.drain()
                    for tt in range(NTT):
                        v.tensor_single_scalar(cmb_t[tt][:, :], cmb_t[tt][:, :],
                                               sec_t[tt][:, :], OP.mult).then_inc(cf_sem, 1)
                    v.drain()
                    for tt in range(NTT):
                        v.wait_ge(cb_mm, tt + 1)
                        v.tensor_copy(cmbT[:, tt * 128:(tt + 1) * 128],
                                      ps_cb[tt % 3][:, :]).then_inc(cbT_sem, 1)
                blk.vector(pd_vector)

                dcmb = dbg_out("cmb", [E0, TPC])
                if dcmb is not None:
                    dbg_sem_d = ctx.sem("dbg_sem_d")
                    def pd_dbg(g):
                        g.wait_ge(cbT_sem, NTT)
                        g.dma_start(out=dcmb[:, :], in_=cmbT[:, :]).then_inc(dbg_sem_d, 16)
                        g.wait_ge(dbg_sem_d, 16)
                    blk.gpsimd(pd_dbg)

                ctx.barrier()

            if upto == 'D':
                return nc, dbg_tensors
            with ExitStack() as se:
                NW = 20
                gT_w = [[ctx.sb(f"gTw{p}_{c}", [128, I0], BF16, se) for c in range(ND)] for p in range(2)]
                uT_w = [[ctx.sb(f"uTw{p}_{c}", [128, I0], BF16, se) for c in range(ND)] for p in range(2)]
                dT_w = [[ctx.sb(f"dTw{p}_{c}", [128, D], BF16, se) for c in range(4)] for p in range(2)]
                act_t = [ctx.sb(f"act{ic}", [128, TPC], BF16, se) for ic in range(4)]
                gob = [ctx.sb(f"gob{i}", [128, TPC], BF16, se) for i in range(2)]
                cstage = [ctx.sb(f"cstage{i}", [1, TPC], BF16, se) for i in range(2)]

                ps_g = [ctx.ps(f"ps_g{i}", [128, 512], F32, se) for i in range(2)]
                ps_u = [ctx.ps(f"ps_u{i}", [128, 512], F32, se) for i in range(2)]
                ps_d = [ctx.ps(f"ps_d{i}", [128, 512], F32, se) for i in range(3)]
                ps_wr = ctx.ps("ps_wr", [128, 512], F32, se)
                wrep = ctx.sb("wrep", [128, TPC], F32, se)
                sg2 = [ctx.sb(f"sg2_{i}", [128, 1024], F32, se) for i in range(2)]

                we_sem = [ctx.sem("we_sem0"), ctx.sem("we_sem1")]
                cs_sem = [ctx.sem("cs_sem0"), ctx.sem("cs_sem1")]
                wr_mm = ctx.sem("wr_mm")
                wr_ev = ctx.sem("wr_ev")
                g_mm = ctx.sem("g_mm")
                u_mm = ctx.sem("u_mm")
                s_sil = ctx.sem("s_sil")
                va1 = ctx.sem("va1")
                p_act = ctx.sem("p_act")
                y_mm = ctx.sem("y_mm")
                y_ev = ctx.sem("y_ev")
                sl_sem = ctx.sem("sl_sem")
                go_sem = [ctx.sem("go_sem0"), ctx.sem("go_sem1")]

                def pe_sync(sy):
                    sy.wait_ge(mc_sem, 48)
                    for e in range(E0):
                        if e >= 2:
                            sy.wait_ge(y_mm, (e - 1) * 16)
                        p = e % 2
                        for c in range(ND):
                            sy.dma_start(out=gT_w[p][c][:, :],
                                         in_=gpbf[e, c * 128:(c + 1) * 128, :]
                                         ).then_inc(we_sem[p], 16)
                        for c in range(ND):
                            sy.dma_start(out=uT_w[p][c][:, :],
                                         in_=upbf[e, c * 128:(c + 1) * 128, :]
                                         ).then_inc(we_sem[p], 16)
                        for c in range(4):
                            sy.dma_start(out=dT_w[p][c][:, :],
                                         in_=dpbf[e, c * 128:(c + 1) * 128, :]
                                         ).then_inc(we_sem[p], 16)
                blk.sync(pe_sync)

                def pe_gpsimd_rows(g):
                    g.wait_ge(cbT_sem, NTT)
                    for e in range(E0):
                        if e >= 2:
                            g.wait_ge(wr_mm, (e - 1) * 4)
                        g.dma_start(out=cstage[e % 2][:, :], in_=cmbT[e:e + 1, :]
                                    ).then_inc(cs_sem[e % 2], 16)
                blk.gpsimd(pe_gpsimd_rows)

                def pe_vector0(v):
                    for c in range(ND):
                        v.memset(yacc[c][:, :], 0.0)
                    v.drain()
                    v.engine_nop().then_inc(y_ev, 1)
                blk.vector(pe_vector0)

                def pe_tensor(t):
                    t.wait_ge(y_ev, 1)
                    for e in range(E0):
                        p = e % 2
                        t.wait_ge(we_sem[p], 16 * NW * (e // 2 + 1))
                        t.wait_ge(cs_sem[p], 16 * (e // 2 + 1))
                        for tc in range(NTC):
                            w0 = e * 4 + tc
                            if w0 >= 1:
                                t.wait_ge(wr_ev, w0)
                            if e >= 1 and tc == 0:
                                t.wait_ge(p_act, 16 * e)
                            t.matmul(ps_wr[:, :], ones1_bf[:, :],
                                     cstage[p][0:1, tc * 512:(tc + 1) * 512],
                                     start=True, stop=True).then_inc(wr_mm, 1)
                        for th in range(2):
                            tcs = (2 * th, 2 * th + 1)
                            for ic in range(4):
                                gs = (e * 2 + th) * 4 + ic
                                if gs >= 1:
                                    t.wait_ge(s_sil, gs)
                                ins = None
                                for d in range(ND):
                                    for j, tc in enumerate(tcs):
                                        ins = t.matmul(
                                            ps_g[j][:, :],
                                            gT_w[p][d][:, ic * 128:(ic + 1) * 128],
                                            xn3[d][:, tc * 512:(tc + 1) * 512],
                                            start=(d == 0), stop=(d == ND - 1))
                                ins.then_inc(g_mm, 1)
                                if gs >= 1:
                                    t.wait_ge(va1, 2 * gs)
                                ins = None
                                for d in range(ND):
                                    for j, tc in enumerate(tcs):
                                        ins = t.matmul(
                                            ps_u[j][:, :],
                                            uT_w[p][d][:, ic * 128:(ic + 1) * 128],
                                            xn3[d][:, tc * 512:(tc + 1) * 512],
                                            start=(d == 0), stop=(d == ND - 1))
                                ins.then_inc(u_mm, 1)
                            t.wait_ge(p_act, 2 * ((e * 2 + th) * 4 + 4))
                            for hc in range(ND):
                                ds = (e * 2 + th) * 8 + hc
                                if ds >= 2:
                                    t.wait_ge(y_ev, 2 * ds - 1)
                                if ds >= 1:
                                    t.wait_ge(y_ev, 2 * ds)
                                ins = None
                                for icd in range(4):
                                    for j, tc in enumerate(tcs):
                                        ins = t.matmul(
                                            ps_d[(2 * ds + j) % 3][:, :],
                                            dT_w[p][icd][:, hc * 128:(hc + 1) * 128],
                                            act_t[icd][:, tc * 512:(tc + 1) * 512],
                                            start=(icd == 0), stop=(icd == 3))
                                ins.then_inc(y_mm, 1)
                blk.tensor(pe_tensor)

                def pe_scalar(s):
                    for e in range(E0):
                        for tc in range(NTC):
                            s.wait_ge(wr_mm, e * 4 + tc + 1)
                            s.activation(out=wrep[:, tc * 512:(tc + 1) * 512],
                                         in_=ps_wr[:, :],
                                         func=AF.Copy).then_inc(wr_ev, 1)
                        for th in range(2):
                            for ic in range(4):
                                gs = (e * 2 + th) * 4 + ic
                                s.wait_ge(g_mm, gs + 1)
                                if gs >= 2:
                                    s.wait_ge(va1, 2 * gs - 2)
                                ins = None
                                for j in range(2):
                                    ins = s.activation(
                                        out=sg2[gs % 2][:, j * 512:(j + 1) * 512],
                                        in_=ps_g[j][:, :], func=AF.Silu)
                                ins.then_inc(s_sil, 1)
                blk.scalar(pe_scalar)

                def pe_vector(v):
                    for e in range(E0):
                        for th in range(2):
                            tcs = (2 * th, 2 * th + 1)
                            for ic in range(4):
                                gs = (e * 2 + th) * 4 + ic
                                v.wait_ge(s_sil, gs + 1)
                                v.wait_ge(u_mm, gs + 1)
                                if e >= 1 and th == 0 and ic == 0:
                                    v.wait_ge(y_mm, 16 * e)
                                v.wait_ge(wr_ev, 4 * (e + 1))
                                for j, tc in enumerate(tcs):
                                    v.tensor_mul(act_t[ic][:, tc * 512:(tc + 1) * 512],
                                                 sg2[gs % 2][:, j * 512:(j + 1) * 512],
                                                 ps_u[j][:, :]).then_inc(va1, 1)
                                v.drain()
                                for tc in tcs:
                                    v.tensor_mul(
                                        act_t[ic][:, tc * 512:(tc + 1) * 512],
                                        act_t[ic][:, tc * 512:(tc + 1) * 512],
                                        wrep[:, tc * 512:(tc + 1) * 512]
                                    ).then_inc(p_act, 1)
                            for hc in range(ND):
                                ds = (e * 2 + th) * 8 + hc
                                v.wait_ge(y_mm, ds + 1)
                                for j, tc in enumerate(tcs):
                                    v.scalar_tensor_tensor(
                                        yacc[hc][:, tc * 512:(tc + 1) * 512],
                                        ps_d[(2 * ds + j) % 3][:, :], 1.0,
                                        yacc[hc][:, tc * 512:(tc + 1) * 512],
                                        OP.mult, OP.add).then_inc(y_ev, 1)
                            v.drain()
                blk.vector(pe_vector)


                def pe_scalar2(s):
                    s.wait_ge(y_ev, 1 + E0 * 32)
                    s.drain()
                    for hc in range(ND):
                        if hc >= 2:
                            s.wait_ge(go_sem[hc % 2], 16 * (hc // 2))
                        s.activation(out=gob[hc % 2][:, :], in_=yacc[hc][:, :],
                                     func=AF.Silu).then_inc(sl_sem, 1)
                blk.scalar(pe_scalar2)

                def pe_gpsimd2(g):
                    for hc in range(ND):
                        g.wait_ge(sl_sem, hc + 1)
                        g.dma_start(out=g_out[hc * 128:(hc + 1) * 128, :],
                                    in_=gob[hc % 2][:, :]).then_inc(go_sem[hc % 2], 16)
                    g.wait_ge(go_sem[0], 16 * (ND // 2))
                    g.wait_ge(go_sem[1], 16 * (ND // 2))
                blk.gpsimd(pe_gpsimd2)

                ctx.barrier()

    return nc, dbg_tensors



def build_launch2(debug=(), upto='Z'):
    nc = bass.Bass("TRN2", target_bir_lowering=False)
    dbg = set(debug)

    xd = nc.dram_tensor("x", [TPC, D], F32, kind="ExternalInput").ap()
    qbf = nc.dram_tensor("qbf", [D, TPC], BF16, kind="ExternalInput").ap()
    apbf = nc.dram_tensor("apbf", [D, TPC], BF16, kind="ExternalInput").ap()
    gbf = nc.dram_tensor("gbf", [D, TPC], BF16, kind="ExternalInput").ap()
    ckvall = nc.dram_tensor("ckvall", [4 * DH0, H0 * DH0], F32, kind="ExternalInput").ap()
    mcur = nc.dram_tensor("mcur", [DH0, 4], F32, kind="ExternalInput").ap()
    posr = nc.dram_tensor("posr", [1, TPC], F32, kind="ExternalInput").ap()
    wo = nc.dram_tensor("wo", [D, D], F32, kind="ExternalInput").ap()
    win = nc.dram_tensor("win", [D, D_IN_PROJ], F32, kind="ExternalInput").ap()
    simm = nc.dram_tensor("simm", [D, E1], F32, kind="ExternalInput").ap()
    gatesd = nc.dram_tensor("gatesd", [E1, 1], F32, kind="ExternalInput").ap()
    qkp = nc.dram_tensor("qkp", [E1, D, 3 * DV1], F32, kind="ExternalInput").ap()

    x1_out = nc.dram_tensor("x1_out", [D, TPC], F32, kind="ExternalOutput").ap()
    szo = nc.dram_tensor("siluz_out", [D_INNER, TPC], BF16, kind="ExternalOutput").ap()
    xbco = nc.dram_tensor("xbc_out", [CONV_DIM, TPC], BF16, kind="ExternalOutput").ap()
    dto = nc.dram_tensor("dtraw_out", [NH1, TPC], F32, kind="ExternalOutput").ap()
    mrwo = nc.dram_tensor("mrw_out", [E1, TPC], F32, kind="ExternalOutput").ap()
    qkvo = nc.dram_tensor("qkv_out", [TPC, 3 * DV1], BF16, kind="ExternalOutput").ap()

    wobf = nc.dram_tensor("wobf", [D, D], BF16).ap()
    winbf = nc.dram_tensor("winbf", [D, D_IN_PROJ], BF16).ap()
    qkpbf = nc.dram_tensor("qkpbf", [E1, D, 3 * DV1], BF16).ap()

    dbg_tensors = {}
    def dbg_out(name, shape, dtype=F32):
        if name in dbg:
            dbg_tensors[name] = nc.dram_tensor("dbg_" + name, shape, dtype,
                                               kind="ExternalOutput").ap()
            return dbg_tensors[name]
        return None

    with ExitStack() as st:
        ctx = Ctx(nc, st, const_vals=(1e-5, 1e-20, -1e30, -1e9, 1e-12))
        blk = ctx.blk

        ones_bf = ctx.sb("ones_bf", [128, 1], BF16)
        ones1_f = ctx.sb("ones1_f", [1, 128], F32)
        iv = ctx.sb("iv", [128, 128], F32)
        ident_f = ctx.sb("ident_f", [128, 128], F32)
        mcur_t = ctx.sb("mcur_t", [DH0, 4], F32)
        gates_t = ctx.sb("gates_t", [E1, 1], F32)

        wc_sem = ctx.sem("wc_sem")
        qc_sem = ctx.sem("qc_sem")
        sm_sem = ctx.sem("sm_sem")
        k_sem = ctx.sem("k_sem")

        def p0_gpsimd(g):
            g.dma_start(out=wobf[:, :], in_=wo[:, :]).then_inc(wc_sem, 16)
            g.dma_start(out=winbf[:, :], in_=win[:, :]).then_inc(wc_sem, 16)
            g.dma_start(out=qkpbf[:, :, :], in_=qkp[:, :, :]).then_inc(qc_sem, 16)
            g.iota(iv[:, :], pattern=[[1, 128]], base=0, channel_multiplier=-1,
                   allow_small_or_imprecise_dtypes=True).then_inc(k_sem, 1)
        blk.gpsimd(p0_gpsimd)

        def p0_sync(sy):
            sy.dma_start(out=mcur_t[:, :], in_=mcur[:, :]).then_inc(sm_sem, 16)
            sy.dma_start(out=gates_t[:, :], in_=gatesd[:, :]).then_inc(sm_sem, 16)
        blk.sync(p0_sync)

        def p0_vector(v):
            ctx.emit_const_memsets(v)
            v.memset(ones_bf[:, :], 1.0)
            v.memset(ones1_f[:, :], 1.0)
            v.wait_ge(k_sem, 1)
            v.tensor_single_scalar(ident_f[:, :], iv[:, :], 0.0, OP.is_equal)
        blk.vector(p0_vector)

        ctx.barrier()

        xn1T = [ctx.sb(f"xn1T{c}", [128, TPC], BF16) for c in range(ND)]
        xnh_sem = ctx.sem("xnh_sem")

        with ExitStack() as s1:
            xf = [ctx.sb(f"xf{c}", [128, TPC], F32, s1) for c in range(ND)]
            apT = [ctx.sb(f"apT{c}", [128, TPC], BF16, s1) for c in range(ND)]
            gT = [ctx.sb(f"gT{c}", [128, TPC], BF16, s1) for c in range(ND)]
            ckvp = ctx.sb("ckvp", [64, H0 * DH0], F32, s1)
            ckvp_bf = ctx.sb("ckvp_bf", [128, (H0 // 2) * DH0], BF16, s1)
            ckvs = [ctx.sb(f"ckvs{s}", [64, H0 * DH0], F32, s1) for s in range(2)]
            invt = ctx.sb("invt", [1, TPC], F32, s1)
            scr1 = ctx.sb("scr1", [128, 512], F32, s1)
            scr2 = ctx.sb("scr2", [128, 512], F32, s1)

            ld1 = ctx.sem("ld1")
            ck_sem = ctx.sem("ck_sem")
            xr_sem = [ctx.sem("xr_sem0"), ctx.sem("xr_sem1")]
            xt1_mm = ctx.sem("xt1_mm")
            xt1_ev = ctx.sem("xt1_ev")
            cr_mm = ctx.sem("cr_mm")
            cr_ev = ctx.sem("cr_ev")
            iv_mm = ctx.sem("iv_mm")
            wo_sem = ctx.sem("wo_sem")
            wm_mm = ctx.sem("wm_mm")
            x1_ev = ctx.sem("x1_ev")
            x1o_sem = ctx.sem("x1o_sem")

            with ExitStack() as s1b:
                qT = [ctx.sb(f"qT{c}", [128, TPC], BF16, s1b) for c in range(2)]
                xrow = [ctx.sb(f"xrow{i}", [128, D], F32, s1b) for i in range(2)]
                woT = [ctx.sb(f"woT{c}", [128, D], BF16, s1b) for c in range(2)]

                ps_tx = [ctx.ps(f"ps_tx{i}", [128, 128], F32, s1b) for i in range(2)]
                ps_cr = [ctx.ps(f"ps_cr{i}", [64, 512], F32, s1b) for i in range(2)]
                ps_iv = [ctx.ps(f"ps_iv{i}", [128, 512], F32, s1b) for i in range(4)]

                qst = [ctx.sem("qst0"), ctx.sem("qst1")]
                cks = [ctx.sem("cks0"), ctx.sem("cks1")]

                def p1_sync(sy):
                    for c in range(ND):
                        sy.dma_start(out=apT[c][:, :], in_=apbf[c * 128:(c + 1) * 128, :]
                                     ).then_inc(ld1, 16)
                        sy.dma_start(out=gT[c][:, :], in_=gbf[c * 128:(c + 1) * 128, :]
                                     ).then_inc(ld1, 16)
                    sy.dma_start(out=invt[:, :], in_=posr[:, :]).then_inc(ld1, 16)
                    for s in range(4):
                        if s >= 2:
                            sy.wait_ge(ck_sem, s - 1)
                        sy.dma_start(out=ckvs[s % 2][:, :],
                                     in_=ckvall[s * DH0:(s + 1) * DH0, :]
                                     ).then_inc(cks[s % 2], 16)
                blk.sync(p1_sync)
                N_LD1 = 16 * (2 * ND)

                def p1_gpsimd(g):
                    for i in range(NTT):
                        if i >= 2:
                            g.wait_ge(xt1_mm, (i - 1) * ND)
                        g.dma_start(out=xrow[i % 2][:, :], in_=xd[i * 128:(i + 1) * 128, :]
                                    ).then_inc(xr_sem[i % 2], 16)
                blk.gpsimd(p1_gpsimd)

                def p1_tensor_tx(t):
                    for i in range(NTT):
                        t.wait_ge(xr_sem[i % 2], 16 * (i // 2 + 1))
                        for c in range(ND):
                            j = i * ND + c
                            if j >= 2:
                                t.wait_ge(xt1_ev, j - 1)
                            t.transpose(ps_tx[j % 2][:, :],
                                        xrow[i % 2][:, c * 128:(c + 1) * 128],
                                        ident_f[:, :]).then_inc(xt1_mm, 1)
                blk.tensor(p1_tensor_tx)

                def p1_vector_tx(v):
                    for i in range(NTT):
                        for c in range(ND):
                            j = i * ND + c
                            v.wait_ge(xt1_mm, j + 1)
                            v.tensor_copy(xf[c][:, i * 128:(i + 1) * 128],
                                          ps_tx[j % 2][:, :]).then_inc(xt1_ev, 1)
                blk.vector(p1_vector_tx)

                def p1b_vector(v):
                    v.wait_ge(sm_sem, 32)
                    v.memset(ckvp[:, :], 0.0)
                    v.drain()
                    for s in range(4):
                        v.wait_ge(cks[s % 2], 16 * (s // 2 + 1))
                        v.scalar_tensor_tensor(ckvp[:, :], ckvs[s % 2][:, :],
                                               mcur_t[:, s:s + 1], ckvp[:, :],
                                               OP.mult, OP.add)
                        v.drain()
                        v.engine_nop().then_inc(ck_sem, 1)
                    for h in range(H0):
                        v.tensor_copy(ckvp_bf[64 * (h % 2):64 * (h % 2) + 64,
                                              (h // 2) * DH0:(h // 2) * DH0 + DH0],
                                      ckvp[:, h * DH0:(h + 1) * DH0])
                    v.wait_ge(ld1, N_LD1 + 16)
                    v.reciprocal(invt[:, :], invt[:, :]).then_inc(ck_sem, 1)
                    v.drain()
                blk.vector(p1b_vector)

                ctx.barrier()

                def p1b_sync(sy):
                    for c in range(ND):
                        if c >= 2:
                            sy.wait_ge(cr_mm, (c - 1) * 2 * NTC)
                        sy.dma_start(out=qT[c % 2][:, :], in_=qbf[c * 128:(c + 1) * 128, :]
                                     ).then_inc(qst[c % 2], 16)
                blk.sync(p1b_sync)

                def p1b_tensor(t):
                    t.wait_ge(ck_sem, 5)
                    for tc in range(NTC):
                        t.matmul(ps_iv[tc][:, :], ones1_f[:, :],
                                 invt[:, tc * 512:(tc + 1) * 512],
                                 start=True, stop=True).then_inc(iv_mm, 1)
                    for c in range(ND):
                        t.wait_ge(qst[c % 2], 16 * (c // 2 + 1))
                        for hh in range(2):
                            h = c * 2 + hh
                            hs = slice(64 * hh, 64 * hh + 64)
                            for tc in range(NTC):
                                u = h * NTC + tc
                                if u >= 2:
                                    t.wait_ge(cr_ev, u - 1)
                                t.matmul(ps_cr[u % 2][:, :],
                                         ckvp_bf[hs, (h // 2) * DH0:(h // 2) * DH0 + DH0],
                                         qT[c % 2][hs, tc * 512:(tc + 1) * 512],
                                         start=True, stop=True).then_inc(cr_mm, 1)
                blk.tensor(p1b_tensor)

                def p1b_vector2(v):
                    v.wait_ge(iv_mm, NTC)
                    for h in range(H0):
                        hs = slice(64 * (h % 2), 64 * (h % 2) + 64)
                        for tc in range(NTC):
                            u = h * NTC + tc
                            sl = slice(tc * 512, (tc + 1) * 512)
                            v.wait_ge(cr_mm, u + 1)
                            v.scalar_tensor_tensor(scr1[hs, :], ps_cr[u % 2][:, :], 1.0,
                                                   apT[h // 2][hs, sl], OP.mult, OP.add)
                            v.drain()
                            v.tensor_mul(apT[h // 2][hs, sl], scr1[hs, :],
                                         ps_iv[tc][hs, :]).then_inc(cr_ev, 1)
                            v.drain()
                blk.vector(p1b_vector2)

                ctx.barrier()

                wos = [ctx.sem("wos0"), ctx.sem("wos1")]

                def p1c_sync(sy):
                    sy.wait_ge(wc_sem, 32)
                    for oc in range(ND):
                        if oc >= 2:
                            sy.wait_ge(wm_mm, (oc - 1) * NTC)
                        sy.dma_start(out=woT[oc % 2][:, :],
                                     in_=wobf[oc * 128:(oc + 1) * 128, :]
                                     ).then_inc(wos[oc % 2], 16)
                blk.sync(p1c_sync)

                def p1c_tensor(t):
                    for oc in range(ND):
                        t.wait_ge(wos[oc % 2], 16 * (oc // 2 + 1))
                        for tc in range(NTC):
                            u = oc * NTC + tc
                            if u >= 2:
                                t.wait_ge(x1_ev, 2 * (u - 1))
                            for d in range(ND):
                                ins = t.matmul(ps_iv[u % 2][:, :],
                                               woT[oc % 2][:, d * 128:(d + 1) * 128],
                                               apT[d][:, tc * 512:(tc + 1) * 512],
                                               start=(d == 0), stop=(d == ND - 1))
                            ins.then_inc(wm_mm, 1)
                blk.tensor(p1c_tensor)

                def p1c_vector(v):
                    for oc in range(ND):
                        for tc in range(NTC):
                            u = oc * NTC + tc
                            sl = slice(tc * 512, (tc + 1) * 512)
                            v.wait_ge(wm_mm, u + 1)
                            v.tensor_mul(scr2[:, :], ps_iv[u % 2][:, :], gT[oc][:, sl]
                                         ).then_inc(x1_ev, 1)
                            v.drain()
                            v.tensor_add(xf[oc][:, sl], xf[oc][:, sl], scr2[:, :]
                                         ).then_inc(x1_ev, 1)
                            v.drain()
                blk.vector(p1c_vector)

                def p1c_gpsimd(g):
                    for oc in range(ND):
                        g.wait_ge(x1_ev, 2 * (oc + 1) * NTC)
                        g.dma_start(out=x1_out[oc * 128:(oc + 1) * 128, :],
                                    in_=xf[oc][:, :]).then_inc(x1o_sem, 16)
                    g.wait_ge(x1o_sem, 16 * ND)
                blk.gpsimd(p1c_gpsimd)

                ctx.barrier()

            with ExitStack() as s2:
                sqb = [ctx.sb(f"sqb{i}", [128, 512], BF16, s2) for i in range(2)]
                xb = [ctx.sb(f"xb{i}", [128, 512], BF16, s2) for i in range(2)]
                sr = ctx.sb("sr", [1, 512], F32, s2)
                qr = ctx.sb("qr", [1, 512], F32, s2)
                mn = ctx.sb("mn", [1, 512], F32, s2)
                vr = ctx.sb("vr", [1, 512], F32, s2)
                rs = ctx.sb("rs", [1, 512], F32, s2)
                mr = ctx.sb("mr", [1, 512], F32, s2)
                tscr = ctx.sb("tscr", [128, 512], F32, s2)

                ps_s = [ctx.ps(f"ps_s{i}", [1, 512], F32, s2) for i in range(2)]
                ps_q = [ctx.ps(f"ps_q{i}", [1, 512], F32, s2) for i in range(2)]
                ps_r = [ctx.ps(f"ps_r{i}", [128, 512], F32, s2) for i in range(2)]
                ps_m = [ctx.ps(f"ps_m{i}", [128, 512], F32, s2) for i in range(2)]

                sq_sem = ctx.sem("sq_sem")
                tcc_sem = ctx.sem("tcc_sem")
                mm2_sem = ctx.sem("mm2_sem")
                row2_sem = ctx.sem("row2_sem")
                rst_sem = ctx.sem("rst_sem")
                bc2_sem = ctx.sem("bc2_sem")
                xn_sem = ctx.sem("xn_sem")

                def p2_scalar(s):
                    for tc in range(NTC):
                        sl = slice(tc * 512, (tc + 1) * 512)
                        for c in range(ND):
                            u = tc * ND + c
                            if u >= 2:
                                tcp, cp = divmod(u - 2, ND)
                                if cp < ND - 1:
                                    s.wait_ge(tcc_sem, tcp * (ND - 1) + cp + 1)
                                else:
                                    s.wait_ge(mm2_sem, tcp + 1)
                            s.activation(out=sqb[c % 2][:, :], in_=xf[c][:, sl],
                                         func=AF.Square)
                            s.drain()
                            s.activation(out=xb[c % 2][:, :], in_=xf[c][:, sl],
                                         func=AF.Copy).then_inc(sq_sem, 1)
                        s.wait_ge(row2_sem, 3 * tc + 2)
                        s.activation(out=rs[:, :], in_=vr[:, :], func=AF.Ln, bias=1e-5)
                        s.drain()
                        s.activation(out=rs[:, :], in_=rs[:, :], func=AF.Exp,
                                     scale=-0.5).then_inc(rst_sem, 1)
                        s.drain()
                blk.scalar(p2_scalar)

                def p2_tensor(t):
                    for tc in range(NTC):
                        for c in range(ND):
                            u = tc * ND + c
                            t.wait_ge(sq_sem, u + 1)
                            t.matmul(ps_q[tc % 2][:, :], ones_bf[:, :], sqb[c % 2][:, :],
                                     start=(c == 0), stop=(c == ND - 1))
                            ins = t.matmul(ps_s[tc % 2][:, :], ones_bf[:, :], xb[c % 2][:, :],
                                           start=(c == 0), stop=(c == ND - 1))
                            if c < ND - 1:
                                ins.then_inc(tcc_sem, 1)
                            else:
                                ins.then_inc(mm2_sem, 1)
                        t.wait_ge(row2_sem, 3 * tc + 3)
                        if tc >= 2:
                            t.wait_ge(xn_sem, (tc - 1) * ND)
                        t.matmul(ps_r[tc % 2][:, :], ones1_f[:, :], rs[:, :],
                                 start=True, stop=True)
                        t.matmul(ps_m[tc % 2][:, :], ones1_f[:, :], mr[:, :],
                                 start=True, stop=True).then_inc(bc2_sem, 1)
                blk.tensor(p2_tensor)

                def p2_vector(v):
                    for tc in range(NTC):
                        sl = slice(tc * 512, (tc + 1) * 512)
                        v.wait_ge(mm2_sem, tc + 1)
                        v.tensor_copy(sr[:, :], ps_s[tc % 2][:, :])
                        v.tensor_copy(qr[:, :], ps_q[tc % 2][:, :]).then_inc(row2_sem, 1)
                        v.drain()
                        v.tensor_scalar_mul(mn[:, :], sr[:, :], 1.0 / D)
                        v.tensor_scalar_mul(vr[:, :], qr[:, :], 1.0 / D)
                        v.drain()
                        v.tensor_mul(sr[:, :], mn[:, :], mn[:, :])
                        v.drain()
                        v.tensor_sub(vr[:, :], vr[:, :], sr[:, :]).then_inc(row2_sem, 1)
                        v.wait_ge(rst_sem, tc + 1)
                        v.drain()
                        v.tensor_mul(mr[:, :], mn[:, :], rs[:, :]).then_inc(row2_sem, 1)
                        v.drain()
                        v.wait_ge(bc2_sem, tc + 1)
                        for c in range(ND):
                            v.tensor_mul(tscr[:, :], xf[c][:, sl], ps_r[tc % 2][:, :])
                            v.drain()
                            v.tensor_sub(xn1T[c][:, sl], tscr[:, :],
                                         ps_m[tc % 2][:, :]).then_inc(xn_sem, 1)
                            v.drain()
                blk.vector(p2_vector)

                ctx.barrier()

        with ExitStack() as s3:
            wbuf = [[ctx.sb(f"wbuf{p}_{d}", [128, 128], BF16, s3) for d in range(ND)]
                    for p in range(2)]
            zb = [ctx.sb(f"zb{i}", [128, TPC], BF16, s3) for i in range(2)]
            dtt = ctx.sb("dtt", [NH1, TPC], F32, s3)

            ps_w3 = [ctx.ps(f"ps_w3{i}", [128, 512], F32, s3) for i in range(8)]

            wi_sem = [ctx.sem("wi_sem0"), ctx.sem("wi_sem1")]
            w3_mm = ctx.sem("w3_mm")
            w3_ev = ctx.sem("w3_ev")
            w3_st = [ctx.sem("w3_st0"), ctx.sem("w3_st1")]

            NOC = D_IN_PROJ // 128 + 1

            def ocrows(oc):
                return 32 if oc == NOC - 1 else 128

            def p3_sync(sy):
                sy.wait_ge(wc_sem, 32)
                for oc in range(NOC):
                    p = oc % 2
                    if oc >= 2:
                        sy.wait_ge(w3_mm, (oc - 1) * NTC)
                    for d in range(ND):
                        sy.dma_start(
                            out=wbuf[p][d][:, 0:ocrows(oc)],
                            in_=winbf[d * 128:(d + 1) * 128,
                                      oc * 128:oc * 128 + ocrows(oc)]
                        ).then_inc(wi_sem[p], 16)
            blk.sync(p3_sync)

            def p3_tensor(t):
                for oc in range(NOC):
                    p = oc % 2
                    t.wait_ge(wi_sem[p], 16 * ND * (oc // 2 + 1))
                    for tc in range(NTC):
                        u = oc * NTC + tc
                        if u >= 8:
                            t.wait_ge(w3_ev, u - 7)
                    ins = None
                    for d in range(ND):
                        for tc in range(NTC):
                            u = oc * NTC + tc
                            ins = t.matmul(ps_w3[u % 8][0:ocrows(oc), :],
                                           wbuf[p][d][:, 0:ocrows(oc)],
                                           xn1T[d][:, tc * 512:(tc + 1) * 512],
                                           start=(d == 0), stop=(d == ND - 1))
                            if d == ND - 1:
                                ins.then_inc(w3_mm, 1)
            blk.tensor(p3_tensor)

            def p3_scalar(s):
                for oc in range(NOC):
                    for tc in range(NTC):
                        u = oc * NTC + tc
                        s.wait_ge(w3_mm, u + 1)
                        if oc < 16:
                            if u >= 2:
                                s.wait_ge(w3_st[u % 2], 16 * ((u - 2) // 2 + 1))
                            s.activation(out=zb[u % 2][:, tc * 512:(tc + 1) * 512],
                                         in_=ps_w3[u % 8][:, :],
                                         func=AF.Silu).then_inc(w3_ev, 1)
                        elif oc < NOC - 1:
                            if u >= 2:
                                s.wait_ge(w3_st[u % 2], 16 * ((u - 2) // 2 + 1))
                            s.activation(out=zb[u % 2][:, tc * 512:(tc + 1) * 512],
                                         in_=ps_w3[u % 8][:, :],
                                         func=AF.Copy).then_inc(w3_ev, 1)
                        else:
                            s.wait_ge(w3_ev, u)
                            s.activation(out=dtt[:, tc * 512:(tc + 1) * 512],
                                         in_=ps_w3[u % 8][0:NH1, :],
                                         func=AF.Copy).then_inc(w3_ev, 1)
            blk.scalar(p3_scalar)

            def p3_gpsimd(g):
                for oc in range(NOC):
                    for tc in range(NTC):
                        u = oc * NTC + tc
                        if oc < NOC - 1:
                            g.wait_ge(w3_ev, u + 1)
                            dst = szo if oc < 16 else xbco
                            row = oc * 128 if oc < 16 else (oc - 16) * 128
                            g.dma_start(out=dst[row:row + 128, tc * 512:(tc + 1) * 512],
                                        in_=zb[u % 2][:, tc * 512:(tc + 1) * 512]
                                        ).then_inc(w3_st[u % 2], 16)
                g.wait_ge(w3_ev, NOC * NTC)
                g.dma_start(out=dto[:, :], in_=dtt[:, :]).then_inc(w3_st[0], 16)
                g.wait_ge(w3_st[0], 16 * ((NOC - 1) * NTC // 2 + ((NOC - 1) * NTC) % 2 + 1))
                g.wait_ge(w3_st[1], 16 * ((NOC - 1) * NTC // 2))
            blk.gpsimd(p3_gpsimd)

            ctx.barrier()

        m_tok = [ctx.sb(f"m_tok{i}", [128, E1], F32) for i in range(NTT)]
        with ExitStack() as s4:
            simt = [ctx.sb(f"simt{c}", [128, E1], F32, s4) for c in range(ND)]
            simn = [ctx.sb(f"simn{c}", [128, E1], BF16, s4) for c in range(ND)]
            sq8 = [ctx.sb(f"sq8_{c}", [128, 512], BF16, s4) for c in range(ND)]
            snr = ctx.sb("snr", [1, E1], F32, s4)
            grow = ctx.sb("grow", [1, E1], F32, s4)
            grep = ctx.sb("grep", [128, E1], F32, s4)
            rncol = [ctx.sb(f"rncol{i}", [128, 1], F32, s4) for i in range(NTT)]
            lgT = ctx.sb("lgT", [E1, TPC], F32, s4)
            lt = [ctx.sb(f"lt{i}", [128, E1], F32, s4) for i in range(NTT)]
            ga = [ctx.sb(f"ga{i}", [128, E1], F32, s4) for i in range(NTT)]
            eqt = [ctx.sb(f"eqt{i}", [128, E1], F32, s4) for i in range(NTT)]
            mxt = [ctx.sb(f"mxt{i}", [128, 1], F32, s4) for i in range(NTT)]
            smt = [ctx.sb(f"smt{i}", [128, 1], F32, s4) for i in range(NTT)]
            mrw = [ctx.sb(f"mrw{i}", [128, E1], F32, s4) for i in range(NTT)]
            srow4 = ctx.sb("srow4", [1, TPC], F32, s4)

            ps_a = [ctx.ps(f"ps_a{i}", [128, 512], F32, s4) for i in range(4)]
            ps_c = [ctx.ps(f"ps_c{i}", [128, 16], F32, s4) for i in range(2)]
            ps_d = [ctx.ps(f"ps_d{i}", [E1, 128], F32, s4) for i in range(2)]

            steps = []

            def s_load(sy, dma):
                for c in range(ND):
                    dma(simt[c][:, :], simm[c * 128:(c + 1) * 128, :])
            steps.append(("sync", s_load))

            def s_simsq(s, dma):
                for c in range(ND):
                    s.activation(out=simn[c][:, :], in_=simt[c][:, :], func=AF.Square)
            steps.append(("scalar", s_simsq))

            def s_colsum(t, dma):
                for c in range(ND):
                    t.matmul(ps_c[0][0:1, 0:E1], ones_bf[:, :], simn[c][:, :],
                             start=(c == 0), stop=(c == ND - 1))
                t.transpose(ps_c[1][0:1, 0:E1], gates_t[:, :], ident_f[0:E1, 0:E1])
                
            steps.append(("tensor", s_colsum))

            def s_snrow(v, dma):
                v.tensor_copy(snr[:, :], ps_c[0][0:1, 0:E1])
                v.tensor_copy(grow[:, :], ps_c[1][0:1, 0:E1])
            steps.append(("vector", s_snrow))

            def s_snmath(s, dma):
                s.activation(out=snr[:, :], in_=snr[:, :], func=AF.Ln, bias=1e-20)
                s.drain()
                s.activation(out=snr[:, :], in_=snr[:, :], func=AF.Exp, scale=0.5)
                s.drain()
                s.activation(out=grow[:, :], in_=grow[:, :], func=AF.Sigmoid)
            steps.append(("scalar", s_snmath))

            def s_snfin(v, dma):
                v.tensor_scalar_max(snr[:, :], snr[:, :], 1e-12)
                v.drain()
                v.reciprocal(snr[:, :], snr[:, :])
            steps.append(("vector", s_snfin))

            def s_snbc(t, dma):
                t.matmul(ps_c[0][:, 0:E1], ones1_f[:, :], snr[:, :],
                         start=True, stop=True)
                t.matmul(ps_c[1][:, 0:E1], ones1_f[:, :], grow[:, :],
                         start=True, stop=True)
            steps.append(("tensor", s_snbc))

            def s_simn(v, dma):
                v.tensor_copy(grep[:, :], ps_c[1][:, 0:E1])
                for c in range(ND):
                    v.tensor_mul(simn[c][:, :], simt[c][:, :], ps_c[0][:, 0:E1])
            steps.append(("vector", s_simn))

            for tc in range(NTC):
                def s_sq(s, dma, tc=tc):
                    for c in range(ND):
                        s.activation(out=sq8[c][:, :],
                                     in_=xn1T[c][:, tc * 512:(tc + 1) * 512],
                                     func=AF.Square)
                steps.append(("scalar", s_sq))

                def s_sum(t, dma, tc=tc):
                    for c in range(ND):
                        t.matmul(ps_a[tc % 4][0:1, :], ones_bf[:, :], sq8[c][:, :],
                                 start=(c == 0), stop=(c == ND - 1))
                steps.append(("tensor", s_sum))

                def s_ev(v, dma, tc=tc):
                    v.tensor_copy(srow4[:, tc * 512:(tc + 1) * 512], ps_a[tc % 4][0:1, :])
                steps.append(("vector", s_ev))

            for tc in range(NTC):
                def s_lg(t, dma, tc=tc):
                    for d in range(ND):
                        t.matmul(ps_a[tc % 4][0:E1, :], simn[d][:, :],
                                 xn1T[d][:, tc * 512:(tc + 1) * 512],
                                 start=(d == 0), stop=(d == ND - 1))
                steps.append(("tensor", s_lg))

                def s_lgev(v, dma, tc=tc):
                    v.tensor_copy(lgT[:, tc * 512:(tc + 1) * 512], ps_a[tc % 4][0:E1, :])
                steps.append(("vector", s_lgev))

            def s_tr(t, dma):
                for tt in range(NTT):
                    t.transpose(ps_c[tt % 2][:, 0:1], srow4[:, tt * 128:(tt + 1) * 128],
                                ident_f[:, :])
                    t.transpose(ps_c[2 + tt % 2][:, 0:E1],
                                lgT[:, tt * 128:(tt + 1) * 128],
                                ident_f[0:E1, 0:E1])
                    if tt % 2 == 1 or tt == NTT - 1:
                        pass
            for tt in range(NTT):
                def s_tr1(t, dma, tt=tt):
                    t.transpose(ps_c[tt % 2][:, 0:1], srow4[:, tt * 128:(tt + 1) * 128],
                                ident_f[0:1, 0:1])
                    t.transpose(ps_c[tt % 2][:, 1:1 + E1],
                                lgT[:, tt * 128:(tt + 1) * 128],
                                ident_f[0:E1, 0:E1])
                steps.append(("tensor", s_tr1))

                def s_tr1e(v, dma, tt=tt):
                    v.tensor_copy(rncol[tt][:, :], ps_c[tt % 2][:, 0:1])
                    v.tensor_copy(lt[tt][:, :], ps_c[tt % 2][:, 1:1 + E1])
                steps.append(("vector", s_tr1e))

            def s_rn1(s, dma):
                for tt in range(NTT):
                    s.activation(out=rncol[tt][:, :], in_=rncol[tt][:, :],
                                 func=AF.Ln, bias=1e-20)
                s.drain()
                for tt in range(NTT):
                    s.activation(out=rncol[tt][:, :], in_=rncol[tt][:, :],
                                 func=AF.Exp, scale=0.5)
            steps.append(("scalar", s_rn1))

            def s_rn2(v, dma):
                for tt in range(NTT):
                    v.tensor_scalar_max(rncol[tt][:, :], rncol[tt][:, :], 1e-12)
                v.drain()
                for tt in range(NTT):
                    v.reciprocal(rncol[tt][:, :], rncol[tt][:, :])
            steps.append(("vector", s_rn2))

            def s_route1(v, dma):
                for tt in range(NTT):
                    v.tensor_single_scalar(lt[tt][:, :], lt[tt][:, :], rncol[tt][:, :],
                                           OP.mult)
                v.drain()
                for tt in range(NTT):
                    v.tensor_sub(lt[tt][:, :], lt[tt][:, :], grep[:, :])
                v.drain()
                for tt in range(NTT):
                    v.tensor_relu(ga[tt][:, :], lt[tt][:, :])
                v.drain()
                for tt in range(NTT):
                    v.tensor_single_scalar(m_tok[tt][:, :], ga[tt][:, :], 0.0, OP.is_gt)
                v.drain()
                for tt in range(NTT):
                    v.reduce_max(mxt[tt][:, :], lt[tt][:, :], axis=AX.X)
                v.drain()
                for tt in range(NTT):
                    v.tensor_single_scalar(eqt[tt][:, :], lt[tt][:, :], mxt[tt][:, :],
                                           OP.is_equal)
                v.drain()
                for tt in range(NTT):
                    v.tensor_scalar_mul(eqt[tt][:, :], eqt[tt][:, :], -1e30)
                v.drain()
                for tt in range(NTT):
                    v.tensor_add(eqt[tt][:, :], eqt[tt][:, :], lt[tt][:, :])
                v.drain()
                for tt in range(NTT):
                    v.reduce_max(mxt[tt][:, :], eqt[tt][:, :], axis=AX.X)
                v.drain()
                for tt in range(NTT):
                    v.tensor_single_scalar(eqt[tt][:, :], lt[tt][:, :], mxt[tt][:, :],
                                           OP.is_ge)
                v.drain()
                for tt in range(NTT):
                    v.reduce_sum(smt[tt][:, :], m_tok[tt][:, :], axis=AX.X)
                v.drain()
                for tt in range(NTT):
                    v.tensor_single_scalar(smt[tt][:, :], smt[tt][:, :], 0.0, OP.is_equal)
                v.drain()
                for tt in range(NTT):
                    v.tensor_single_scalar(eqt[tt][:, :], eqt[tt][:, :], smt[tt][:, :],
                                           OP.mult)
                v.drain()
                for tt in range(NTT):
                    v.tensor_max(m_tok[tt][:, :], m_tok[tt][:, :], eqt[tt][:, :])
                v.drain()
                for tt in range(NTT):
                    v.tensor_scalar(eqt[tt][:, :], m_tok[tt][:, :], 1.0, 1e9,
                                    OP.subtract, OP.mult)
                v.drain()
                for tt in range(NTT):
                    v.tensor_mul(ga[tt][:, :], ga[tt][:, :], m_tok[tt][:, :])
                v.drain()
                for tt in range(NTT):
                    v.tensor_add(ga[tt][:, :], ga[tt][:, :], eqt[tt][:, :])
                v.drain()
                for tt in range(NTT):
                    v.reduce_max(mxt[tt][:, :], ga[tt][:, :], axis=AX.X)
                v.drain()
                for tt in range(NTT):
                    v.tensor_scalar_mul(mxt[tt][:, :], mxt[tt][:, :], -1.0)
            steps.append(("vector", s_route1))

            def s_exp(s, dma):
                for tt in range(NTT):
                    s.activation(out=mrw[tt][:, :], in_=ga[tt][:, :], func=AF.Exp,
                                 bias=mxt[tt][:, :])
            steps.append(("scalar", s_exp))

            def s_route2(v, dma):
                for tt in range(NTT):
                    v.reduce_sum(smt[tt][:, :], mrw[tt][:, :], axis=AX.X)
                v.drain()
                for tt in range(NTT):
                    v.reciprocal(smt[tt][:, :], smt[tt][:, :])
                v.drain()
                for tt in range(NTT):
                    v.tensor_single_scalar(mrw[tt][:, :], mrw[tt][:, :], smt[tt][:, :],
                                           OP.mult)
                v.drain()
                for tt in range(NTT):
                    v.tensor_mul(mrw[tt][:, :], mrw[tt][:, :], m_tok[tt][:, :])
            steps.append(("vector", s_route2))

            for tt in range(NTT):
                def s_trb(t, dma, tt=tt):
                    t.transpose(ps_d[tt % 2][:, :], mrw[tt][:, :], ident_f[:, :])
                steps.append(("tensor", s_trb))

                def s_trbe(v, dma, tt=tt):
                    v.tensor_copy(lgT[:, tt * 128:(tt + 1) * 128], ps_d[tt % 2][:, :])
                steps.append(("vector", s_trbe))

            def s_mrwout(g, dma):
                dma(mrwo[:, :], lgT[:, :])
            steps.append(("gpsimd", s_mrwout))

            ctx.seq("p4", steps)
            ctx.barrier()

        with ExitStack() as s5:
            wq8 = [[ctx.sb(f"wq8_{e}_{d}", [128, 3 * DV1], BF16, s5) for d in range(ND)]
                   for e in range(E1)]
            qacc = [ctx.sb(f"qacc{i}", [128, 3 * DV1], BF16, s5) for i in range(NTT)]
            ps5 = [ctx.ps(f"ps5_{i}", [128, 3 * DV1], F32, s5) for i in range(8)]

            w5 = ctx.sem("w5")
            in5 = ctx.sem("in5")
            mm5 = ctx.sem("mm5")
            ev5 = ctx.sem("ev5")
            st5 = [ctx.sem("st5_0"), ctx.sem("st5_1")]

            def p5_sync(sy):
                sy.wait_ge(qc_sem, 16)
                for e in range(E1):
                    for d in range(ND):
                        sy.dma_start(out=wq8[e][d][:, :],
                                     in_=qkpbf[e, d * 128:(d + 1) * 128, :]
                                     ).then_inc(w5, 16)
            blk.sync(p5_sync)

            def p5_vector0(v):
                for tt in range(NTT):
                    v.memset(qacc[tt][:, :], 0.0)
                v.drain()
                v.engine_nop().then_inc(in5, 1)
            blk.vector(p5_vector0)

            def p5_tensor(t):
                t.wait_ge(w5, 16 * E1 * ND)
                t.wait_ge(in5, 1)
                for tt in range(NTT):
                    for eg in range(2):
                        g = tt * 2 + eg
                        if g >= 2:
                            t.wait_ge(ev5, 4 * (g - 1))
                        for d in range(ND):
                            for e4 in range(4):
                                e = eg * 4 + e4
                                ins = t.matmul(ps5[(g % 2) * 4 + e4][:, :],
                                               xn1T[d][:, tt * 128:(tt + 1) * 128],
                                               wq8[e][d][:, :],
                                               start=(d == 0), stop=(d == ND - 1))
                        ins.then_inc(mm5, 1)
            blk.tensor(p5_tensor)

            def p5_vector(v):
                for tt in range(NTT):
                    for eg in range(2):
                        g = tt * 2 + eg
                        v.wait_ge(mm5, g + 1)
                        for e4 in range(4):
                            e = eg * 4 + e4
                            v.scalar_tensor_tensor(
                                qacc[tt][:, :], ps5[(g % 2) * 4 + e4][:, :],
                                m_tok[tt][:, e:e + 1], qacc[tt][:, :],
                                OP.mult, OP.add).then_inc(ev5, 1)
                            v.drain()
                blk_dummy = None
            blk.vector(p5_vector)

            def p5_gpsimd(g):
                for tt in range(NTT):
                    g.wait_ge(ev5, 4 * (tt * 2 + 2))
                    g.dma_start(out=qkvo[tt * 128:(tt + 1) * 128, :],
                                in_=qacc[tt][:, :]).then_inc(st5[tt % 2], 16)
                g.wait_ge(st5[0], 16 * (NTT // 2))
                g.wait_ge(st5[1], 16 * (NTT // 2))
            blk.gpsimd(p5_gpsimd)

            ctx.barrier()

    return nc, dbg_tensors



def build_launch3(debug=(), upto='Z'):
    nc = bass.Bass("TRN2", target_bir_lowering=False)
    dbg = set(debug)

    xbcp = nc.dram_tensor("xbcp", [CONV_DIM, TPC], BF16, kind="ExternalInput").ap()
    bnd3 = nc.dram_tensor("bnd3", [CONV_DIM, 4], BF16, kind="ExternalInput").ap()
    convw = nc.dram_tensor("convw", [CONV_DIM, DCONV], F32, kind="ExternalInput").ap()
    convb = nc.dram_tensor("convb", [CONV_DIM, 1], F32, kind="ExternalInput").ap()
    dtraw = nc.dram_tensor("dtraw", [NH1, TPC], F32, kind="ExternalInput").ap()
    dtb = nc.dram_tensor("dtb", [NH1, 1], F32, kind="ExternalInput").ap()
    alog = nc.dram_tensor("alog", [NH1, 1], F32, kind="ExternalInput").ap()
    dpar = nc.dram_tensor("dpar", [1, NH1], F32, kind="ExternalInput").ap()
    qtl = nc.dram_tensor("qtl", [TPC, DV1], BF16, kind="ExternalInput").ap()
    KROWS = 160 * 128
    kcat = nc.dram_tensor("kcat", [KROWS, DV1], BF16, kind="ExternalInput").ap()
    vcat = nc.dram_tensor("vcat", [KROWS, DV1], BF16, kind="ExternalInput").ap()
    kdg = nc.dram_tensor("kdg", [TPC, DV1], BF16, kind="ExternalInput").ap()
    vdgd = nc.dram_tensor("vdg", [TPC, DV1], BF16, kind="ExternalInput").ap()
    mvals = nc.dram_tensor("mvals", [160, 512], F32, kind="ExternalInput").ap()
    mrwg = nc.dram_tensor("mrwg", [E1, TPC], F32, kind="ExternalInput").ap()
    oproj = nc.dram_tensor("oproj", [E1, DV1, D], F32, kind="ExternalInput").ap()

    ylo = nc.dram_tensor("ylocal_out", [TPC, D_INNER], BF16, kind="ExternalOutput").ap()
    sto = nc.dram_tensor("states_out", [N1, NH1 * P1], BF16, kind="ExternalOutput").ap()
    eco = nc.dram_tensor("expcum_out", [TPC, NH1], F32, kind="ExternalOutput").ap()
    dco = nc.dram_tensor("decay_out", [NH1, 1], F32, kind="ExternalOutput").ap()
    co = nc.dram_tensor("C_out", [N1, TPC], BF16, kind="ExternalOutput").ap()
    g1o = nc.dram_tensor("g1_out", [D, TPC], BF16, kind="ExternalOutput").ap()

    xoh = nc.dram_tensor("xoh", [D_INNER, TPC], BF16, kind="ExternalOutput").ap()
    bhb = nc.dram_tensor("bhb", [N1, TPC], BF16).ap()
    cumhb = nc.dram_tensor("cumhb", [NH1, TPC], F32).ap()
    opbf = nc.dram_tensor("opbf", [E1, DV1, D], BF16).ap()

    dbg_tensors = {}
    def dbg_out(name, shape, dtype=F32):
        if name in dbg:
            dbg_tensors[name] = nc.dram_tensor("dbg_" + name, shape, dtype,
                                               kind="ExternalOutput").ap()
            return dbg_tensors[name]
        return None

    NBC = CONV_DIM // 128
    with ExitStack() as st:
        ctx = Ctx(nc, st, const_vals=(1e-5, 1e-20, -1e30, -1e9, 1e-12, 30.0))
        blk = ctx.blk

        ones_bf = ctx.sb("ones_bf", [128, 1], BF16)
        ones1_f = ctx.sb("ones1_f", [1, 128], F32)
        iv = ctx.sb("iv", [128, 128], F32)
        ident_f = ctx.sb("ident_f", [128, 128], F32)
        ident_bf = ctx.sb("ident_bf", [128, 128], BF16)
        ones1_bf = ctx.sb("ones1_bf", [1, 128], BF16)
        bigmask = ctx.sb("bigmask", [128, 128], F32)
        bigmask4 = ctx.sb("bigmask4", [128, 512], BF16)
        dmaskadd = ctx.sb("dmaskadd", [128, 128], F32)
        dmaskadd_bf = ctx.sb("dmaskadd_bf", [128, 128], BF16)

        oc_sem = ctx.sem("oc_sem")
        k_sem = ctx.sem("k_sem")

        def p0_gpsimd(g):
            g.dma_start(out=opbf[:, :, :], in_=oproj[:, :, :]).then_inc(oc_sem, 16)
            g.iota(iv[:, :], pattern=[[1, 128]], base=0, channel_multiplier=-1,
                   allow_small_or_imprecise_dtypes=True).then_inc(k_sem, 1)
        blk.gpsimd(p0_gpsimd)

        def p0_vector(v):
            ctx.emit_const_memsets(v)
            v.memset(ones_bf[:, :], 1.0)
            v.memset(ones1_f[:, :], 1.0)
            v.memset(ones1_bf[:, :], 1.0)
            v.wait_ge(k_sem, 1)
            v.tensor_single_scalar(ident_f[:, :], iv[:, :], 0.0, OP.is_equal)
            v.drain()
            v.tensor_copy(ident_bf[:, :], ident_f[:, :])
            v.tensor_single_scalar(bigmask[:, :], iv[:, :], 0.0, OP.is_lt)
            v.drain()
            v.tensor_scalar_mul(bigmask[:, :], bigmask[:, :], -1e9)
            v.drain()
            for _i4 in range(4):
                v.tensor_copy(bigmask4[:, _i4 * 128:(_i4 + 1) * 128], bigmask[:, :])
            v.drain()
            v.tensor_single_scalar(dmaskadd[:, :], iv[:, :], 0.0, OP.is_lt)
            v.drain()
            v.tensor_scalar_mul(dmaskadd[:, :], dmaskadd[:, :], -1e9)
            v.drain()
            v.tensor_copy(dmaskadd_bf[:, :], dmaskadd[:, :])
        blk.vector(p0_vector)

        ctx.barrier()

        xbcs_bc = [ctx.sb(f"xbcs_bc{i}", [128, TPC], BF16) for i in range(2)]
        dtT = ctx.sb("dtT", [NH1, TPC], F32)
        cumc = ctx.sb("cumc", [NH1, TPC], F32)
        lnb = ctx.sb("lnb", [NH1, TPC], F32)
        wdtf = ctx.sb("wdtf", [NH1, TPC], F32)
        expcf = ctx.sb("expcf", [NH1, TPC], F32)
        tot_c = ctx.sb("tot_c", [NH1, 16], F32)
        base_c = ctx.sb("base_c", [NH1, 16], F32)
        dect = ctx.sb("dect", [NH1, 16], F32)
        bias_tok = [ctx.sb(f"bias_tok{i}", [128, NH1], F32) for i in range(NCH1)]
        expc_tok = [ctx.sb(f"expc_tok{i}", [128, NH1], F32) for i in range(NCH1)]
        wdt_tok = [ctx.sb(f"wdt_tok{i}", [128, NH1], F32) for i in range(NCH1)]
        dpar_t = ctx.sb("dpar_t", [128, NH1], F32)

        with ExitStack() as q1:
            xpad = [ctx.sb(f"xpad{i}", [128, TPC + 4], BF16, q1) for i in range(2)]
            cw = [ctx.sb(f"cw{i}", [128, DCONV], F32, q1) for i in range(2)]
            cb = [ctx.sb(f"cb{i}", [128, 1], F32, q1) for i in range(2)]
            cacc = [ctx.sb(f"cacc{i}", [128, TPC], F32, q1) for i in range(2)]
            xout = [ctx.sb(f"xout{i}", [128, TPC], BF16, q1) for i in range(2)]

            ld1 = [ctx.sem("ld1_0"), ctx.sem("ld1_1")]
            cv_ev = ctx.sem("cv_ev")
            sl_ev = ctx.sem("sl_ev")
            st1 = [ctx.sem("st1_0"), ctx.sem("st1_1")]

            def q1_sync(sy):
                for c in range(NBC):
                    p = c % 2
                    if c >= 2:
                        sy.wait_ge(sl_ev, c - 1)
                    sy.dma_start(out=xpad[p][:, 0:4], in_=bnd3[c * 128:(c + 1) * 128, :]
                                 ).then_inc(ld1[p], 16)
                    sy.dma_start(out=xpad[p][:, 4:4 + TPC],
                                 in_=xbcp[c * 128:(c + 1) * 128, :]).then_inc(ld1[p], 16)
                    sy.dma_start(out=cw[p][:, :], in_=convw[c * 128:(c + 1) * 128, :]
                                 ).then_inc(ld1[p], 16)
                    sy.dma_start(out=cb[p][:, :], in_=convb[c * 128:(c + 1) * 128, :]
                                 ).then_inc(ld1[p], 16)
            blk.sync(q1_sync)

            def q1_vector(v):
                for c in range(NBC):
                    p = c % 2
                    v.wait_ge(ld1[p], 64 * (c // 2 + 1))
                    if c >= 2:
                        v.wait_ge(sl_ev, c - 1)
                    v.tensor_single_scalar(cacc[p][:, :], xpad[p][:, 1:1 + TPC],
                                           cw[p][:, 0:1], OP.mult)
                    v.drain()
                    for kk in range(1, DCONV):
                        v.scalar_tensor_tensor(cacc[p][:, :],
                                               xpad[p][:, 1 + kk:1 + kk + TPC],
                                               cw[p][:, kk:kk + 1], cacc[p][:, :],
                                               OP.mult, OP.add)
                        v.drain()
                    v.engine_nop().then_inc(cv_ev, 1)
            blk.vector(q1_vector)

            def q1_scalar(s):
                for c in range(NBC):
                    p = c % 2
                    s.wait_ge(cv_ev, c + 1)
                    if c >= 2 and c < 16:
                        s.wait_ge(st1[p], 16 * ((c - 2) // 2 + 1))
                    dst = (xout[p] if c < 16 else xbcs_bc[c - 16])
                    s.activation(out=dst[:, :], in_=cacc[p][:, :], func=AF.Silu,
                                 bias=cb[p][:, :]).then_inc(sl_ev, 1)
            blk.scalar(q1_scalar)

            def q1_gpsimd(g):
                for c in range(NBC):
                    p = c % 2
                    g.wait_ge(sl_ev, c + 1)
                    if c < 16:
                        g.dma_start(out=xoh[c * 128:(c + 1) * 128, :], in_=xout[p][:, :]
                                    ).then_inc(st1[p], 16)
                    elif c == 16:
                        g.dma_start(out=bhb[:, :], in_=xbcs_bc[0][:, :]).then_inc(st1[0], 16)
                    else:
                        g.dma_start(out=co[:, :], in_=xbcs_bc[1][:, :]).then_inc(st1[1], 16)
                g.wait_ge(st1[0], 16 * 9)
                g.wait_ge(st1[1], 16 * 9)
            blk.gpsimd(q1_gpsimd)

            ctx.barrier()

        with ExitStack() as q2:
            scr = ctx.sb("scr_q2", [NH1, TPC], F32, q2)
            acol = ctx.sb("acol", [NH1, 1], F32, q2)
            ecst = [ctx.sb(f"ecst{i}", [128, NH1], F32, q2) for i in range(2)]
            ps_q2 = [ctx.ps(f"ps_q2{i}", [128, 128], F32, q2) for i in range(4)]

            steps = []

            def s_ld(sy, dma):
                dma(dtT[:, :], dtraw[:, :])
                dma(acol[:, :], dtb[:, :])
                dma(scr[:, 0:1], alog[:, :])
                dma(dpar_t[:, :], dpar[0:1, :].partition_broadcast(128))
            steps.append(("sync", s_ld))

            def s_dt1(v, dma):
                v.tensor_single_scalar(dtT[:, :], dtT[:, :], acol[:, :], OP.add)
            steps.append(("vector", s_dt1))

            def s_dt2(s, dma):
                s.activation(out=scr[:, 1:2], in_=scr[:, 0:1], func=AF.Exp)
                s.drain()
                s.activation(out=lnb[:, :], in_=dtT[:, :], func=AF.Abs)
                s.drain()
                s.activation(out=lnb[:, :], in_=lnb[:, :], func=AF.Exp, scale=-1.0)
                s.drain()
                s.activation(out=lnb[:, :], in_=lnb[:, :], func=AF.Ln, bias=1.0)
            steps.append(("scalar", s_dt2))

            def s_dt3(v, dma):
                v.tensor_scalar_mul(acol[:, :], scr[:, 1:2], -1.0)
                v.drain()
                v.tensor_scalar_max(scr[:, :], dtT[:, :], 0.0)
                v.drain()
                v.tensor_add(dtT[:, :], scr[:, :], lnb[:, :])
            steps.append(("vector", s_dt3))

            def s_dt4(s, dma):
                s.activation(out=lnb[:, :], in_=dtT[:, :], func=AF.Ln)
            steps.append(("scalar", s_dt4))

            def s_dt5(v, dma):
                v.tensor_single_scalar(scr[:, :], dtT[:, :], acol[:, :], OP.mult)
            steps.append(("vector", s_dt5))

            def s_cum(v, dma):
                for ch in range(NCH1):
                    sl = slice(ch * Q1, (ch + 1) * Q1)
                    v.tensor_tensor_scan(cumc[:, sl], scr[:, sl], scr[:, sl],
                                         0.0, OP.add, OP.bypass)
                v.drain()
                for ch in range(NCH1):
                    v.tensor_copy(tot_c[:, ch:ch + 1], cumc[:, ch * Q1 + Q1 - 1:ch * Q1 + Q1])
                v.drain()
                v.tensor_tensor_scan(base_c[:, :], tot_c[:, :], tot_c[:, :],
                                     0.0, OP.add, OP.bypass)
                v.drain()
                v.tensor_sub(base_c[:, :], base_c[:, :], tot_c[:, :])
                v.drain()
                v.tensor_sub(lnb[:, :], lnb[:, :], cumc[:, :])
            steps.append(("vector", s_cum))

            def s_exp1(s, dma):
                s.activation(out=expcf[:, :], in_=cumc[:, :], func=AF.Exp)
                for ch in range(NCH1):
                    sl = slice(ch * Q1, (ch + 1) * Q1)
                    s.activation(out=wdtf[:, sl], in_=cumc[:, sl], func=AF.Exp,
                                 scale=-1.0, bias=tot_c[:, ch:ch + 1])
                s.drain()
                s.activation(out=dect[:, :], in_=tot_c[:, :], func=AF.Exp)
            steps.append(("scalar", s_exp1))

            def s_wdt(v, dma):
                v.tensor_mul(wdtf[:, :], wdtf[:, :], dtT[:, :])
            steps.append(("vector", s_wdt))

            def s_ec(s, dma):
                for ch in range(NCH1):
                    sl = slice(ch * Q1, (ch + 1) * Q1)
                    s.activation(out=scr[:, sl], in_=cumc[:, sl], func=AF.Exp,
                                 bias=base_c[:, ch:ch + 1])
            steps.append(("scalar", s_ec))

            def s_dec(v, dma):
                v.tensor_copy(acol[:, :], scr[:, TPC - 1:TPC])
            steps.append(("vector", s_dec))

            def s_deco(g, dma):
                dma(dco[:, :], acol[:, :])
            steps.append(("gpsimd", s_deco))

            for ch in range(NCH1):
                def s_tr(t, dma, ch=ch):
                    sl = slice(ch * Q1, (ch + 1) * Q1)
                    t.transpose(ps_q2[0][:, 0:NH1], scr[:, sl], ident_f[0:NH1, 0:NH1])
                    t.transpose(ps_q2[1][:, 0:NH1], lnb[:, sl], ident_f[0:NH1, 0:NH1])
                    t.transpose(ps_q2[2][:, 0:NH1], expcf[:, sl], ident_f[0:NH1, 0:NH1])
                    t.transpose(ps_q2[3][:, 0:NH1], wdtf[:, sl], ident_f[0:NH1, 0:NH1])
                steps.append(("tensor", s_tr))

                def s_tre(v, dma, ch=ch):
                    v.tensor_copy(bias_tok[ch][:, :], ps_q2[1][:, 0:NH1])
                    v.tensor_copy(expc_tok[ch][:, :], ps_q2[2][:, 0:NH1])
                    v.tensor_copy(wdt_tok[ch][:, :], ps_q2[3][:, 0:NH1])
                    v.tensor_copy(ecst[ch % 2][:, :], ps_q2[0][:, 0:NH1])
                steps.append(("vector", s_tre))

                def s_ecg(g, dma, ch=ch):
                    dma(eco[ch * 128:(ch + 1) * 128, :], ecst[ch % 2][:, :])
                steps.append(("gpsimd", s_ecg))

            def s_cumout(g, dma):
                dma(cumhb[:, :], cumc[:, :])
            steps.append(("gpsimd", s_cumout))

            ctx.seq("q2", steps)
            ctx.barrier()

        if upto == 'B':
            return nc, dbg_tensors

        with ExitStack() as q4:
            X_tok = [ctx.sb(f"X_tok{i}", [128, D_INNER], BF16, q4) for i in range(NCH1)]
            B_tok = [ctx.sb(f"B_tok{i}", [128, N1], BF16, q4) for i in range(NCH1)]
            ylp = [ctx.sb(f"ylp{i}", [128, D_INNER], BF16, q4) for i in range(2)]
            Sst = [ctx.sb(f"Sst{i}", [128, D_INNER], BF16, q4) for i in range(2)]
            slsb = ctx.sb("slsb", [128, D_INNER], F32, q4)
            wdtX = ctx.sb("wdtX", [128, D_INNER], BF16, q4)
            CBm = ctx.sb("CBm", [128, 128], BF16, q4)
            expdt = [ctx.sb(f"expdt{i}", [128, 128], F32, q4) for i in range(4)]
            scT = [ctx.sb(f"scT{i}", [128, 128], BF16, q4) for i in range(NH1)]
            stage = ctx.sb("stage", [1, NH1 * 128], F32, q4)
            drow = ctx.sb("drow", [1, NH1], F32, q4)
            decn = ctx.sb("decn", [128, NH1], F32, q4)

            psTw = [ctx.ps(f"psTw{i}", [128, 512], F32, q4) for i in range(2)]
            psCB = ctx.ps("psCB", [128, 128], F32, q4)
            psIN = [ctx.ps(f"psIN{i}", [128, 512], F32, q4) for i in range(2)]
            psCR = [ctx.ps(f"psCR{i}", [128, 512], F32, q4) for i in range(2)]
            psDN = ctx.ps("psDN", [128, NH1], F32, q4)

            xt3 = ctx.sem("xt3")
            bt3 = ctx.sem("bt3")
            stg = ctx.sem("stg")
            s_wdt = ctx.sem("s_wdt")
            t_cb = ctx.sem("t_cb")
            v_cb = ctx.sem("v_cb")
            t_trep = ctx.sem("t_trep")
            s_exp = ctx.sem("s_exp")
            v_sc = ctx.sem("v_sc")
            t_ic = ctx.sem("t_ic")
            s_ic = ctx.sem("s_ic")
            v_cr = ctx.sem("v_cr")
            t_sl = ctx.sem("t_sl")
            t_dnt = ctx.sem("t_dnt")
            v_dn = ctx.sem("v_dn")
            t_dnb = ctx.sem("t_dnb")
            v_rec = ctx.sem("v_rec")
            s_sl = ctx.sem("s_sl")
            yst = [ctx.sem("yst0"), ctx.sem("yst1")]
            fin = ctx.sem("fin")
            t_in = ctx.sem("t_in")
            v_init = ctx.sem("v_init")

            def q4_sync(sy):
                sy.wait_ge(st1[0], 16 * 9)
                sy.wait_ge(st1[1], 16 * 9)
                for i in range(NCH1):
                    sy.dma_start_transpose(X_tok[i][:, :], xoh[:, i * 128:(i + 1) * 128]
                                           ).then_inc(xt3, 16)
                for i in range(NCH1):
                    sy.dma_start_transpose(B_tok[i][:, :], bhb[:, i * 128:(i + 1) * 128]
                                           ).then_inc(bt3, 16)
            blk.sync(q4_sync)

            def scw(h):
                return v_sc

            def scv(ch, h):
                return ch * 32 + h + 1

            def q4_gpsimd(g):
                g.wait_ge(xt3, 16 * NCH1)
                for ch in range(NCH1):
                    if ch >= 1:
                        g.wait_ge(t_trep, 8 * ch)
                    g.dma_start(out=stage[:, :], in_=cumhb[:, ch * 128:(ch + 1) * 128]
                                ).then_inc(stg, 16)
                    g.wait_ge(s_ic, (ch + 1) * 4)
                    g.wait_ge(v_cr, (ch + 1) * 32)
                    g.dma_start(out=ylo[ch * 128:(ch + 1) * 128, :], in_=ylp[ch % 2][:, :]
                                ).then_inc(yst[ch % 2], 16)
                g.wait_ge(v_rec, 16 * NCH1)
                g.dma_start(out=sto[:, :], in_=Sst[NCH1 % 2][:, :]).then_inc(fin, 16)
                g.wait_ge(fin, 16)
                g.wait_ge(yst[0], 16 * (NCH1 // 2))
                g.wait_ge(yst[1], 16 * (NCH1 // 2))
            blk.gpsimd(q4_gpsimd)

            def q4_vector0(v):
                v.memset(Sst[0][:, :], 0.0)
                v.drain()
                v.engine_nop().then_inc(v_init, 1)
            blk.vector(q4_vector0)

            def q4_tensor(t):
                t.wait_ge(bt3, 16 * NCH1)
                t.wait_ge(v_init, 1)
                for ch in range(NCH1):
                    sl = slice(ch * 128, (ch + 1) * 128)
                    cur = ch % 2
                    t.wait_ge(stg, 16 * (ch + 1))
                    if ch >= 1:
                        t.wait_ge(v_cb, ch)
                    t.matmul(psCB[:, :], xbcs_bc[0][:, sl], xbcs_bc[1][:, sl],
                             start=True, stop=True).then_inc(t_cb, 1)
                    for gg in range(8):
                        G = ch * 8 + gg
                        if G >= 2:
                            t.wait_ge(s_exp, 4 * (G - 1))
                        t.matmul(psTw[G % 2][:, :], ones1_f[:, :],
                                 stage[0:1, gg * 512:(gg + 1) * 512],
                                 start=True, stop=False)
                        t.matmul(psTw[G % 2][:, :], ident_bf[:, :], bigmask4[:, :],
                                 start=False, stop=True).then_inc(t_trep, 1)
                    for oct in range(4):
                        G = ch * 4 + oct
                        if G >= 2:
                            t.wait_ge(s_ic, G - 1)
                            t.wait_ge(v_cr, 8 * (G - 1))
                        if ch >= 1 and oct == 0:
                            t.wait_ge(v_rec, 16 * ch)
                        for h8 in range(8):
                            h = oct * 8 + h8
                            t.wait_ge(scw(h), scv(ch, h))
                            t.matmul(psIN[oct % 2][:, h8 * 64:(h8 + 1) * 64],
                                     scT[h][:, :],
                                     X_tok[ch][:, h * P1:(h + 1) * P1],
                                     start=True, stop=True).then_inc(t_in, 1)
                        t.matmul(psCR[oct % 2][:, :], xbcs_bc[1][:, sl],
                                 Sst[cur][:, oct * 512:(oct + 1) * 512],
                                 start=True, stop=True).then_inc(t_ic, 1)
                    t.wait_ge(s_wdt, ch + 1)
                    t.wait_ge(s_ic, (ch + 1) * 4)
                    t.wait_ge(v_cr, (ch + 1) * 32)
                    slb = [psIN[0], psIN[1], psCR[0], psCR[1]]
                    ins = None
                    for oct in range(4):
                        ins = t.matmul(slb[oct][:, :], B_tok[ch][:, :],
                                       wdtX[:, oct * 512:(oct + 1) * 512],
                                       start=True, stop=True)
                    ins.then_inc(t_sl, 1)
                    if ch >= 1:
                        t.wait_ge(v_dn, 2 * ch)
                    t.transpose(psDN[0:1, :], dect[:, ch:ch + 1],
                                ident_f[0:NH1, 0:NH1]).then_inc(t_dnt, 1)
                    t.wait_ge(v_dn, 2 * ch + 1)
                    t.matmul(psDN[:, :], ones1_f[:, :], drow[:, :],
                             start=True, stop=True).then_inc(t_dnb, 1)
            blk.tensor(q4_tensor)

            def q4_scalar(s):
                for ch in range(NCH1):
                    for h in range(NH1):
                        u = ch * 32 + h
                        t_ = u - 4
                        if t_ >= 0:
                            s.wait_ge(scw(t_ % 32), scv(t_ // 32, t_ % 32))
                        G = ch * 8 + h // 4
                        s.wait_ge(t_trep, G + 1)
                        s.activation(out=expdt[h % 4][:, :],
                                     in_=psTw[G % 2][:, (h % 4) * 128:(h % 4 + 1) * 128],
                                     func=AF.Exp,
                                     bias=bias_tok[ch][:, h:h + 1]).then_inc(s_exp, 1)
                    if ch >= 1:
                        s.wait_ge(t_sl, ch)
                    ins = None
                    for h in range(NH1):
                        ins = s.activation(out=wdtX[:, h * P1:(h + 1) * P1],
                                           in_=X_tok[ch][:, h * P1:(h + 1) * P1],
                                           func=AF.Copy,
                                           scale=wdt_tok[ch][:, h:h + 1])
                    ins.then_inc(s_wdt, 1)
                    for oct in range(4):
                        G = ch * 4 + oct
                        s.wait_ge(t_in, ch * 32 + oct * 8 + 8)
                        if oct == 0:
                            s.wait_ge(yst[ch % 2], 16 * (ch // 2))
                        s.activation(out=ylp[ch % 2][:, oct * 512:(oct + 1) * 512],
                                     in_=psIN[oct % 2][:, :],
                                     func=AF.Copy).then_inc(s_ic, 1)
                    s.wait_ge(t_sl, ch + 1)
                    ins = None
                    for oct in range(4):
                        slb = [psIN[0], psIN[1], psCR[0], psCR[1]]
                        ins = s.activation(out=slsb[:, oct * 512:(oct + 1) * 512],
                                           in_=slb[oct][:, :], func=AF.Copy)
                    ins.then_inc(s_sl, 1)
            blk.scalar(q4_scalar)

            def q4_vector(v):
                for ch in range(NCH1):
                    cur, nxt = ch % 2, (ch + 1) % 2
                    v.wait_ge(t_cb, ch + 1)
                    v.tensor_copy(CBm[:, :], psCB[:, :]).then_inc(v_cb, 1)
                    v.drain()
                    for h in range(NH1):
                        u = ch * 32 + h
                        v.wait_ge(s_exp, u + 1)
                        if ch >= 1:
                            v.wait_ge(t_in, (ch - 1) * 32 + h + 1)
                        v.tensor_mul(scT[h][:, :], expdt[h % 4][:, :], CBm[:, :]
                                     ).then_inc(v_sc, 1)
                    for oct in range(4):
                        G = ch * 4 + oct
                        v.wait_ge(t_ic, G + 1)
                        v.wait_ge(s_ic, G + 1)
                        for h8 in range(8):
                            h = oct * 8 + h8
                            v.scalar_tensor_tensor(
                                ylp[ch % 2][:, h * P1:(h + 1) * P1],
                                psCR[oct % 2][:, h8 * 64:(h8 + 1) * 64],
                                expc_tok[ch][:, h:h + 1],
                                ylp[ch % 2][:, h * P1:(h + 1) * P1],
                                OP.mult, OP.add).then_inc(v_cr, 1)
                        v.drain()
                    v.wait_ge(t_dnt, ch + 1)
                    v.tensor_copy(drow[:, :], psDN[0:1, :]).then_inc(v_dn, 1)
                    v.drain()
                    v.wait_ge(t_dnb, ch + 1)
                    v.tensor_copy(decn[:, :], psDN[:, :]).then_inc(v_dn, 1)
                    v.drain()
                    v.wait_ge(s_sl, ch + 1)
                    for h in range(NH1):
                        v.scalar_tensor_tensor(Sst[nxt][:, h * P1:(h + 1) * P1],
                                               Sst[cur][:, h * P1:(h + 1) * P1],
                                               decn[:, h:h + 1],
                                               slsb[:, h * P1:(h + 1) * P1],
                                               OP.mult, OP.add)
                    v.drain()
                    v.engine_nop().then_inc(v_rec, 16)
            blk.vector(q4_vector)

            ctx.barrier()

        if upto == 'C':
            return nc, dbg_tensors

        ao = ctx.sb("ao", [128, TPC], F32)
        lr = ctx.sb("lr", [1, TPC], F32)
        with ExitStack() as q5:
            qTt = ctx.sb("qTt", [128, TPC], BF16, q5)
            kdgT = ctx.sb("kdgT", [128, TPC], BF16, q5)
            ktt = [ctx.sb(f"ktt{i}", [128, TPC], BF16, q5) for i in range(10)]
            vtt = [ctx.sb(f"vtt{i}", [128, DV1], BF16, q5) for i in range(160)]
            vdgt = [ctx.sb(f"vdgt{i}", [128, DV1], BF16, q5) for i in range(NTT)]
            mxq = ctx.sb("mxq", [128, NTT], F32, q5)
            mtmp = ctx.sb("mtmp", [128, 256], F32, q5)
            mrow = ctx.sb("mrow", [1, TPC], F32, q5)
            mstg = [ctx.sb(f"mstg{i}", [1, 512], F32, q5) for i in range(4)]
            mrow2 = [ctx.sb(f"mrow2_{i}", [1, 512], BF16, q5) for i in range(4)]
            PT = [ctx.sb(f"PT{i}", [128, 512], BF16, q5) for i in range(4)]

            psS = [ctx.ps(f"psS{i}", [128, 512], F32, q5) for i in range(2)]
            psST = [ctx.ps(f"psST{i}", [128, 512], F32, q5) for i in range(4)]
            psAV = ctx.ps("psAV", [128, 512], F32, q5)
            psL = ctx.ps("psL", [1, 512], F32, q5)

            a_ld = ctx.sem("a_ld")
            q_ld = ctx.sem("q_ld")
            av_ld = ctx.sem("av_ld")
            v_ini = ctx.sem("v_ini")
            p1mm = ctx.sem("p1mm")
            p1ev = ctx.sem("p1ev")
            trx = ctx.sem("trx")
            trev = ctx.sem("trev")
            mgl = [ctx.sem(f"mgl{i}") for i in range(4)]
            mrw2v = ctx.sem("mrw2v")
            stmm = ctx.sem("stmm")
            pte = ctx.sem("pte")
            avmm = ctx.sem("avmm")
            aoev = ctx.sem("aoev")

            KBC = [16 * (g + 1) for g in range(4)]
            KB0 = [sum(KBC[:g]) for g in range(4)]
            P1U = []
            for g in range(4):
                for r in range(4):
                    j = g * 4 + r
                    for kc in range(KBC[g] // 4):
                        P1U.append((j, "k", KB0[g] + kc * 4))
                    P1U.append((j, "d", j))
            NP1 = len(P1U)
            ULAST = {}
            for i, (j, kind, idx) in enumerate(P1U):
                ULAST[j] = i + 1
            P2U = []
            for g in range(4):
                for kb in range(KBC[g]):
                    P2U.append((g, "k", kb))
                for r in range(4):
                    P2U.append((g, "d", r))
            NP2 = len(P2U)
            GFIRST = {}
            n = 0
            for g in range(4):
                GFIRST[g] = n
                n += KBC[g] + 4
            GLAST = {g: GFIRST[g] + KBC[g] + 4 for g in range(4)}

            def kslice(gkb):
                return ktt[gkb // NTT], (gkb % NTT) * 128

            def q5_sync(sy):
                sy.dma_start_transpose(qTt[:, :], qtl[:, :]).then_inc(q_ld, 16)
                sy.dma_start_transpose(kdgT[:, :], kdg[:, :]).then_inc(a_ld, 16)
                for i in range(10):
                    sy.dma_start_transpose(ktt[i][:, :],
                                           kcat[i * TPC:(i + 1) * TPC, :]).then_inc(a_ld, 16)
                for i in range(160):
                    sy.dma_start(out=vtt[i][:, :], in_=vcat[i * 128:(i + 1) * 128, :]
                                 ).then_inc(av_ld, 16)
                for i in range(NTT):
                    sy.dma_start(out=vdgt[i][:, :], in_=vdgd[i * 128:(i + 1) * 128, :]
                                 ).then_inc(av_ld, 16)
            blk.sync(q5_sync)

            def q5_gpsimd(g_):
                nk = {0: 0, 1: 0, 2: 0, 3: 0}
                for n2, (g, kind, idx) in enumerate(P2U):
                    if n2 >= 4:
                        g_.wait_ge(stmm, n2 - 3)
                    if kind == "k":
                        nk[n2 % 4] += 1
                        g_.dma_start(out=mstg[n2 % 4][:, :],
                                     in_=mvals[KB0[g] + idx:KB0[g] + idx + 1, :]
                                     ).then_inc(mgl[n2 % 4], 16)
                        g_.wait_ge(mgl[n2 % 4], 16 * nk[n2 % 4])
                        g_.wait_ge(trev, 4 * (g + 1))
                        g_.tensor_sub(mrow2[n2 % 4][:, :],
                                      mrow[0:1, g * 512:(g + 1) * 512],
                                      mstg[n2 % 4][:, :]).then_inc(mrw2v, 1)
                    else:
                        jj = g * 4 + idx
                        g_.wait_ge(trev, jj + 1)
                        g_.tensor_copy(mrow2[n2 % 4][0:1, 0:128],
                                       mrow[0:1, jj * 128:(jj + 1) * 128]
                                       ).then_inc(mrw2v, 1)
                    g_.drain()
            blk.gpsimd(q5_gpsimd)

            def q5_vector0(v):
                v.wait_ge(q_ld, 16)
                v.drain()
                v.tensor_scalar_mul(qTt[:, :], qTt[:, :], 0.08838834764831845)
                v.drain()
                v.engine_nop().then_inc(v_ini, 1)
            blk.vector(q5_vector0)

            def q5_tensor(t):
                t.wait_ge(a_ld, 16 * 11)
                t.wait_ge(av_ld, 16 * (160 + NTT))
                t.wait_ge(v_ini, 1)
                for n1, (j, kind, idx) in enumerate(P1U):
                    qs = slice(j * 128, (j + 1) * 128)
                    if n1 >= 2:
                        t.wait_ge(p1ev, n1 - 1)
                    if kind == "k":
                        kt, k0 = kslice(idx)
                        t.matmul(psS[n1 % 2][:, :], qTt[:, qs], kt[:, k0:k0 + 512],
                                 start=True, stop=True).then_inc(p1mm, 1)
                    else:
                        t.matmul(psS[n1 % 2][:, 0:128], qTt[:, qs], kdgT[:, qs],
                                 start=True, stop=True).then_inc(p1mm, 1)
                t.wait_ge(p1ev, NP1 + NTT)
                for j in range(NTT):
                    if j >= 1:
                        t.wait_ge(trev, j)
                    t.transpose(psS[0][0:1, 0:128], mxq[:, j:j + 1], ident_f[:, :]
                                ).then_inc(trx, 1)
                for n2, (g, kind, idx) in enumerate(P2U):
                    qg = slice(g * 512, (g + 1) * 512)
                    t.wait_ge(mrw2v, n2 + 1)
                    if n2 >= 4:
                        t.wait_ge(pte, n2 - 3)
                    if kind == "k":
                        kt, k0 = kslice(KB0[g] + idx)
                        t.matmul(psST[n2 % 4][:, :], kt[:, k0:k0 + 128], qTt[:, qg],
                                 start=True, stop=False)
                        t.matmul(psST[n2 % 4][:, :], ones1_bf[:, :],
                                 mrow2[n2 % 4][0:1, :],
                                 start=False, stop=True).then_inc(stmm, 1)
                    else:
                        j = g * 4 + idx
                        qs = slice(j * 128, (j + 1) * 128)
                        t.matmul(psST[n2 % 4][:, 0:128], kdgT[:, qs], qTt[:, qs],
                                 start=True, stop=False)
                        t.matmul(psST[n2 % 4][:, 0:128], ones1_bf[:, :],
                                 mrow2[n2 % 4][0:1, 0:128],
                                 start=False, stop=False)
                        t.matmul(psST[n2 % 4][:, 0:128], ident_bf[:, :],
                                 dmaskadd_bf[:, :],
                                 start=False, stop=True).then_inc(stmm, 1)
                    if n2 >= 2:
                        emit_avl(t, n2 - 2)
                for nn in (NP2 - 2, NP2 - 1):
                    emit_avl(t, nn)
            def emit_avl(t, nn):
                g, kind, idx = P2U[nn]
                first = nn == GFIRST[g]
                last = nn == GLAST[g] - 1
                t.wait_ge(pte, nn + 1)
                if first:
                    t.wait_ge(aoev, 2 * g)
                if kind == "k":
                    t.matmul(psAV[:, :], vtt[KB0[g] + idx][:, :], PT[nn % 4][:, :],
                             start=first, stop=last)
                    t.matmul(psL[:, :], ones_bf[:, :], PT[nn % 4][:, :],
                             start=first, stop=last).then_inc(avmm, 1)
                else:
                    j = g * 4 + idx
                    rs = slice(idx * 128, (idx + 1) * 128)
                    t.matmul(psAV[:, rs], vdgt[j][:, :], PT[nn % 4][:, 0:128],
                             start=False, stop=last)
                    t.matmul(psL[0:1, rs], ones_bf[:, :], PT[nn % 4][:, 0:128],
                             start=False, stop=last).then_inc(avmm, 1)
            blk.tensor(q5_tensor)

            def q5_vector(v):
                for n1, (j, kind, idx) in enumerate(P1U):
                    v.wait_ge(p1mm, n1 + 1)
                    w1 = 512 if kind == "k" else 128
                    v.reduce_max(mtmp[:, n1:n1 + 1], psS[n1 % 2][:, 0:w1],
                                 axis=AX.X).then_inc(p1ev, 1)
                v.drain()
                for j in range(NTT):
                    u1 = ULAST[j]
                    u0 = u1 - (KBC[j // 4] // 4 + 1)
                    v.reduce_max(mxq[:, j:j + 1], mtmp[:, u0:u1], axis=AX.X
                                 ).then_inc(p1ev, 1)
                v.drain()
                for j in range(NTT):
                    v.wait_ge(trx, j + 1)
                    v.tensor_scalar_mul(mrow[0:1, j * 128:(j + 1) * 128],
                                        psS[0][0:1, 0:128], -1.0).then_inc(trev, 1)
                    v.drain()
                for gg in range(4):
                    v.wait_ge(avmm, GLAST[gg])
                    v.tensor_copy(lr[0:1, gg * 512:(gg + 1) * 512], psL[0:1, :]
                                  ).then_inc(aoev, 1)
                    v.drain()
            blk.vector(q5_vector)

            def q5_scalar(s):
                for n2, (g, kind, idx) in enumerate(P2U):
                    w = 512 if kind == "k" else 128
                    s.wait_ge(stmm, n2 + 1)
                    if n2 >= 4:
                        s.wait_ge(avmm, n2 - 3)
                    s.activation(out=PT[n2 % 4][:, 0:w], in_=psST[n2 % 4][:, 0:w],
                                 func=AF.Exp).then_inc(pte, 1)
                    for gg in range(4):
                        if n2 == GLAST[gg] + 1:
                            s.wait_ge(avmm, GLAST[gg])
                            s.activation(out=ao[:, gg * 512:(gg + 1) * 512], in_=psAV[:, :],
                                         func=AF.Copy).then_inc(aoev, 1)
                s.wait_ge(avmm, NP2)
                s.activation(out=ao[:, 3 * 512:4 * 512], in_=psAV[:, :],
                             func=AF.Copy).then_inc(aoev, 1)
            blk.scalar(q5_scalar)

            ctx.barrier()

        with ExitStack() as q6:
            opw = [ctx.sb(f"opw{e}", [128, D], BF16, q6) for e in range(E1)]
            mrwt = ctx.sb("mrwt", [E1, TPC], F32, q6)
            lrec = ctx.sb("lrec", [1, TPC], F32, q6)
            wstg = [ctx.sb(f"wstg{i}", [1, 512], F32, q6) for i in range(2)]
            rhse = [ctx.sb(f"rhse{i}", [128, 512], BF16, q6) for i in range(E1)]
            aos = ctx.sb("aos", [128, TPC], F32, q6)
            g1t = [ctx.sb(f"g1t{i}", [128, 512], BF16, q6) for i in range(2)]

            psW = [ctx.ps(f"psW{i}", [128, 512], F32, q6) for i in range(2)]
            psG = [ctx.ps(f"psG{i}", [128, 512], F32, q6) for i in range(2)]

            o_ld = ctx.sem("o_ld")
            o_pre = ctx.sem("o_pre")
            o_lbc = ctx.sem("o_lbc")
            o_aos = ctx.sem("o_aos")
            o_row = [ctx.sem("o_row0"), ctx.sem("o_row1")]
            o_wbc = ctx.sem("o_wbc")
            o_rhs = ctx.sem("o_rhs")
            o_mm = ctx.sem("o_mm")
            o_sil = ctx.sem("o_sil")
            o_st = [ctx.sem("o_st0"), ctx.sem("o_st1")]

            def q6_sync(sy):
                sy.wait_ge(oc_sem, 16)
                for e in range(E1):
                    sy.dma_start(out=opw[e][:, :], in_=opbf[e, :, :]).then_inc(o_ld, 16)
                sy.dma_start(out=mrwt[:, :], in_=mrwg[:, :]).then_inc(o_ld, 16)
            blk.sync(q6_sync)

            def q6_gpsimd(g_):
                for qc in range(NTC):
                    if qc >= 1:
                        for oc in range(ND):
                            go = (qc - 1) * ND + oc
                            g_.wait_ge(o_sil, go + 1)
                            g_.dma_start(out=g1o[oc * 128:(oc + 1) * 128,
                                                 (qc - 1) * 512:qc * 512],
                                         in_=g1t[go % 2][:, :]).then_inc(o_st[go % 2], 16)
                    for e in range(E1):
                        u = qc * E1 + e
                        if u >= 2:
                            g_.wait_ge(o_wbc, u - 1)
                        g_.dma_start(out=wstg[u % 2][:, :],
                                     in_=mrwg[e:e + 1, qc * 512:(qc + 1) * 512]
                                     ).then_inc(o_row[u % 2], 16)
                for oc in range(ND):
                    go = 3 * ND + oc
                    g_.wait_ge(o_sil, go + 1)
                    g_.dma_start(out=g1o[oc * 128:(oc + 1) * 128, 3 * 512:4 * 512],
                                 in_=g1t[go % 2][:, :]).then_inc(o_st[go % 2], 16)
                g_.wait_ge(o_st[0], 16 * 16)
                g_.wait_ge(o_st[1], 16 * 16)
            blk.gpsimd(q6_gpsimd)

            def q6_tensor(t):
                t.wait_ge(o_ld, 16 * (E1 + 1))
                t.wait_ge(o_pre, 1)
                for qc in range(NTC):
                    if qc >= 2:
                        t.wait_ge(o_aos, qc - 1)
                    t.matmul(psW[qc % 2][:, :], ones1_f[:, :],
                             lrec[0:1, qc * 512:(qc + 1) * 512],
                             start=True, stop=True).then_inc(o_lbc, 1)
                for qc in range(NTC):
                    for e in range(E1):
                        u = qc * E1 + e
                        t.wait_ge(o_row[u % 2], 16 * (u // 2 + 1))
                        if u == 0:
                            t.wait_ge(o_aos, NTC)
                        if u >= 2:
                            t.wait_ge(o_rhs, u - 1)
                        t.matmul(psW[u % 2][:, :], ones1_f[:, :], wstg[u % 2][:, :],
                                 start=True, stop=True).then_inc(o_wbc, 1)
                    for oc in range(ND):
                        go = qc * ND + oc
                        if go >= 2:
                            t.wait_ge(o_sil, go - 1)
                        for e in range(E1):
                            t.wait_ge(o_rhs, qc * E1 + e + 1)
                            ins = t.matmul(psG[go % 2][:, :],
                                           opw[e][:, oc * 128:(oc + 1) * 128],
                                           rhse[e][:, :],
                                           start=(e == 0), stop=(e == E1 - 1))
                        ins.then_inc(o_mm, 1)
            blk.tensor(q6_tensor)

            def q6_vector(v):
                v.wait_ge(aoev, 8)
                v.reciprocal(lrec[:, :], lr[:, :]).then_inc(o_pre, 1)
                v.drain()
                for qc in range(NTC):
                    v.wait_ge(o_lbc, qc + 1)
                    v.tensor_mul(aos[:, qc * 512:(qc + 1) * 512],
                                 ao[:, qc * 512:(qc + 1) * 512],
                                 psW[qc % 2][:, :]).then_inc(o_aos, 1)
                    v.drain()
                for qc in range(NTC):
                    for e in range(E1):
                        u = qc * E1 + e
                        v.wait_ge(o_wbc, u + 1)
                        if u >= E1:
                            v.wait_ge(o_mm, (qc - 1) * ND + ND)
                        v.tensor_mul(rhse[e][:, :], aos[:, qc * 512:(qc + 1) * 512],
                                     psW[u % 2][:, :]).then_inc(o_rhs, 1)
                        v.drain()
            blk.vector(q6_vector)

            def q6_scalar(s):
                for qc in range(NTC):
                    for oc in range(ND):
                        go = qc * ND + oc
                        s.wait_ge(o_mm, go + 1)
                        if go >= 2:
                            s.wait_ge(o_st[go % 2], 16 * ((go - 2) // 2 + 1))
                        s.activation(out=g1t[go % 2][:, :], in_=psG[go % 2][:, :],
                                     func=AF.Silu).then_inc(o_sil, 1)
            blk.scalar(q6_scalar)


            ctx.barrier()

    return nc, dbg_tensors



def build_launch4(debug=()):
    nc = bass.Bass("TRN2", target_bir_lowering=False)
    dbg = set(debug)
    dbg_tensors = {}

    ylt = nc.dram_tensor("ylt", [TPC, D_INNER], BF16, kind="ExternalInput").ap()
    xfm = nc.dram_tensor("xfm", [D_INNER, TPC], BF16, kind="ExternalInput").ap()
    szf = nc.dram_tensor("szf", [D_INNER, TPC], BF16, kind="ExternalInput").ap()
    g1f = nc.dram_tensor("g1f", [D, TPC], BF16, kind="ExternalInput").ap()
    x1f = nc.dram_tensor("x1f", [D, TPC], F32, kind="ExternalInput").ap()
    cto4 = nc.dram_tensor("cto", [N1, TPC], BF16, kind="ExternalInput").ap()
    exct = nc.dram_tensor("exct", [NH1, TPC], F32, kind="ExternalInput").ap()
    st_in = nc.dram_tensor("st_in", [3, N1, D_INNER], BF16, kind="ExternalInput").ap()
    dfr = nc.dram_tensor("dfr", [3, D_INNER], F32, kind="ExternalInput").ap()
    nwd = nc.dram_tensor("nwd", [128, D_INNER // 128], F32, kind="ExternalInput").ap()
    dpd = nc.dram_tensor("dpd", [128, D_INNER // 128], F32, kind="ExternalInput").ap()
    wout = nc.dram_tensor("wout", [D_INNER, D], F32, kind="ExternalInput").ap()

    x2o = nc.dram_tensor("x2_out", [D, TPC], F32, kind="ExternalOutput").ap()
    woutbf = nc.dram_tensor("woutbf", [D_INNER, D], BF16).ap()

    def dbg_out(name, shape, dtype):
        if name in dbg:
            dbg_tensors[name] = nc.dram_tensor("dbg_" + name, shape, dtype,
                                               kind="ExternalOutput").ap()
            return dbg_tensors[name]
        return None

    NFC = D_INNER // 128
    with ExitStack() as st:
        ctx = Ctx(nc, st, const_vals=(1e-5,))
        blk = ctx.blk

        ones1_f = ctx.sb("ones1_f", [1, 128], F32)
        ones_bf = ctx.sb("ones_bf", [128, 1], BF16)
        ygb = [ctx.sb(f"ygb{i}", [128, TPC], BF16) for i in range(NFC)]
        cto_s = ctx.sb("cto_s", [N1, TPC], BF16)
        Sbf = ctx.sb("Sbf", [N1, D_INNER], BF16)
        nwst = ctx.sb("nwst", [128, NFC], F32)
        dpst = ctx.sb("dpst", [128, NFC], F32)
        rstd_sb = ctx.sb("rstd_sb", [1, TPC], F32)
        rstdrep = ctx.sb("rstdrep", [128, TPC], F32)

        cast_sem = ctx.sem("cast_sem")

        def p0_gpsimd(g):
            g.dma_start(out=woutbf[:, :], in_=wout[:, :]).then_inc(cast_sem, 16)

        def p0_vector(v):
            ctx.emit_const_memsets(v)
            v.memset(ones1_f[:, :], 1.0)
            v.memset(ones_bf[:, :], 1.0)

        blk.gpsimd(p0_gpsimd)
        blk.vector(p0_vector)
        ctx.barrier()

        with ExitStack() as p1:
            acc = ctx.sb("acc", [N1, D_INNER], F32, p1)
            sts = [ctx.sb(f"sts{j}", [N1, D_INNER], BF16, p1) for j in range(3)]
            dfs = [ctx.sb(f"dfs{j}", [1, D_INNER], F32, p1) for j in range(3)]
            psD = [ctx.ps(f"psD{i}", [128, 512], F32, p1) for i in range(2)]

            steps = []

            def s_load(sy, dma):
                for j in range(3):
                    dma(out=sts[j][:, :], in_=st_in[j, :, :])
                    dma(out=dfs[j][:, :], in_=dfr[j:j + 1, :])
                dma(out=cto_s[:, :], in_=cto4[:, :])
                dma(out=nwst[:, :], in_=nwd[:, :])
                dma(out=dpst[:, :], in_=dpd[:, :])
            steps.append(("sync", s_load))

            def s_init(v, dma):
                v.tensor_copy(acc[:, :], sts[0][:, :])
            steps.append(("vector", s_init))

            for j in (1, 2):
                for half in range(2):
                    def s_psd(t, dma, j=j, half=half):
                        for i in range(2):
                            tc = half * 2 + i
                            t.matmul(psD[i][:, :], ones1_f[0:1, :],
                                     dfs[j][0:1, tc * 512:(tc + 1) * 512],
                                     start=True, stop=True)
                    steps.append(("tensor", s_psd))

                    def s_mul(v, dma, half=half):
                        for i in range(2):
                            tc = half * 2 + i
                            sl = slice(tc * 512, (tc + 1) * 512)
                            v.tensor_mul(acc[:, sl], acc[:, sl], psD[i][:, :])
                            v.drain()
                    steps.append(("vector", s_mul))

                def s_add(v, dma, j=j):
                    v.tensor_tensor(acc[:, :], acc[:, :], sts[j][:, :], OP.add)
                steps.append(("vector", s_add))

            def s_cast(v, dma):
                v.tensor_copy(Sbf[:, :], acc[:, :])
            steps.append(("vector", s_cast))

            ctx.seq("p1s", steps)
            ctx.barrier()

        with ExitStack() as pA:
            psS = [ctx.ps(f"psS{i}", [1, 512], F32, pA) for i in range(NTC)]
            with ExitStack() as p2:
                ylTt = [ctx.sb(f"ylTt{i}", [128, TPC], BF16, p2) for i in range(2)]
                Xt2 = [ctx.sb(f"Xt{i}", [128, TPC], BF16, p2) for i in range(2)]
                szt2 = [ctx.sb(f"szt{i}", [128, TPC], BF16, p2) for i in range(2)]
                exc2 = [[ctx.sb(f"exc2_{i}_{hh}", [1, TPC], F32, p2)
                         for hh in range(2)] for i in range(2)]
                yt2 = [ctx.sb(f"yt{i}", [128, TPC], F32, p2) for i in range(2)]
                sqb2 = [ctx.sb(f"sqb{i}", [128, TPC], BF16, p2) for i in range(2)]
                exrep2 = [ctx.sb(f"exrep{i}", [128, TPC], F32, p2) for i in range(2)]
                psE = [ctx.ps(f"psE{i}", [128, 512], F32, p2) for i in range(2)]
                psC = [ctx.ps(f"psC{i}", [128, 512], F32, p2) for i in range(2)]
                dyg = dbg_out("yg", [D_INNER, TPC], BF16)

                l4l = [ctx.sem("l4l0"), ctx.sem("l4l1")]
                t_pe = ctx.sem("t_pe")
                t_pc = ctx.sem("t_pc")
                t_sq = ctx.sem("t_sq")
                s_ex = ctx.sem("s_ex")
                a_cast = ctx.sem("a_cast")
                v_cr = ctx.sem("v_cr")
                v_gate = ctx.sem("v_gate")
                v_done = ctx.sem("v_done")

                def p2_sync(sy):
                    for fc in range(NFC):
                        p = fc % 2
                        if fc >= 2:
                            sy.wait_ge(v_done, fc - 1)
                        sy.dma_start_transpose(ylTt[p][:, :],
                                               ylt[:, fc * 128:(fc + 1) * 128]
                                               ).then_inc(l4l[p], 16)
                        sy.dma_start(out=Xt2[p][:, :],
                                     in_=xfm[fc * 128:(fc + 1) * 128, :]
                                     ).then_inc(l4l[p], 16)
                        sy.dma_start(out=szt2[p][:, :],
                                     in_=szf[fc * 128:(fc + 1) * 128, :]
                                     ).then_inc(l4l[p], 16)
                        for hh in range(2):
                            sy.dma_start(out=exc2[p][hh][:, :],
                                         in_=exct[2 * fc + hh:2 * fc + hh + 1, :]
                                         ).then_inc(l4l[p], 16)
                blk.sync(p2_sync)

                def p2_tensor(t):
                    def emit_pss(fc):
                        t.wait_ge(v_done, fc + 1)
                        for tc in range(NTC):
                            t.matmul(psS[tc][:, :], ones_bf[:, :],
                                     sqb2[fc % 2][:, tc * 512:(tc + 1) * 512],
                                     start=(fc == 0), stop=(fc == NFC - 1)
                                     ).then_inc(t_sq, 1)
                    for fc in range(NFC):
                        p = fc % 2
                        t.wait_ge(l4l[p], 80 * (fc // 2 + 1))
                        for tc in range(NTC):
                            u = fc * 4 + tc
                            sl = slice(tc * 512, (tc + 1) * 512)
                            if u >= 2:
                                t.wait_ge(s_ex, u - 1)
                            for hh in range(2):
                                ins = t.matmul(psE[u % 2][hh * 64:(hh + 1) * 64, :],
                                               ones1_f[0:1, 0:64],
                                               exc2[p][hh][0:1, sl],
                                               start=True, stop=True)
                            ins.then_inc(t_pe, 1)
                            if u >= 2:
                                t.wait_ge(v_cr, u - 1)
                            t.matmul(psC[u % 2][:, :],
                                     Sbf[:, fc * 128:(fc + 1) * 128],
                                     cto_s[:, sl], start=True, stop=True
                                     ).then_inc(t_pc, 1)
                        if fc >= 1:
                            emit_pss(fc - 1)
                    emit_pss(NFC - 1)
                blk.tensor(p2_tensor)

                def p2_scalar(s):
                    for fc in range(NFC):
                        p = fc % 2
                        for tc in range(NTC):
                            u = fc * 4 + tc
                            if tc == 0 and fc >= 2:
                                s.wait_ge(v_cr, 4 * (fc - 1))
                            s.wait_ge(t_pe, u + 1)
                            s.activation(out=exrep2[p][:, tc * 512:(tc + 1) * 512],
                                         in_=psE[u % 2][:, :],
                                         func=AF.Copy).then_inc(s_ex, 1)
                        s.wait_ge(v_done, fc + 1)
                        s.activation(out=ygb[fc][:, :], in_=yt2[p][:, :],
                                     func=AF.Copy).then_inc(a_cast, 1)
                        if dyg is not None:
                            s.drain()
                            s.dma_start(out=dyg[fc * 128:(fc + 1) * 128, :],
                                        in_=ygb[fc][:, :])
                blk.scalar(p2_scalar)

                def p2_vector(v):
                    for fc in range(NFC):
                        p = fc % 2
                        for tc in range(NTC):
                            u = fc * 4 + tc
                            sl = slice(tc * 512, (tc + 1) * 512)
                            v.wait_ge(t_pc, u + 1)
                            v.wait_ge(s_ex, u + 1)
                            if tc == 0 and fc >= 2:
                                v.wait_ge(a_cast, fc - 1)
                            v.tensor_mul(yt2[p][:, sl], psC[u % 2][:, :],
                                         exrep2[p][:, sl]).then_inc(v_cr, 1)
                        v.wait_ge(l4l[p], 80 * (fc // 2 + 1))
                        v.drain()
                        v.tensor_tensor(yt2[p][:, :], yt2[p][:, :], ylTt[p][:, :],
                                        OP.add)
                        v.drain()
                        v.scalar_tensor_tensor(yt2[p][:, :], Xt2[p][:, :],
                                               dpst[:, fc:fc + 1], yt2[p][:, :],
                                               OP.mult, OP.add)
                        v.drain()
                        v.tensor_mul(yt2[p][:, :], yt2[p][:, :], szt2[p][:, :]
                                     ).then_inc(v_gate, 1)
                blk.vector(p2_vector)

                def p2_gpsimd(g):
                    for fc in range(NFC):
                        p = fc % 2
                        g.wait_ge(v_gate, fc + 1)
                        if fc >= 2:
                            g.wait_ge(t_sq, 4 * (fc - 1))
                        g.tensor_mul(sqb2[p][:, :], yt2[p][:, :], yt2[p][:, :]
                                     ).then_inc(v_done, 1)
                blk.gpsimd(p2_gpsimd)

                ctx.barrier()

            psR = [ctx.ps(f"psR{i}", [128, 512], F32, pA) for i in range(2)]
            drs = dbg_out("rstd", [1, TPC], F32)
            steps = []

            def s_rstd(s, dma):
                for tc in range(NTC):
                    sl = slice(tc * 512, (tc + 1) * 512)
                    s.activation(out=rstd_sb[0:1, sl], in_=psS[tc][0:1, :],
                                 func=AF.Ln, scale=1.0 / D_INNER, bias=1e-5)
                    s.drain()
                    s.activation(out=rstd_sb[0:1, sl], in_=rstd_sb[0:1, sl],
                                 func=AF.Exp, scale=-0.5)
                    s.drain()
                if drs is not None:
                    dma(out=drs[:, :], in_=rstd_sb[:, :])
            steps.append(("scalar", s_rstd))

            for half in range(2):
                def s_psr(t, dma, half=half):
                    for i in range(2):
                        tc = half * 2 + i
                        t.matmul(psR[i][:, :], ones1_f[0:1, :],
                                 rstd_sb[0:1, tc * 512:(tc + 1) * 512],
                                 start=True, stop=True)
                steps.append(("tensor", s_psr))

                def s_rcp(s, dma, half=half):
                    for i in range(2):
                        tc = half * 2 + i
                        sl = slice(tc * 512, (tc + 1) * 512)
                        s.activation(out=rstdrep[:, sl], in_=psR[i][:, :],
                                     func=AF.Copy)
                        s.drain()
                steps.append(("scalar", s_rcp))

            ctx.seq("p3s", steps)
            ctx.barrier()

        dyn = dbg_out("yn", [D_INNER, TPC], BF16)
        steps = []
        for fc in range(NFC):
            def s_yn(v, dma, fc=fc):
                v.scalar_tensor_tensor(ygb[fc][:, :], ygb[fc][:, :],
                                       nwst[:, fc:fc + 1], rstdrep[:, :],
                                       OP.mult, OP.mult)
            steps.append(("vector", s_yn))
            if dyn is not None:
                def s_dyn(g, dma, fc=fc):
                    dma(out=dyn[fc * 128:(fc + 1) * 128, :], in_=ygb[fc][:, :])
                steps.append(("gpsimd", s_dyn))
        ctx.seq("p4s", steps)
        ctx.barrier()

        with ExitStack() as p5:
            woT = [ctx.sb(f"woT{i}", [128, D], BF16, p5) for i in range(NFC)]
            x1t2 = [ctx.sb(f"x1t{i}", [128, TPC], F32, p5) for i in range(2)]
            g1t2 = [ctx.sb(f"g1t{i}", [128, TPC], BF16, p5) for i in range(2)]
            x2t2 = [ctx.sb(f"x2t{i}", [128, TPC], F32, p5) for i in range(2)]
            psO = [ctx.ps(f"psO{i}", [128, 512], F32, p5) for i in range(2)]

            w5 = ctx.sem("w5")
            l5 = [ctx.sem("l5_0"), ctx.sem("l5_1")]
            t_o = ctx.sem("t_o")
            v_f = ctx.sem("v_f")
            v_oc = ctx.sem("v_oc")
            st5 = [ctx.sem("st5_0"), ctx.sem("st5_1")]

            def p5_sync(sy):
                sy.wait_ge(cast_sem, 16)
                for i in range(NFC):
                    sy.dma_start(out=woT[i][:, :],
                                 in_=woutbf[i * 128:(i + 1) * 128, :]
                                 ).then_inc(w5, 16)
                for oc in range(ND):
                    p = oc % 2
                    if oc >= 2:
                        sy.wait_ge(st5[p], 16 * (oc // 2))
                    sy.dma_start(out=x1t2[p][:, :],
                                 in_=x1f[oc * 128:(oc + 1) * 128, :]
                                 ).then_inc(l5[p], 16)
                    sy.dma_start(out=g1t2[p][:, :],
                                 in_=g1f[oc * 128:(oc + 1) * 128, :]
                                 ).then_inc(l5[p], 16)
            blk.sync(p5_sync)

            def p5_tensor(t):
                t.wait_ge(w5, 16 * NFC)
                for oc in range(ND):
                    for tc in range(NTC):
                        u = oc * 4 + tc
                        if u >= 2:
                            t.wait_ge(v_f, u - 1)
                        ins = None
                        for fc in range(NFC):
                            ins = t.matmul(psO[u % 2][:, :],
                                           woT[fc][:, oc * 128:(oc + 1) * 128],
                                           ygb[fc][:, tc * 512:(tc + 1) * 512],
                                           start=(fc == 0), stop=(fc == NFC - 1))
                        ins.then_inc(t_o, 1)
            blk.tensor(p5_tensor)

            def p5_vector(v):
                for oc in range(ND):
                    p = oc % 2
                    v.wait_ge(l5[p], 32 * (oc // 2 + 1))
                    if oc >= 2:
                        v.wait_ge(st5[p], 16 * (oc // 2))
                    for tc in range(NTC):
                        u = oc * 4 + tc
                        sl = slice(tc * 512, (tc + 1) * 512)
                        v.wait_ge(t_o, u + 1)
                        v.tensor_mul(x2t2[p][:, sl], psO[u % 2][:, :],
                                     g1t2[p][:, sl]).then_inc(v_f, 1)
                    v.drain()
                    v.engine_nop().then_inc(v_oc, 1)
            blk.vector(p5_vector)

            def p5_gpsimd(g):
                for oc in range(ND):
                    p = oc % 2
                    g.wait_ge(v_oc, oc + 1)
                    g.wait_ge(l5[p], 32 * (oc // 2 + 1))
                    if oc >= 2:
                        g.wait_ge(st5[p], 16 * (oc // 2))
                    for tc in range(NTC):
                        sl = slice(tc * 512, (tc + 1) * 512)
                        g.tensor_tensor(x2t2[p][:, sl], x2t2[p][:, sl],
                                        x1t2[p][:, sl], OP.add)
                    g.drain()
                    g.dma_start(out=x2o[oc * 128:(oc + 1) * 128, :],
                                in_=x2t2[p][:, :]).then_inc(st5[p], 16)
                g.wait_ge(st5[0], 16 * (ND // 2))
                g.wait_ge(st5[1], 16 * (ND // 2))
            blk.gpsimd(p5_gpsimd)

            ctx.barrier()

    return nc, dbg_tensors



def kernel(**inputs):
    import ml_dtypes
    bf = ml_dtypes.bfloat16
    f32 = np.float32
    CORES = list(range(NCORES))

    x = np.ascontiguousarray(np.asarray(inputs["L0_attn_x"], f32).reshape(B * T, D))
    wqkvT = np.ascontiguousarray(np.asarray(inputs["L0_attn_W_qkv"], f32).T)
    gpT = np.ascontiguousarray(
        np.asarray(inputs["L0_ffn_gate_proj"], f32).transpose(0, 2, 1))
    upT = np.ascontiguousarray(
        np.asarray(inputs["L0_ffn_up_proj"], f32).transpose(0, 2, 1))
    dpT = np.ascontiguousarray(
        np.asarray(inputs["L0_ffn_down_proj"], f32).transpose(0, 2, 1))

    maps1 = []
    for c in CORES:
        maps1.append({
            "x": np.ascontiguousarray(x[c * TPC:(c + 1) * TPC]),
            "wqkv": wqkvT,
            "gw": np.asarray(inputs["L0_ffn_gate_weight"], f32),
            "ebias": np.asarray(inputs["L0_ffn_e_bias"], f32).reshape(1, E0),
            "gatep": gpT,
            "upp": upT,
            "downp": dpT,
        })
    nc1, _ = build_launch1()
    r1 = run_bass_kernel_spmd(nc1, maps1, core_ids=CORES).results

    qkp = np.ascontiguousarray(np.concatenate(
        [np.asarray(inputs["L1_ffn_q_proj"], f32),
         np.asarray(inputs["L1_ffn_k_proj"], f32),
         np.asarray(inputs["L1_ffn_v_proj"], f32)], axis=2))
    woT_h = np.ascontiguousarray(np.asarray(inputs["L0_attn_W_o"], f32).T)
    winT_h = np.ascontiguousarray(np.asarray(inputs["L1_attn_W_in"], f32).T)
    maps2 = []
    for c in CORES:
        b, seg = c // 4, c % 4
        ckvall = np.zeros((4 * DH0, H0 * DH0), f32)
        for s in range(4):
            ckvall[s * DH0:(s + 1) * DH0] = np.asarray(
                r1[b * 4 + s]["ckv_sums"], f32)
        mcur = np.zeros((DH0, 4), f32)
        mcur[:, :seg] = 1.0
        posr = np.arange(seg * TPC + 1, (seg + 1) * TPC + 1,
                         dtype=f32).reshape(1, TPC)
        maps2.append({
            "x": np.ascontiguousarray(x[c * TPC:(c + 1) * TPC]),
            "qbf": np.asarray(r1[c]["q_out"]),
            "apbf": np.asarray(r1[c]["attn_part"]),
            "gbf": np.asarray(r1[c]["g_out"]),
            "ckvall": ckvall,
            "mcur": mcur,
            "posr": posr,
            "wo": woT_h,
            "win": winT_h,
            "simm": np.asarray(inputs["L1_ffn_sim_matrix"], f32),
            "gatesd": np.asarray(inputs["L1_ffn_gates"], f32).reshape(E1, 1),
            "qkp": qkp,
        })
    nc2, _ = build_launch2()
    r2 = run_bass_kernel_spmd(nc2, maps2, core_ids=CORES).results

    qkvall = np.concatenate([np.asarray(r2[c]["qkv_out"]) for c in CORES],
                            axis=0)
    mrwall = np.concatenate([np.asarray(r2[c]["mrw_out"], f32) for c in CORES],
                            axis=1)
    cw = np.asarray(inputs["L1_attn_conv_w"], f32).reshape(CONV_DIM, DCONV)
    maps3 = []
    for c in CORES:
        b, r = c // 4, c % 4
        seg = c % 4
        bnd = np.zeros((CONV_DIM, 4), bf)
        if seg > 0:
            bnd[:, 1:4] = np.asarray(r2[c - 1]["xbc_out"])[:, TPC - 3:TPC]
        qrows, krows, vrows, kdrows, vdrows, mvals_l, mrwg_l = \
            [], [], [], [], [], [], []
        for j in range(16):
            i = 4 * j + r
            g_sl = slice(b * T + i * 128, b * T + (i + 1) * 128)
            qrows.append(qkvall[g_sl, 0:128])
            kdrows.append(qkvall[g_sl, 128:256])
            vdrows.append(qkvall[g_sl, 256:384])
            mrwg_l.append(mrwall[:, g_sl])
        for g in range(4):
            nkb = 16 * (g + 1)
            krows.append(qkvall[b * T:b * T + nkb * 128, 128:256])
            vrows.append(qkvall[b * T:b * T + nkb * 128, 256:384])
            M = np.zeros((nkb, 512), f32)
            for rr in range(4):
                j = 4 * g + rr
                i = 4 * j + r
                for kb in range(nkb):
                    if kb >= i:
                        M[kb, rr * 128:(rr + 1) * 128] = 1e9
            mvals_l.append(M)
        maps3.append({
            "xbcp": np.asarray(r2[c]["xbc_out"]),
            "bnd3": bnd,
            "convw": cw,
            "convb": np.asarray(inputs["L1_attn_conv_b"], f32).reshape(CONV_DIM, 1),
            "dtraw": np.asarray(r2[c]["dtraw_out"], f32),
            "dtb": np.asarray(inputs["L1_attn_dt_bias"], f32).reshape(NH1, 1),
            "alog": np.asarray(inputs["L1_attn_A_log"], f32).reshape(NH1, 1),
            "dpar": np.asarray(inputs["L1_attn_D_param"], f32).reshape(1, NH1),
            "qtl": np.ascontiguousarray(np.concatenate(qrows, 0)),
            "kcat": np.ascontiguousarray(np.concatenate(krows, 0)),
            "vcat": np.ascontiguousarray(np.concatenate(vrows, 0)),
            "kdg": np.ascontiguousarray(np.concatenate(kdrows, 0)),
            "vdg": np.ascontiguousarray(np.concatenate(vdrows, 0)),
            "mvals": np.concatenate(mvals_l, 0),
            "mrwg": np.ascontiguousarray(np.concatenate(mrwg_l, 1)),
        })
        maps3[-1]["oproj"] = np.asarray(inputs["L1_ffn_o_proj"], f32)
    nc3, _ = build_launch3()
    r3 = run_bass_kernel_spmd(nc3, maps3, core_ids=CORES).results

    nwd = np.ascontiguousarray(
        np.asarray(inputs["L1_attn_norm_w"], f32).reshape(NFC_, 128).T)
    dpd = np.ascontiguousarray(np.repeat(
        np.asarray(inputs["L1_attn_D_param"], f32), P1).reshape(NFC_, 128).T)
    g1glob = np.empty((B * T, D), dtype=np.asarray(r3[0]["g1_out"]).dtype)
    for c in CORES:
        b, r = c // 4, c % 4
        g1c = np.asarray(r3[c]["g1_out"])
        for j in range(16):
            i = 4 * j + r
            g1glob[b * T + i * 128:b * T + (i + 1) * 128] = \
                g1c[:, j * 128:(j + 1) * 128].T
    maps4 = []
    for c in CORES:
        b = c // 4
        sts = np.zeros((3, N1, D_INNER), bf)
        dfrs = np.ones((3, D_INNER), f32)
        js = list(range(4 * b, c))
        for slot, j in zip(range(3 - len(js), 3), js):
            sts[slot] = np.asarray(r3[j]["states_out"])
            dfrs[slot] = np.repeat(np.asarray(r3[j]["decay_out"], f32).ravel(), P1)
        maps4.append({
            "ylt": np.asarray(r3[c]["ylocal_out"]),
            "xfm": np.asarray(r3[c]["xoh"]),
            "szf": np.asarray(r2[c]["siluz_out"]),
            "g1f": np.ascontiguousarray(g1glob[c * TPC:(c + 1) * TPC].T),
            "x1f": np.asarray(r2[c]["x1_out"], f32),
            "cto": np.asarray(r3[c]["C_out"]),
            "exct": np.ascontiguousarray(np.asarray(r3[c]["expcum_out"], f32).T),
            "st_in": sts,
            "dfr": dfrs,
            "nwd": nwd,
            "dpd": dpd,
            "wout": np.ascontiguousarray(np.asarray(inputs["L1_attn_W_out"], f32).T),
        })
    nc4, _ = build_launch4()
    r4 = run_bass_kernel_spmd(nc4, maps4, core_ids=CORES).results

    out = np.empty((B * T, D), f32)
    for c in CORES:
        out[c * TPC:(c + 1) * TPC] = np.asarray(r4[c]["x2_out"], f32).T
    return out.reshape(B, T, D)


NFC_ = D_INNER // 128
